# version 03
# baseline (speedup 1.0000x reference)
import os
import numpy as np
import ml_dtypes
import concourse.bass as bass
import concourse.mybir as mybir
from concourse.bass_utils import run_bass_kernel_spmd

F32 = mybir.dt.float32
BF16 = mybir.dt.bfloat16
I32 = mybir.dt.int32
AF = mybir.ActivationFunctionType
ALU = mybir.AluOpType
AX = mybir.AxisListType
BFNP = ml_dtypes.bfloat16

NCORES = 8
SEQS = 4
T = 2048
NT = 16
D = 1024
ND = 8
TP = T + 2
NEG = -1.0e38

HEADS = 16
DH = 64
WIN = 256
NW = T // WIN
INNER = 1024
DI = 2730
NC_CHUNKS = 22
C_LAST = DI - 21 * 128

DBG_STAGE = os.environ.get("BK_STAGE", "")


def host_tables():
    half = np.arange(0, DH, 2, dtype=np.float64)
    inv_freq = 1.0 / 10000.0 ** (half / DH)
    t = np.arange(T, dtype=np.float64)
    freqs = t[:, None] * inv_freq[None, :]
    freqs = np.concatenate([freqs, freqs], -1)
    base = (half + 0.4 * DH) / (1.4 * DH)
    power = (t - T // 2) / (WIN // 2)
    scale = base[None, :] ** power[:, None]
    scale = np.concatenate([scale, scale], -1)
    cos = np.cos(freqs)
    sin = np.sin(freqs)
    cs = (cos * scale).T
    ss = (sin * scale).T
    csi = (cos / scale).T
    ssi = (sin / scale).T
    out = {}
    for nm, v in (("cs", cs), ("ss", ss), ("csi", csi), ("ssi", ssi)):
        out[nm] = np.concatenate([v, v], 0).astype(np.float32)
    return out


def host_consts():
    c = {}
    c["ident"] = np.eye(128, dtype=BFNP)
    oh = np.zeros((128, 128), dtype=BFNP); oh[0, :] = 1
    c["onehot"] = oh
    c["onehot32"] = oh.astype(np.float32)
    rot = np.zeros((128, 128), dtype=np.float32)
    for m in range(128):
        blk = (m // 64) * 64
        j = m % 64
        if j < 32:
            rot[blk + j + 32, m] = -1.0
        else:
            rot[blk + j - 32, m] = 1.0
    c["rot"] = rot.astype(BFNP)
    bo = np.zeros((128, 2), dtype=BFNP)
    bo[0:64, 0] = 1; bo[64:128, 1] = 1
    c["bo"] = bo
    bot = np.zeros((16, 8 * 128), dtype=BFNP)
    for i in range(8):
        bot[2 * i, i * 128:i * 128 + 64] = 1
        bot[2 * i + 1, i * 128 + 64:i * 128 + 128] = 1
    c["bot16"] = bot
    oc16 = np.zeros((128, 16 * 16), dtype=BFNP)
    for h in range(16):
        oc16[:, h * 16 + h] = 1
    c["onescol16"] = oc16
    m0 = np.where(np.arange(128)[:, None] >= np.arange(128)[None, :], 0.0, NEG)
    m2 = np.where(np.arange(128)[:, None] <= np.arange(128)[None, :], 0.0, NEG)
    c["mask0"] = m0.astype(np.float32)
    c["mask2"] = m2.astype(np.float32)
    for mc in range(2):
        mm = (np.arange(128)[:, None] + mc * 128 <= np.arange(256)[None, :])
        c[f"cmask{mc}"] = mm.astype(BFNP)
    return c


class Builder:

    ENGS = ("tensor", "vector", "scalar", "gpsimd", "sync")

    def __init__(self, nc):
        self.nc = nc
        self.ops = []
        self.sem = {}
        self.cnt = {}
        self._es = []
        for name in ("pe", "ve", "se", "ge", "qw", "qx", "qs"):
            h = nc.semaphore(name)
            self.sem[name] = h.__enter__()
            self._es.append(h)
            self.cnt[name] = 0

    def op(self, eng, fn, waits=(), inc=None, amt=1):
        w = [(s, v) for (s, v) in waits if v > 0]
        incs = []
        r = None
        if inc is not None:
            self.cnt[inc] += amt
            r = self.cnt[inc]
            incs.append((inc, amt))
        self.ops.append((eng, fn, w, incs))
        return r

    def dma(self, queue_eng, out, in_, waits=(), inc=None, slow=False):
        fn = (lambda e, out=out, in_=in_, slow=slow: e.dma_start(
            out=out, in_=in_, allow_slow_non_contiguous=slow))
        return self.op(queue_eng, fn, waits=waits, inc=inc, amt=16)

    def val(self, sem):
        return self.cnt[sem]

    def emit(self):
        nc = self.nc
        with nc.Block() as block:
            for eng in self.ENGS:
                ops = [o for o in self.ops if o[0] == eng]
                if not ops:
                    continue
                dec = getattr(block, eng)

                def make(ops):
                    def body(e):
                        for (_, fn, waits, incs) in ops:
                            for (s, v) in waits:
                                e.wait_ge(self.sem[s], v)
                            ins = fn(e)
                            for (s, a) in incs:
                                ins.then_inc(self.sem[s], a)
                    return body
                dec(make(ops))

    def close(self):
        for h in self._es:
            h.__exit__(None, None, None)


class Mem:

    BASE = 20480
    CAP = 229340

    def __init__(self, nc):
        self.nc = nc
        self.cur = Mem.BASE
        self.pool_base = None
        self._uid = 0

    def _alloc(self, name, off, shape, dtype):
        self._uid += 1
        h = self.nc.alloc_sbuf_tensor_at(f"{name}_{self._uid}", list(shape), dtype, offset=off)
        return h.ap()

    def persist(self, name, shape, dtype, align=64):
        itm = 2 if dtype in (BF16, mybir.dt.float16) else 4
        n = int(np.prod(shape[1:])) * itm
        off = (self.cur + align - 1) // align * align
        ap = self._alloc(name, off, shape, dtype)
        self.cur = off + n
        assert self.cur <= Mem.CAP, f"persistent overflow at {name}: {self.cur}"
        return ap

    def freeze(self):
        self.pool_base = (self.cur + 63) // 64 * 64

    def pool(self):
        return PoolCursor(self)


class PoolCursor:
    def __init__(self, mem, cur=None):
        self.mem = mem
        self.cur = mem.pool_base if cur is None else cur

    def fork(self):
        return PoolCursor(self.mem, self.cur)

    def alloc(self, name, shape, dtype, align=64):
        itm = 2 if dtype in (BF16, mybir.dt.float16) else 4
        n = int(np.prod(shape[1:])) * itm
        off = (self.cur + align - 1) // align * align
        ap = self.mem._alloc(name, off, shape, dtype)
        self.cur = off + n
        assert self.cur <= Mem.CAP, f"pool overflow at {name}: {self.cur} > {Mem.CAP}"
        return ap


class Prog:
    def __init__(self):
        nc = bass.Bass("TRN2", target_bir_lowering=False,
                       detect_race_conditions=False)
        self.nc = nc
        self.b = Builder(nc)
        self.mem = Mem(nc)
        self.tab = host_tables()
        self.con = host_consts()
        self.inputs = {}

        self.x_in = nc.dram_tensor("x", [SEQS, T, D], F32, kind="ExternalInput").ap()
        self.x_out = nc.dram_tensor("y", [SEQS, T, D], F32, kind="ExternalOutput").ap()
        self.dram = {}
        self.hb_xo = nc.dram_tensor("hxo", [ND, 128, T], F32, kind="Internal").ap()
        self.hb_xi = nc.dram_tensor("hxi", [ND, 128, T], F32, kind="Internal").ap()
        self.hb_qrhs = nc.dram_tensor("hqrhs", [128, ND * D], BF16, kind="Internal").ap()

        self._win = {}

    def w(self, name, shape):
        if name not in self._win:
            self._win[name] = self.nc.dram_tensor(name, list(shape), F32,
                                                  kind="ExternalInput").ap()
        return self._win[name]

    def extra(self, name, arr):
        if name not in self._win:
            self._win[name] = self.nc.dram_tensor(
                name, list(arr.shape),
                BF16 if arr.dtype == BFNP else F32, kind="ExternalInput").ap()
            self.inputs[name] = arr
        return self._win[name]

    def dump(self, name, ap):
        want = os.environ.get("BK_DUMP", "")
        if name not in want.split(","):
            return
        shape = list(ap.shape)
        dt = ap.dtype
        d = self.nc.dram_tensor(f"dump_{name}", shape, dt, kind="ExternalOutput").ap()
        self.b.dma("sync", d, ap, waits=[(s, self.b.cnt[s]) for s in
                                          ("pe", "ve", "se", "ge", "qw", "qx")
                                          if self.b.cnt[s] > 0], inc="qx")

    def setup_persistent(self):
        m = self.mem
        self.XBF = m.persist("XBF", [128, ND, TP], BF16)
        self.COLS = m.persist("COLS", [128, 512], F32)
        self.RP32 = m.persist("RP32", [128, 1024], F32)
        self.ONEC = m.persist("ONEC", [128, 16], BF16)
        c = self.con
        self.IDENT = m.persist("IDENT", [128, 128], BF16)
        self.IDENT32 = m.persist("IDENT32", [128, 128], F32)
        self.OH = m.persist("OH", [128, 128], BF16)
        self.OH32 = m.persist("OH32", [128, 128], F32)
        self.ROT = m.persist("ROT", [128, 128], BF16)
        self.BOT16 = m.persist("BOT16", [16, 8 * 128], BF16)
        self.OC16 = m.persist("OC16", [128, 16 * 16], BF16)
        self.CMASK = m.persist("CMASK", [128, 2, 256], BF16)
        self.BOSQ = m.persist("BOSQ", [128, 128], BF16)
        self.DMASKB = m.persist("DMASKB", [128, 384], BF16)
        m.freeze()

        b = self.b
        nc = self.nc
        loads = [
            (self.IDENT, self.con["ident"]), (self.OH, self.con["onehot"]),
            (self.ROT, self.con["rot"]),
            (self.BOT16, self.con["bot16"]), (self.OC16, self.con["onescol16"]),
        ]
        loads32 = [
            (self.IDENT32, np.eye(128, dtype=np.float32)),
            (self.OH32, self.con["onehot32"]),
        ]
        self._ci = 0
        def _inline(arr):
            self._ci += 1
            return nc.inline_tensor(np.ascontiguousarray(arr), f"icst{self._ci}").ap()
        for ap, arr in loads + loads32:
            d = _inline(arr)
            b.dma("sync", ap, d, inc="qx")
        cm = np.stack([self.con["cmask0"], self.con["cmask1"]], 0)
        d = _inline(cm.transpose(1, 0, 2))
        b.dma("sync", self.CMASK, d, inc="qx")
        bosq = np.zeros((128, 128), dtype=BFNP)
        for oc in range(8):
            for pp in range(128):
                bosq[pp, oc * 16 + 2 * oc + (pp // 64)] = 1
        d = _inline(bosq)
        b.dma("sync", self.BOSQ, d, inc="qx")
        qq = np.arange(128)[:, None]; pp_ = np.arange(128)[None, :]
        dm = np.concatenate([(qq >= pp_), np.ones((128, 128), bool),
                             (qq <= pp_)], 1).astype(BFNP)
        d = _inline(dm)
        b.dma("sync", self.DMASKB, d, inc="qx")
        self.t_consts = b.val("qx")
        b.op("vector", lambda e: e.memset(self.RP32[:], 0.0))
        b.op("vector", lambda e: e.memset(self.ONEC[:], 1.0), inc="ve")
        self.t_zeros = b.val("ve")

    def xbf_data(self, d, t0=0, tn=T):
        return self.XBF[:, d, 2 + t0:2 + t0 + tn]

    def xbf_shift(self, d, t0=0, tn=T):
        if d < 4:
            return self.XBF[:, d, 2 + t0:2 + t0 + tn]
        return self.XBF[:, d, 1 + t0:1 + t0 + tn]

    def replicate32(self, pool_dst, row_ap, n, ps_bank_ap, pe_waits, ve_waits):
        b = self.b
        nchunk = (n + 511) // 512
        assert nchunk <= 1
        b.op("vector", lambda e: e.tensor_copy(self.RP32[0:1, 0:n], row_ap),
             waits=ve_waits, inc="ve")
        vcp = b.val("ve")
        b.op("tensor", lambda e: e.matmul(ps_bank_ap[:, 0:n], self.OH32[:],
                                          self.RP32[:, 0:n], start=True, stop=True),
             waits=[("ve", vcp)] + pe_waits, inc="pe")
        pv = b.val("pe")
        b.op("vector", lambda e: e.tensor_copy(pool_dst[:, 0:n], ps_bank_ap[:, 0:n]),
             waits=[("pe", pv)], inc="ve")
        return b.val("ve")


def _np_f32(x):
    return np.ascontiguousarray(x, dtype=np.float32)


class Stages:
    def __init__(self, p: Prog):
        self.p = p
        self.b = p.b
        self.nc = p.nc
        self.PSH = self.nc.alloc_psum_tensor("PS", [128, 4096], F32)
        self.PS = self.PSH.ap()
        self.ps_gate = [("pe", 0)] * 8
        self.wb_free = {}
        self.wo_free = {}
        self.c_w_free = 0
        self.c_wot_free = {}
        self.c_xold_free = {}
        self.c_xst_qx = {}
        self.c_xst_se = {}
        self.c_tt_free = 0
        self.a_t1_se = 0
        self.d_w_free = 0
        self.d_av_pe = 0
        self.d_fd_pe = 0
        self.d_pt_pe = 0
        self.d_xold_free = {}
        self.d_xstd_se = {}
        self.d_xstd_qx = {}
        self.d_xbf_se = 0
        self.f_w1_free = {}
        self.f_w2_free = {}
        self.f_g_ve = 0
        self.f_xold_free = {}
        self.f_xst_qx = {}
        self.f_xst_se = {}
        self.c_gau_v = 0
        self.c_bg_ld = 0
        self.acc_read = {}
        self.rp32_user = 0
        self.bocol_ld = 0
        self._colcur = 0
        self.colslots = {}
        for nm, n in (("tq", 16), ("tk", 16), ("tk2", 16), ("ntk2", 16),
                      ("acc", 2), ("bo", 8), ("rs", 8), ("gau", 8),
                      ("lng", 8), ("lnb", 8), ("b1a", 22), ("b1g", 22),
                      ("b2", 8), ("bout", 8), ("bin", 1), ("qs8", 1), ("ks", 1), ("binc", 1), ("bgc", 16), ("am", 8), ("rc", 8), ("ls", 16), ("lt", 2)):
            self.colslots[nm] = None

    def bind_cols(self):
        p = self.p
        cur = 0
        for nm, n in (("tq", 16), ("tk", 16), ("tk2", 16), ("ntk2", 16),
                      ("acc", 2), ("bo", 8), ("rs", 8), ("gau", 8),
                      ("lng", 8), ("lnb", 8), ("b1a", 22), ("b1g", 22),
                      ("b2", 8), ("bout", 8), ("bin", 1), ("qs8", 1), ("ks", 1), ("binc", 1), ("bgc", 16), ("am", 8), ("rc", 8), ("ls", 16), ("lt", 2)):
            self.colslots[nm] = p.COLS[:, cur:cur + n]
            cur += n
        assert cur <= 512

    def join(self, queues=False):
        b = self.b
        sems = ("pe", "ve", "se", "ge") + (("qw", "qx") if queues else ())
        snap = {s: b.cnt[s] for s in sems}
        semmap = {"pe": "tensor", "ve": "vector", "se": "scalar", "ge": "gpsimd"}
        for eng in ("tensor", "vector", "scalar", "gpsimd", "sync"):
            waits = [(s, v) for s, v in snap.items()
                     if v > 0 and semmap.get(s) != eng]
            if waits:
                b.op(eng, lambda e: e.engine_nop() if hasattr(e, "engine_nop") else e.nop(),
                     waits=waits)

    def bank(self, i, n=512, off=0):
        return self.PS[:, 512 * i + off: 512 * i + off + n]

    def pwait(self, banks):
        return [self.ps_gate[i] for i in banks]

    def prelease(self, banks, sem, val):
        for i in banks:
            self.ps_gate[i] = (sem, val)

    def stats(self, pool, t0, tn, shifted, waits):
        p, b = self.p, self.b
        R1 = pool.alloc("R1", [128, tn], F32)
        R2 = pool.alloc("R2", [128, tn], F32)
        R3 = pool.alloc("R3", [128, tn], F32)
        SQ = [pool.alloc("SQa", [128, tn], BF16), pool.alloc("SQb", [128, tn], BF16)]
        nch = tn // 512
        view = p.xbf_shift if shifted else p.xbf_data
        base = ("qx", p.t_consts)
        sq_done = {}
        pe_after_sqx = {}
        for d in range(ND):
            w = list(waits) + [base]
            if d >= 2:
                w.append(("pe", pe_after_sqx[d - 2]))
            b.op("vector", lambda e, d=d: e.tensor_mul(
                SQ[d % 2][:], view(d, t0, tn), view(d, t0, tn)), waits=w, inc="ve")
            sq_done[d] = b.val("ve")
            for c in range(nch):
                w2 = []
                if d == 0 and c == 0:
                    w2 = list(waits) + self.pwait(range(0, nch))
                b.op("tensor", lambda e, d=d, c=c: e.matmul(
                    self.PS[0:1, 512 * c:512 * c + 512],
                    p.ONEC[:, 0:1], view(d, t0 + 512 * c, 512),
                    start=(d == 0), stop=(d == ND - 1)), waits=w2,
                    inc="pe" if (d == ND - 1 and c == nch - 1) else None)
            for c in range(nch):
                w2 = [("ve", sq_done[d])] if c == 0 else []
                if d == 0 and c == 0:
                    w2 += self.pwait(range(4, 4 + nch))
                b.op("tensor", lambda e, d=d, c=c: e.matmul(
                    self.PS[0:1, 2048 + 512 * c:2048 + 512 * c + 512],
                    p.ONEC[:, 0:1], SQ[d % 2][:, 512 * c:512 * c + 512],
                    start=(d == 0), stop=(d == ND - 1)),
                    waits=w2,
                    inc="pe")
            pe_after_sqx[d] = b.val("pe")
        pe_done = b.val("pe")
        for (src_off, dst, bnk) in ((0, R1, 0), (2048, R2, 4)):
            for c in range(nch):
                b.op("vector", lambda e, so=src_off, c=c: e.tensor_copy(
                    p.RP32[0:1, 0:512],
                    self.PS[0:1, so + 512 * c: so + 512 * c + 512]),
                    waits=[("pe", pe_done), base, ("ve", p.t_zeros)], inc="ve")
                vv = b.val("ve")
                b.op("tensor", lambda e, bnk=bnk: e.matmul(
                    self.bank(bnk), p.OH32[:], p.RP32[:, 0:512],
                    start=True, stop=True), waits=[("ve", vv)], inc="pe")
                pv = b.val("pe")
                b.op("vector", lambda e, dst=dst, c=c, bnk=bnk: e.tensor_copy(
                    dst[:, 512 * c:512 * c + 512], self.bank(bnk)),
                    waits=[("pe", pv)], inc="ve")
        self.prelease(range(8), "ve", b.val("ve"))
        b.op("vector", lambda e: e.tensor_scalar_mul(R1[:], R1[:], 1.0 / D))
        b.op("vector", lambda e: e.tensor_scalar_mul(R2[:], R2[:], 1.0 / D))
        b.op("vector", lambda e: e.tensor_mul(R3[:], R1[:], R1[:]))
        b.op("vector", lambda e: e.tensor_sub(R2[:], R2[:], R3[:]))
        b.op("vector", lambda e: e.tensor_scalar_add(R2[:], R2[:], 1.0e-5), inc="ve")
        vv = b.val("ve")
        b.op("scalar", lambda e: e.activation(R2[:], R2[:], AF.Sqrt),
             waits=[("ve", vv)], inc="se")
        sv = b.val("se")
        b.op("vector", lambda e: e.reciprocal(R2[:], R2[:]), waits=[("se", sv)])
        b.op("vector", lambda e: e.tensor_mul(R3[:], R1[:], R2[:]), inc="ve")
        return R2, R3, b.val("ve")

    def normalize(self, dst_bf_view_fn, AREP, BREP, tmp, t0, tn, src_fp32=None,
                  waits=(), gb=None, src_view=None):
        p, b = self.p, self.b
        vals = []
        for d in range(ND):
            if src_view is not None:
                src = src_view(d)
            elif src_fp32 is not None:
                src = src_fp32(d)
            else:
                src = p.xbf_data(d, t0, tn)
            b.op("vector", lambda e, s=src: e.tensor_mul(tmp[:], s, AREP[:]),
                 waits=list(waits) if d == 0 else ())
            if gb is None:
                b.op("vector", lambda e, d=d: e.tensor_sub(
                    dst_bf_view_fn(d), tmp[:], BREP[:]), inc="ve")
            else:
                g_col, b_col = gb
                b.op("vector", lambda e: e.tensor_sub(tmp[:], tmp[:], BREP[:]))
                b.op("vector", lambda e, d=d: e.tensor_scalar(
                    dst_bf_view_fn(d), tmp[:], g_col[d], b_col[d],
                    ALU.mult, ALU.add), inc="ve")
            vals.append(b.val("ve"))
        return vals

    def load_input(self, seq, pool, X):
        p, b = self.p, self.b
        STG = pool.alloc("INSTG", [128, 2, D], F32)
        for d in range(ND):
            b.op("vector", lambda e, d=d: e.memset(p.XBF[:, d, 0:2], 0.0),
                 inc="ve" if d == ND - 1 else None)
        pe_tp = {}
        for tt in range(NT):
            w = [("pe", pe_tp[tt - 2])] if tt >= 2 else []
            b.dma("sync", STG[:, tt % 2],
                  p.x_in[seq, tt * 128:(tt + 1) * 128, :], waits=w, inc="qx")
            qv = b.val("qx")
            pb = (tt % 2) * 2
            for d in range(ND):
                w2 = [("qx", qv)] if d == 0 else []
                if d == 0:
                    w2 += self.pwait([pb, pb + 1])
                b.op("tensor", lambda e, d=d, tt=tt, pb=pb: e.matmul(
                    self.PS[:, pb * 512 + d * 128: pb * 512 + d * 128 + 128],
                    STG[:, tt % 2, d * 128:(d + 1) * 128], p.IDENT32[:],
                    start=True, stop=True), waits=w2,
                    inc="pe" if d == ND - 1 else None)
            pe_tp[tt] = b.val("pe")
            psv = self.PS[:, pb * 512: pb * 512 + 1024].rearrange(
                "p (d c) -> p d c", d=ND)
            b.op("vector", lambda e, tt=tt, psv=psv: e.tensor_copy(
                X[:, :, tt * 128:(tt + 1) * 128], psv),
                waits=[("pe", pe_tp[tt])], inc="ve")
            vcv = b.val("ve")
            b.op("scalar", lambda e, tt=tt, psv=psv: e.copy(
                p.XBF[:, :, 2 + tt * 128: 2 + (tt + 1) * 128], psv),
                waits=[("pe", pe_tp[tt]), ("ve", vcv)], inc="se")
            self.prelease([pb, pb + 1], "se", b.val("se"))
        return b.val("ve"), b.val("se")

    def write_output(self, seq, pool, X=None):
        p, b = self.p, self.b
        STG = pool.alloc("OSTG", [128, 2, D], F32)
        if X is None:
            XF = pool.alloc("XF", [128, 2, T], F32)
            LD = pool.alloc("OLD", [128, 2, T], F32)
        st_val = None
        ve_tile = {}
        for tt in range(NT):
            pb = (tt % 2) * 2
            for d in range(ND):
                if X is not None:
                    lhs = X[:, d, tt * 128:(tt + 1) * 128]
                    w2 = []
                else:
                    raise NotImplementedError
                if d == 0:
                    w2 = w2 + self.pwait([pb, pb + 1])
                b.op("tensor", lambda e, lhs=lhs, d=d, pb=pb: e.matmul(
                    self.PS[:, pb * 512 + d * 128: pb * 512 + d * 128 + 128],
                    lhs, p.IDENT32[:], start=True, stop=True), waits=w2,
                    inc="pe" if d == ND - 1 else None)
            pv = b.val("pe")
            w3 = [("pe", pv)]
            if tt >= 2:
                w3.append(("qx", ve_tile[tt - 2]))
            psv = self.PS[:, pb * 512: pb * 512 + 1024].rearrange(
                "p (d c) -> p d c", d=ND)
            b.op("vector", lambda e, tt=tt, psv=psv: e.tensor_copy(
                STG[:, tt % 2], psv), waits=w3, inc="ve")
            vv = b.val("ve")
            self.prelease([pb, pb + 1], "ve", vv)
            b.dma("sync", p.x_out[seq, tt * 128:(tt + 1) * 128, :],
                  STG[:, tt % 2], waits=[("ve", vv)], inc="qx")
            ve_tile[tt] = b.val("qx")
        return b.val("qx")

    def stage_a(self, seq, pool, X, x_ready):
        p, b = self.p, self.b
        Vtok = pool.alloc("VTOK", [128, NT, D], BF16)
        PH = pool
        ph0 = PH.cur
        sp = PH.fork()
        AREP, BREP, st_v = self.stats(sp, 0, T, False, waits=[x_ready])
        TMP = sp.alloc("TMPA", [128, T], F32)
        xn_vals = self.normalize(lambda d: p.xbf_data(d), AREP, BREP, TMP, 0, T,
                                 src_fp32=lambda d: X[:, d],
                                 waits=[("ve", st_v)])
        xn_done = ("ve", xn_vals[-1])
        self.join()
        if seq == 0:
            p.dump("arep", AREP[:])
            p.dump("brep", BREP[:])
            p.dump("xn", p.XBF[:])
        ap = PH.fork()
        WB = ap.alloc("WA", [128, 2, ND, D], BF16)
        RV = ap.alloc("RV", [128, 2, D], F32)
        SREP = ap.alloc("SREPA", [128, D], BF16)
        B3 = ap.alloc("B3A", [128, D], F32)
        T1 = ap.alloc("T1A", [128, D], F32)
        QD = ap.alloc("QDA", [128, D], BF16)
        assert ap.cur - ph0 <= 57344 + 40960, "A1 phase overflow"
        cS = self.colslots
        TQ, TK = cS["tq"], cS["tk"]
        ACC = cS["acc"]

        wnames = [("L0_attn_Wq", "L0_attn_gq", "L0_attn_vq", "L0_attn_bq", "L0_attn_mq"),
                  ("L0_attn_Wk", "L0_attn_gk", "L0_attn_vk", "L0_attn_bk", "L0_attn_mk"),
                  ("L0_attn_Wv", "L0_attn_gv", "L0_attn_vv", "L0_attn_bv", "L0_attn_mv")]
        wq_dma = {}
        wre = p.w(wnames[0][0], [D, D]).rearrange("(d p) o -> p d o", p=128)
        wq_dma[0] = b.dma("gpsimd", WB[:, 0], wre,
                          waits=[("pe", self.wb_free.get(0, 0))], inc="qw")
        fold_done = {}
        last_row_mm = 0
        sig_vals = []
        for pi in range(3):
            ws = pi % 2
            if pi + 1 < 3:
                wre2 = p.w(wnames[pi + 1][0], [D, D]).rearrange("(d p) o -> p d o", p=128)
                wq_dma[pi + 1] = b.dma("gpsimd", WB[:, 1 - ws], wre2,
                                       waits=[("pe", self.wb_free.get(1 - ws, 0))],
                                       inc="qw")
            Wn, gn, vn, bn, mn = wnames[pi]
            for j, nm in ((0, gn), (1, vn)):
                last_row_mm = self._rep_row(p.w(nm, [D]), RV[:, j], last_row_mm)
            b.op("vector", lambda e: e.tensor_scalar_add(T1[:], RV[:, 1], 1.0e-5),
                 inc="ve")
            vv = b.val("ve")
            b.op("scalar", lambda e: e.activation(T1[:], T1[:], AF.Sqrt),
                 waits=[("ve", vv)], inc="se")
            sv = b.val("se")
            b.op("vector", lambda e: e.reciprocal(T1[:], T1[:]), waits=[("se", sv)])
            b.op("vector", lambda e: e.tensor_mul(T1[:], T1[:], RV[:, 0]))
            b.op("vector", lambda e: e.tensor_copy(SREP[:], T1[:]), inc="ve")
            srep_v = b.val("ve")
            for j, nm in ((0, bn), (1, mn)):
                last_row_mm = self._rep_row(p.w(nm, [D]), RV[:, j], last_row_mm)
            b.op("vector", lambda e: e.tensor_mul(B3[:], RV[:, 1], T1[:]))
            b.op("vector", lambda e: e.tensor_sub(B3[:], RV[:, 0], B3[:]))
            b.op("vector", lambda e: e.tensor_scalar_sub(B3[:], B3[:], 1.0), inc="ve")
            b3_v = b.val("ve")
            for d in range(ND):
                w2 = [("qw", wq_dma[pi]), ("ve", srep_v)] if d == 0 else []
                b.op("vector", lambda e, d=d, ws=ws: e.tensor_mul(
                    WB[:, ws, d], WB[:, ws, d], SREP[:]), waits=w2,
                    inc="ve" if d == ND - 1 else None)
            fold_done[pi] = b.val("ve")
            for tt in range(NT):
                pb = (tt % 2) * 2
                for oc in range(2):
                    for d in range(ND):
                        w2 = []
                        if oc == 0 and d == 0:
                            w2 = [xn_done, ("ve", fold_done[pi])] + self.pwait([pb, pb + 1])
                        b.op("tensor", lambda e, d=d, tt=tt, oc=oc, pb=pb, ws=ws: e.matmul(
                            self.PS[:, pb * 512 + oc * 512: pb * 512 + oc * 512 + 512],
                            p.xbf_data(d)[:, tt * 128:(tt + 1) * 128],
                            WB[:, ws, d, oc * 512:oc * 512 + 512],
                            start=(d == 0), stop=(d == ND - 1)), waits=w2,
                            inc="pe" if (oc == 1 and d == ND - 1) else None)
                mm_v = b.val("pe")
                self.wb_free[ws] = mm_v
                psv = self.PS[:, pb * 512: pb * 512 + 1024]
                b.op("vector", lambda e, psv=psv: e.tensor_add(T1[:], psv, B3[:]),
                     waits=[("pe", mm_v), ("ve", b3_v),
                            ("se", self.a_t1_se)], inc="ve")
                t1v = b.val("ve")
                self.prelease([pb, pb + 1], "ve", t1v)
                if pi == 2:
                    b.op("scalar", lambda e, tt=tt: e.activation(
                        Vtok[:, tt], T1[:], AF.Sigmoid, scale=4.0),
                        waits=[("ve", t1v)], inc="se")
                    sig_vals.append(b.val("se"))
                    self.a_t1_se = b.val("se")
                else:
                    ab = tt % 2
                    w3 = [("ve", t1v)]
                    if tt >= 2:
                        w3.append(("ve", self.acc_read.get((pi, tt - 2), 0)))
                    b.op("scalar", lambda e, ab=ab: e.activation(
                        QD[:], T1[:], AF.Sigmoid, scale=4.0,
                        accum_out=ACC[:, ab:ab + 1]), waits=w3, inc="se")
                    sv = b.val("se")
                    dstc = (TQ if pi == 0 else TK)
                    self.a_t1_se = sv
                    b.op("vector", lambda e, tt=tt, ab=ab, dstc=dstc: e.tensor_scalar(
                        dstc[:, tt:tt + 1], ACC[:, ab:ab + 1], -1.0 / D, 1.0,
                        ALU.mult, ALU.add), waits=[("se", sv)], inc="ve")
                    self.acc_read[(pi, tt)] = b.val("ve")
        v_ready = ("se", sig_vals[-1])
        tqk_ready = ("ve", self.acc_read[(1, NT - 1)])
        self.join()
        if seq == 0:
            p.dump("tq", cS["tq"][:])
            p.dump("tk", cS["tk"][:])
            p.dump("vtok", Vtok[:])

        a2 = PH.fork()
        AVT = a2.alloc("AVT", [128, ND, T], BF16)
        TQR = a2.alloc("TQR", [128, 512], F32)
        E1 = a2.alloc("E1", [128, 512], F32)
        E2 = a2.alloc("E2", [128, 512], F32)
        AB = a2.alloc("AB", [128, 2, 512], BF16)
        TK2 = cS["tk2"]; NTK2 = cS["ntk2"]
        TQT = a2.alloc("TQT", [16, 128], F32)
        b.op("vector", lambda e: e.tensor_scalar_mul(TK2[:], TK[:], 2.0),
             waits=[tqk_ready])
        b.op("vector", lambda e: e.tensor_scalar_mul(NTK2[:], TK[:], -2.0), inc="ve")
        tk2_v = b.val("ve")
        b.op("tensor", lambda e: e.matmul(
            self.PS[:, 512:512 + 128].bitcast(F32)[0:16, :] if False else
            self.PS[0:16, 512:512 + 128], TQ[:], p.IDENT32[:],
            start=True, stop=True),
            waits=[tqk_ready, ("ve", tk2_v)] + self.pwait([1]), inc="pe")
        pv = b.val("pe")
        b.op("vector", lambda e: e.tensor_copy(TQT[:], self.PS[0:16, 512:512 + 128]),
             waits=[("pe", pv)], inc="ve")
        tqt_v = b.val("ve")
        self.prelease([1], "ve", tqt_v)
        av_pe = {}
        abuf_v = {}
        evac_v = 0
        for qg in range(4):
            w = [("ve", tqt_v), ("ve", p.t_zeros), ("pe", self.rp32_user)]
            b.dma("sync", p.RP32[0:1, 0:512],
                  TQT[qg * 4:(qg + 1) * 4, :], waits=w, inc="qx")
            qv = b.val("qx")
            b.op("tensor", lambda e: e.matmul(
                self.bank(0), p.OH32[:], p.RP32[:, 0:512], start=True, stop=True),
                waits=[("qx", qv)] + self.pwait([0]), inc="pe")
            self.rp32_user = b.val("pe")
            b.op("vector", lambda e: e.tensor_copy(TQR[:], self.bank(0)),
                 waits=[("pe", self.rp32_user)], inc="ve")
            tqr_v = b.val("ve")
            self.prelease([0], "ve", tqr_v)
            for kt in range(NT):
                par = kt % 2
                w_s1 = [("ve", tqr_v), ("ve", tk2_v)]
                if (qg, kt) != (0, 0):
                    w_s1.append(("ve", abuf_v.get((qg, kt - 1), abuf_v.get((qg - 1, NT - 1), 0))))
                b.op("scalar", lambda e, kt=kt: e.activation(
                    E1[:], TQR[:], AF.Exp, scale=-2.0, bias=TK2[:, kt:kt + 1]),
                    waits=w_s1)
                b.op("scalar", lambda e, kt=kt: e.activation(
                    E2[:], TQR[:], AF.Exp, scale=2.0, bias=NTK2[:, kt:kt + 1]),
                    inc="se")
                sv = b.val("se")
                b.op("vector", lambda e: e.tensor_tensor(E1[:], E1[:], E2[:], ALU.min),
                     waits=[("se", sv)])
                b.op("vector", lambda e, kt=kt: e.tensor_scalar(
                    E2[:], TQR[:], TK[:, kt:kt + 1], 2.0, ALU.is_lt, ALU.mult))
                b.op("vector", lambda e: e.tensor_mul(E2[:], E1[:], E2[:]))
                b.op("vector", lambda e: e.tensor_sub(E2[:], E2[:], E1[:]))
                wv = []
                if kt >= 2:
                    wv = [("pe", av_pe[(qg, kt - 2)])]
                elif qg > 0 and kt < 2:
                    wv = [("pe", av_pe[(qg - 1, NT - 2 + kt)])]
                b.op("vector", lambda e, par=par: e.tensor_scalar(
                    AB[:, par], E2[:], 0.9, 0.9, ALU.mult, ALU.add), waits=wv,
                    inc="ve")
                abuf_v[(qg, kt)] = b.val("ve")
                for et in range(ND):
                    w2 = []
                    if et == 0:
                        w2 = [("ve", abuf_v[(qg, kt)])]
                        if kt == 0:
                            w2 += self.pwait(range(8)) + [v_ready]
                    b.op("tensor", lambda e, kt=kt, et=et, par=par: e.matmul(
                        self.bank(et), Vtok[:, kt, et * 128:(et + 1) * 128],
                        AB[:, par], start=(kt == 0), stop=(kt == NT - 1)),
                        waits=w2, inc="pe" if et == ND - 1 else None)
                av_pe[(qg, kt)] = b.val("pe")
            for et in range(ND):
                b.op("vector", lambda e, et=et, qg=qg: e.tensor_copy(
                    AVT[:, et, qg * 512:(qg + 1) * 512], self.bank(et)),
                    waits=[("pe", av_pe[(qg, NT - 1)])], inc="ve")
            evac_v = b.val("ve")
            self.prelease(range(8), "ve", evac_v)
        avt_ready = ("ve", evac_v)
        self.join()
        if seq == 0:
            p.dump("avt", AVT[:])

        a3off = a2.cur
        WO = a2.alloc("WOA", [128, 2, ND, 128], BF16)
        wo = p.w("L0_attn_Wo", [D, D])
        wo_dma = {}
        for od in range(2):
            wo_dma[od] = b.dma("gpsimd", WO[:, od % 2],
                               wo[:, od * 128:(od + 1) * 128].rearrange(
                                   "(e p) c -> p e c", p=128), inc="qw")
        bo_col = self.colslots["bo"]
        if seq == 0:
            p.dump("wo0", WO[:, 0])
            p.dump("bocol", bo_col)
        res_v = {}
        for od in range(ND):
            for tc in range(4):
                bk = tc % 2
                for et in range(ND):
                    w2 = []
                    if et == 0:
                        w2 = self.pwait([bk]) + [("qw", wo_dma[od]), avt_ready]
                    b.op("tensor", lambda e, od=od, tc=tc, et=et, bk=bk: e.matmul(
                        self.bank(bk), WO[:, od % 2, et], AVT[:, et, tc * 512:(tc + 1) * 512],
                        start=(et == 0), stop=(et == ND - 1)), waits=w2,
                        inc="pe" if et == ND - 1 else None)
                pv = b.val("pe")
                if tc == 3:
                    self.wo_free[od % 2] = pv
                b.op("vector", lambda e, od=od, tc=tc, bk=bk: e.scalar_tensor_tensor(
                    X[:, od, tc * 512:(tc + 1) * 512], self.bank(bk),
                    bo_col[:, od:od + 1], X[:, od, tc * 512:(tc + 1) * 512],
                    ALU.add, ALU.add), waits=[("pe", pv), ("qx", self.bocol_ld)],
                    inc="ve")
                self.prelease([bk], "ve", b.val("ve"))
            res_v[od] = b.val("ve")
            if od + 2 < ND:
                wo_dma[od + 2] = b.dma(
                    "gpsimd", WO[:, od % 2],
                    wo[:, (od + 2) * 128:(od + 3) * 128].rearrange(
                        "(e p) c -> p e c", p=128),
                    waits=[("pe", self.wo_free[od % 2])], inc="qw")
            b.op("scalar", lambda e, od=od: e.copy(
                p.XBF[:, od, 2:2 + T], X[:, od]), waits=[("ve", res_v[od])],
                inc="se")
        self.join()
        if seq == 0:
            p.dump("xafter", X[:])
        return ("se", b.val("se"))

    def _rep_row(self, row_dram, dst, prev_mm):
        p, b = self.p, self.b
        n = row_dram.shape[0]
        b.dma("sync", p.RP32[0:1, 0:n], row_dram.unsqueeze(0),
              waits=[("ve", p.t_zeros), ("pe", self.rp32_user)], inc="qx")
        qv = b.val("qx")
        nch = (n + 511) // 512
        for c in range(nch):
            b.op("tensor", lambda e, c=c: e.matmul(
                self.bank(0 if c == 0 else 1),
                p.OH32[:], p.RP32[:, c * 512:(c + 1) * 512],
                start=True, stop=True),
                waits=[("qx", qv)] + self.pwait([c]), inc="pe")
        self.rp32_user = b.val("pe")
        pv = self.rp32_user
        for c in range(nch):
            b.op("vector", lambda e, c=c: e.tensor_copy(
                dst[:, c * 512:(c + 1) * 512], self.bank(c)),
                waits=[("pe", pv)], inc="ve")
        self.prelease(list(range(nch)), "ve", b.val("ve"))
        return b.val("ve")


def build_core_program():
    p = Prog()
    p.setup_persistent()
    st = Stages(p)
    st.bind_cols()
    b = p.b

    bo = p.w("L0_attn_bo", [D])
    b.dma("sync", st.colslots["bo"], bo.rearrange("(d q) -> q d", q=128), inc="qx", slow=True)
    st.bocol_ld = b.val("qx")
    for i, nm in enumerate(["g_qq", "b_qq", "g_qk", "b_qk", "g_lq", "b_lq",
                            "g_lk", "b_lk"]):
        b.dma("sync", st.colslots["gau"][:, i:i + 1],
              p.w(f"L1_attn_{nm}", [128]).unsqueeze(1), inc="qx")
    b.dma("sync", st.colslots["bgc"],
          p.w("L1_attn_bg", [2048]).rearrange("(e q) -> q e", q=128), inc="qx",
          slow=True)
    b.dma("sync", st.colslots["binc"],
          p.w("L1_attn_bin_", [128]).unsqueeze(1), inc="qx")
    b.dma("sync", st.colslots["bout"],
          p.w("L1_attn_bout", [D]).rearrange("(d q) -> q d", q=128), inc="qx",
          slow=True)
    st.c_bg_ld = b.val("qx")
    cqk = float(2048.0 ** -0.25)
    for i in range(4):
        b.op("vector", lambda e, i=i: e.tensor_scalar(
            st.colslots["gau"][:, i:i + 1], st.colslots["gau"][:, i:i + 1],
            cqk, None, ALU.mult),
            waits=[("qx", st.c_bg_ld)] if i == 0 else (),
            inc="ve" if i == 3 else None)
    st.c_gau_v = b.b.val("ve") if False else b.val("ve")

    dbg = DBG_STAGE
    def seq_end():
        st.join()
    for seq in range(SEQS):
        pool = p.mem.pool()
        X = pool.alloc("X", [128, ND, T], F32)
        ld_ve, ld_se = st.load_input(seq, pool.fork(), X)
        x_ready = ("se", ld_se)
        if dbg == "in":
            st.join()
            last = st.write_output(seq, pool.fork(), X=X)
            seq_end()
            continue
        st.join()
        xr = st.stage_a(seq, pool.fork(), X, x_ready)
        st.join()
        if dbg == "a":
            last = st.write_output(seq, pool.fork(), X=X)
            seq_end()
            continue
        xr = st.stage_b(seq, pool.fork(), X, xr)
        st.join()
        if dbg == "b":
            last = st.write_output(seq, pool.fork(), X=X)
            seq_end()
            continue
        spill_qx = st.spill_x(seq, X)
        st.join()
        st.stage_c(seq, p.mem.pool(), spill_qx)
        st.join()
        if dbg == "c":
            last = st.write_output_hbm(seq, p.mem.pool(), [p.hb_xo])
            seq_end()
            continue
        st.stage_d(seq, p.mem.pool(), p.b.val("qx"))
        st.join()
        last = st.write_output_hbm(seq, p.mem.pool(), [p.hb_xo, p.hb_xi])
        seq_end()
    b.op("sync", lambda e: e.nop(), waits=[("qx", b.val("qx"))])
    b.emit()
    return p


def kernel(**inputs):
    x_full = np.ascontiguousarray(inputs["L0_attn_x"], dtype=np.float32)
    assert x_full.shape == (NCORES * SEQS, T, D)
    p = build_core_program()
    in_maps = []
    wcache = {}
    for name in p._win:
        if name in p.inputs:
            wcache[name] = p.inputs[name]
        else:
            wcache[name] = np.ascontiguousarray(inputs[name], dtype=np.float32)
    for c in range(NCORES):
        m = dict(wcache)
        m["x"] = x_full[c * SEQS:(c + 1) * SEQS]
        in_maps.append(m)
    res = run_bass_kernel_spmd(p.nc, in_maps, core_ids=list(range(NCORES)))
    out = np.concatenate([np.asarray(r["y"]) for r in res.results], axis=0)
    return out.astype(np.float32)


def _stage_b(self, seq, pool, X, xr):
    p, b = self.p, self.b
    cS = self.colslots
    ph0 = pool.cur
    if seq == 0:
        f1 = pool.fork()
        RH32 = f1.alloc("RH32", [128, ND, D], F32)
        RT32 = f1.alloc("RT32", [128, ND, D], F32)
        rhs = p.w("L0_ffn_rhs", [D, D])
        b.dma("sync", RH32[:], rhs.rearrange("(d q) o -> q d o", q=128),
              waits=[xr], inc="qx")
        qv = b.val("qx")
        for ot in range(ND):
            pb = (ot % 2) * 2
            for d in range(ND):
                w2 = [("qx", qv)] + self.pwait([pb, pb + 1]) if d == 0 else []
                b.op("tensor", lambda e, ot=ot, d=d, pb=pb: e.matmul(
                    self.PS[:, pb * 512 + d * 128: pb * 512 + d * 128 + 128],
                    RH32[:, d, ot * 128:(ot + 1) * 128], p.IDENT32[:],
                    start=True, stop=True), waits=w2,
                    inc="pe" if d == ND - 1 else None)
            pv = b.val("pe")
            psv = self.PS[:, pb * 512: pb * 512 + 1024]
            b.op("vector", lambda e, ot=ot, psv=psv: e.tensor_copy(
                RT32[:, ot], psv), waits=[("pe", pv)], inc="ve")
            self.prelease([pb, pb + 1], "ve", b.val("ve"))
        for ot in range(ND):
            b.op("vector", lambda e, ot=ot: e.tensor_reduce(
                cS["am"][:, ot:ot + 1], RT32[:, ot], AX.X, ALU.max,
                apply_absolute_value=True), inc="ve" if ot == ND - 1 else None)
        amv = b.val("ve")
        b.op("vector", lambda e: e.tensor_scalar(cS["rs"][:], cS["am"][:],
                                                 1.0 / 127.0, None, ALU.mult),
             waits=[("ve", amv)])
        b.op("vector", lambda e: e.reciprocal(cS["rc"][:], cS["am"][:]),
             waits=[("ve", amv)], inc="ve")
        rcv0 = b.val("ve")
        b.op("vector", lambda e: e.tensor_scalar(cS["rc"][:], cS["rc"][:],
                                                 127.0, None, ALU.mult),
             waits=[("ve", rcv0)], inc="ve")
        rc_v = b.val("ve")
        self.join()
        f2 = PoolCursor(p.mem, ph0)
        QR = f2.alloc("QRH", [128, ND, D], BF16)
        QTB = f2.alloc("QTB", [128, D], BF16)
        TMPQ = f2.alloc("TMPQB", [128, D], F32)
        I32T = f2.alloc("I32B", [128, D], I32)
        assert f2.cur <= ph0 + 32768 + 4096, "B prep overlay overflow"
        for ot in range(ND):
            b.op("vector", lambda e, ot=ot: e.tensor_scalar(
                TMPQ[:], RT32[:, ot], cS["rc"][:, ot:ot + 1], None, ALU.mult),
                waits=[("ve", rc_v)] if ot == 0 else ())
            b.op("vector", lambda e: e.tensor_copy(I32T[:], TMPQ[:]))
            b.op("vector", lambda e: e.tensor_copy(TMPQ[:], I32T[:]))
            b.op("vector", lambda e: e.tensor_copy(QTB[:], TMPQ[:]), inc="ve")
            qtv = b.val("ve")
            for d in range(ND):
                pb2 = 4 + (d % 2)
                b.op("tensor", lambda e, d=d, pb2=pb2: e.matmul(
                    self.bank(pb2, n=128), QTB[:, d * 128:(d + 1) * 128],
                    p.IDENT[:], start=True, stop=True),
                    waits=[("ve", qtv)] + self.pwait([pb2]), inc="pe")
                pv = b.val("pe")
                b.op("vector", lambda e, d=d, ot=ot, pb2=pb2: e.tensor_copy(
                    QR[:, d, ot * 128:(ot + 1) * 128], self.bank(pb2, n=128)),
                    waits=[("pe", pv)], inc="ve")
                self.prelease([pb2], "ve", b.val("ve"))
        b.dma("sync", p.hb_qrhs.rearrange("q (d o) -> q d o", d=ND), QR[:],
              waits=[("ve", b.val("ve"))], inc="qx")
        self.join()
    f = PoolCursor(p.mem, ph0)
    QR = f.alloc("QRS", [128, ND, D], BF16)
    LSREP = f.alloc("LSREP", [128, T], F32)
    LST = f.alloc("LST", [16, 128], F32)
    TMPQ = f.alloc("TMPQS", [128, D], F32)
    I32T = f.alloc("I32S", [128, D], I32)
    TMP2 = f.alloc("TMP2B", [128, 512], F32)
    qr_dma = b.dma("gpsimd", QR[:], p.hb_qrhs, waits=[("qx", b.val("qx"))], inc="qw")
    sp = f.fork()
    XNT = f.alloc("XNT", [128, NT, D], BF16)
    AREP, BREP, st_v = self.stats(sp, 0, T, False, waits=[xr])
    TMPN = sp.alloc("TMPNB", [128, T], F32)
    xn_vals = self.normalize(lambda d: p.xbf_data(d), AREP, BREP, TMPN, 0, T,
                             src_fp32=lambda d: X[:, d], waits=[("ve", st_v)])
    xn_done = ("ve", xn_vals[-1])
    self.join()
    if seq == 0:
        p.dump("bxn", p.XBF[:])
    for tt in range(NT):
        pb = (tt % 2) * 2
        for d in range(ND):
            w2 = self.pwait([pb, pb + 1]) if d == 0 else []
            b.op("tensor", lambda e, tt=tt, d=d, pb=pb: e.matmul(
                self.PS[:, pb * 512 + d * 128: pb * 512 + d * 128 + 128],
                p.xbf_data(d)[:, tt * 128:(tt + 1) * 128], p.IDENT[:],
                start=True, stop=True), waits=w2,
                inc="pe" if d == ND - 1 else None)
        pv = b.val("pe")
        psv = self.PS[:, pb * 512: pb * 512 + 1024]
        b.op("vector", lambda e, psv=psv: e.tensor_copy(TMPQ[:], psv),
             waits=[("pe", pv)], inc="ve")
        self.prelease([pb, pb + 1], "ve", b.val("ve"))
        b.op("vector", lambda e: e.tensor_reduce(
            cS["lt"][:, 0:1], TMPQ[:], AX.X, ALU.max, apply_absolute_value=True),
            inc="ve")
        ltv0 = b.val("ve")
        b.op("vector", lambda e, tt=tt: e.tensor_scalar(
            cS["ls"][:, tt:tt + 1], cS["lt"][:, 0:1], 1.0 / 127.0, None, ALU.mult),
            waits=[("ve", ltv0)])
        b.op("vector", lambda e: e.reciprocal(cS["lt"][:, 1:2], cS["lt"][:, 0:1]),
             waits=[("ve", ltv0)], inc="ve")
        ltv1 = b.val("ve")
        b.op("vector", lambda e: e.tensor_scalar(
            cS["lt"][:, 1:2], cS["lt"][:, 1:2], 127.0, None, ALU.mult),
            waits=[("ve", ltv1)], inc="ve")
        ltv = b.val("ve")
        b.op("vector", lambda e: e.tensor_scalar(
            TMPQ[:], TMPQ[:], cS["lt"][:, 1:2], None, ALU.mult),
            waits=[("ve", ltv)])
        b.op("vector", lambda e: e.tensor_copy(I32T[:], TMPQ[:]))
        b.op("vector", lambda e: e.tensor_copy(TMPQ[:], I32T[:]))
        b.op("vector", lambda e, tt=tt: e.tensor_copy(XNT[:, tt], TMPQ[:]), inc="ve")
    if seq == 0:
        self.join()
        p.dump("qlhs", XNT[:])
        p.dump("tmpq", TMPQ[:])
        p.dump("i32t", I32T[:])
        p.dump("ltcol", cS["lt"])
        p.dump("lscol", cS["ls"])
        p.dump("rscol", cS["rs"])
        p.dump("qrhs", QR[:])
    b.op("tensor", lambda e: e.matmul(self.PS[0:16, 512:512 + 128], cS["ls"][:],
                                      p.IDENT32[:], start=True, stop=True),
         waits=[("ve", b.val("ve"))] + self.pwait([1]), inc="pe")
    pv = b.val("pe")
    b.op("vector", lambda e: e.tensor_copy(LST[:], self.PS[0:16, 512:512 + 128]),
         waits=[("pe", pv)], inc="ve")
    lst_v = b.val("ve")
    self.prelease([1], "ve", lst_v)
    for tc in range(4):
        b.dma("sync", p.RP32[0:1, 0:512], LST[tc * 4:(tc + 1) * 4, :],
              waits=[("ve", lst_v), ("ve", p.t_zeros), ("pe", self.rp32_user)],
              inc="qx")
        qv = b.val("qx")
        b.op("tensor", lambda e: e.matmul(self.bank(1), p.OH32[:],
                                          p.RP32[:, 0:512], start=True, stop=True),
             waits=[("qx", qv)] + self.pwait([1]), inc="pe")
        self.rp32_user = b.val("pe")
        b.op("vector", lambda e, tc=tc: e.tensor_copy(
            LSREP[:, tc * 512:(tc + 1) * 512], self.bank(1)),
            waits=[("pe", self.rp32_user)], inc="ve")
        self.prelease([1], "ve", b.val("ve"))
    lsrep_v = b.val("ve")
    tpq_done = {}
    for d in range(ND):
        for tg in range(4):
            pb = 4 + (tg % 2)
            for ti in range(4):
                tt = tg * 4 + ti
                w2 = self.pwait([pb]) if ti == 0 else []
                b.op("tensor", lambda e, d=d, tt=tt, ti=ti, pb=pb: e.matmul(
                    self.bank(pb, n=128, off=ti * 128),
                    XNT[:, tt, d * 128:(d + 1) * 128], p.IDENT[:],
                    start=True, stop=True), waits=w2,
                    inc="pe" if ti == 3 else None)
            pv = b.val("pe")
            b.op("vector", lambda e, d=d, tg=tg, pb=pb: e.tensor_copy(
                p.xbf_data(d)[:, tg * 512:(tg + 1) * 512], self.bank(pb)),
                waits=[("pe", pv)], inc="ve")
            self.prelease([pb], "ve", b.val("ve"))
    qc_done = b.val("ve")
    for od in range(ND):
        for tc in range(4):
            bk = tc % 2
            for d in range(ND):
                w2 = []
                if d == 0:
                    w2 = self.pwait([bk]) + [("ve", qc_done), ("qw", qr_dma)]
                b.op("tensor", lambda e, od=od, tc=tc, d=d, bk=bk: e.matmul(
                    self.bank(bk), QR[:, d, od * 128:(od + 1) * 128],
                    p.xbf_data(d)[:, tc * 512:(tc + 1) * 512],
                    start=(d == 0), stop=(d == ND - 1)), waits=w2,
                    inc="pe" if d == ND - 1 else None)
            pv = b.val("pe")
            b.op("vector", lambda e, od=od, tc=tc, bk=bk: e.scalar_tensor_tensor(
                TMP2[:], self.bank(bk), cS["rs"][:, od:od + 1],
                LSREP[:, tc * 512:(tc + 1) * 512], ALU.mult, ALU.mult),
                waits=[("pe", pv), ("ve", lsrep_v)], inc="ve")
            vv = b.val("ve")
            self.prelease([bk], "ve", vv)
            b.op("vector", lambda e, od=od, tc=tc: e.tensor_add(
                X[:, od, tc * 512:(tc + 1) * 512],
                X[:, od, tc * 512:(tc + 1) * 512], TMP2[:]), inc="ve")
    bmm_done = b.val("pe")
    res_done = b.val("ve")
    if seq == 0:
        self.join()
        p.dump("lsrep", LSREP[:])
        p.dump("xafterb", X[:])
    for od in range(ND):
        b.op("scalar", lambda e, od=od: e.copy(
            p.XBF[:, od, 2:2 + T], X[:, od]),
            waits=[("pe", bmm_done), ("ve", res_done)], inc="se")
    self.join()
    return ("se", b.val("se"))


Stages.stage_b = _stage_b


def _stage_c(self, seq, pool, spill_qx=0):
    p, b = self.p, self.b
    cS = self.colslots
    GP = 4
    WV = pool.alloc("WVC", [128, ND, 2048], BF16)
    WG = pool.alloc("WGC", [128, ND, 2048], BF16)
    WIN = pool.alloc("WINC", [128, ND, 128], BF16)
    WOT = pool.alloc("WOTC", [128, 2, 16, 128], BF16)
    XNC = pool.alloc("XNC", [128, ND, 512], BF16)
    VT = pool.alloc("VTC", [128, 4, 2048], BF16)
    OT = pool.alloc("OTC", [128, 16, 512], BF16)
    KV32 = pool.alloc("KV32", [128, 2048], F32)
    KVBF = pool.alloc("KVBF", [128, 2048], BF16)
    BVR = pool.alloc("BVR", [128, 2048], F32)
    scratch_off = pool.cur
    XH = pool.alloc("XH", [128, 512], F32)
    QQ = pool.alloc("QQC", [128, 512], BF16)
    QK = pool.alloc("QKC", [128, 512], BF16)
    LQ = pool.alloc("LQC", [128, 512], BF16)
    LK = pool.alloc("LKC", [128, 512], BF16)
    ATT = pool.alloc("ATTC", [128, 2, 256], BF16)
    LKT = pool.alloc("LKT", [128, 2, 128], BF16)
    TT = pool.alloc("TTC", [128, 512], F32)
    TV = pool.alloc("TVC", [128, 512], F32)
    XST = pool.alloc("XSTC", [128, 2, 512], F32)
    XOLD = pool.alloc("XOLDC", [128, 2, 512], F32)

    wv = p.w("L1_attn_Wv", [D, 2048]).rearrange("(d q) o -> q d o", q=128)
    wg = p.w("L1_attn_Wg", [D, 2048]).rearrange("(d q) o -> q d o", q=128)
    win = p.w("L1_attn_Win", [D, 128]).rearrange("(d q) o -> q d o", q=128)
    dv = b.dma("gpsimd", WV[:], wv,
               waits=[("pe", self.c_w_free), ("qx", spill_qx)], inc="qw")
    dg = b.dma("gpsimd", WG[:], wg, waits=[("qx", spill_qx)], inc="qw")
    dn = b.dma("gpsimd", WIN[:], win, inc="qw")
    bvw = p.w("L1_attn_bv", [2048])
    for c in range(4):
        b.dma("sync", p.RP32[0:1, 0:512], bvw[c * 512:(c + 1) * 512].unsqueeze(0),
              waits=[("ve", p.t_zeros), ("pe", self.rp32_user)], inc="qx")
        qv = b.val("qx")
        b.op("tensor", lambda e: e.matmul(self.bank(0), p.OH32[:], p.RP32[:, 0:512],
                                          start=True, stop=True),
             waits=[("qx", qv)] + self.pwait([0]), inc="pe")
        self.rp32_user = b.val("pe")
        b.op("vector", lambda e, c=c: e.tensor_copy(
            BVR[:, c * 512:(c + 1) * 512], self.bank(0)),
            waits=[("pe", self.rp32_user)], inc="ve")
        self.prelease([0], "ve", b.val("ve"))
    bvr_v = b.val("ve")
    b.op("vector", lambda e: e.memset(KV32[:], 0.0))
    b.op("vector", lambda e: e.memset(KVBF[:], 0.0), inc="ve")
    kv_v = b.val("ve")
    kvbf_pe = 0

    wout = p.w("L1_attn_Wout", [2048, D])
    gaus = self.c_gau_v

    xh_se = 0
    att_pe = {}
    for gp in range(GP):
        t0 = gp * 512
        sp = PoolCursor(p.mem, scratch_off)
        AREP, BREP, st_v = self.stats(sp, t0, 512, False, waits=[("ve", kv_v)])
        TMPN = sp.alloc("TMPNC", [128, 512], F32)
        xn_vals = self.normalize(lambda d: XNC[:, d], AREP, BREP, TMPN, t0, 512,
                                 waits=[("ve", st_v)])
        xn_v = ("ve", xn_vals[-1])
        vt_se = []
        for tt in range(4):
            for oc in range(4):
                bk = (tt * 4 + oc) % 2
                for d in range(ND):
                    w2 = []
                    if d == 0:
                        w2 = self.pwait([bk]) + [xn_v, ("qw", dv)]
                    b.op("tensor", lambda e, tt=tt, oc=oc, d=d, bk=bk: e.matmul(
                        self.bank(bk), XNC[:, d, tt * 128:(tt + 1) * 128],
                        WV[:, d, oc * 512:(oc + 1) * 512],
                        start=(d == 0), stop=(d == ND - 1)), waits=w2,
                        inc="pe" if d == ND - 1 else None)
                pv = b.val("pe")
                b.op("vector", lambda e, oc=oc, bk=bk: e.tensor_add(
                    TV[:], self.bank(bk), BVR[:, oc * 512:(oc + 1) * 512]),
                    waits=[("pe", pv), ("ve", bvr_v),
                           ("se", vt_se[-1] if vt_se else 0)], inc="ve")
                vv = b.val("ve")
                self.prelease([bk], "ve", vv)
                b.op("scalar", lambda e, tt=tt, oc=oc: e.activation(
                    VT[:, tt, oc * 512:(oc + 1) * 512], TV[:], AF.Silu),
                    waits=[("ve", vv)], inc="se")
                vt_se.append(b.val("se"))
        vt_ready = ("se", vt_se[-1])
        if seq == 0 and gp == 0:
            p.dump("cxn", XNC[:])
            p.dump("cvt", VT[:])
        gt_se = []
        for et in range(16):
            bk = 2 + et % 2
            for d in range(ND):
                w2 = []
                if d == 0:
                    w2 = self.pwait([bk]) + [xn_v, ("qw", dg), ("qx", self.c_bg_ld)]
                b.op("tensor", lambda e, et=et, d=d, bk=bk: e.matmul(
                    self.bank(bk), WG[:, d, et * 128:(et + 1) * 128],
                    XNC[:, d, 0:512],
                    start=(d == 0), stop=(d == ND - 1)), waits=w2,
                    inc="pe" if d == ND - 1 else None)
            pv = b.val("pe")
            b.op("scalar", lambda e, et=et, bk=bk: e.activation(
                OT[:, et], self.bank(bk), AF.Silu,
                bias=cS["bgc"][:, et:et + 1]), waits=[("pe", pv)], inc="se")
            gt_se.append(b.val("se"))
            self.prelease([bk], "se", b.val("se"))
        gt_ready = ("se", gt_se[-1])
        if seq == 0 and gp == 0:
            p.dump("cgate", OT[:])
        for d in range(ND):
            w2 = []
            if d == 0:
                w2 = self.pwait([4]) + [xn_v, ("qw", dn), ("qx", self.c_bg_ld)]
            b.op("tensor", lambda e, d=d: e.matmul(
                self.bank(4), WIN[:, d], XNC[:, d, 0:512],
                start=(d == 0), stop=(d == ND - 1)), waits=w2,
                inc="pe" if d == ND - 1 else None)
        pv = b.val("pe")
        b.op("scalar", lambda e: e.activation(
            XH[:], self.bank(4), AF.Silu, bias=cS["binc"][:]),
            waits=[("pe", pv), ("se", xh_se)], inc="se")
        xh_se = b.val("se")
        self.prelease([4], "se", xh_se)
        if seq == 0 and gp == 0:
            p.dump("cxh", XH[:])
        for (dst, gsl, bsl) in ((QQ, 0, 1), (QK, 2, 3), (LQ, 4, 5), (LK, 6, 7)):
            b.op("vector", lambda e, dst=dst, gsl=gsl, bsl=bsl: e.tensor_scalar(
                dst[:], XH[:], cS["gau"][:, gsl:gsl + 1], cS["gau"][:, bsl:bsl + 1],
                ALU.mult, ALU.add),
                waits=[("se", xh_se), ("ve", gaus), ("pe", att_pe.get(gp - 1, 0))],
                inc="ve")
        qq_v = b.val("ve")
        if seq == 0 and gp == 0:
            p.dump("cqq", QQ[:]); p.dump("cqk", QK[:])
            p.dump("clq", LQ[:]); p.dump("clk", LK[:])
        for gi in range(2):
            g0 = gi * 256
            at_v = []
            for mc in range(2):
                b.op("tensor", lambda e, g0=g0, mc=mc: e.matmul(
                    self.bank(5, n=256, off=mc * 256),
                    QK[:, g0 + mc * 128: g0 + mc * 128 + 128],
                    QQ[:, g0:g0 + 256], start=True, stop=True),
                    waits=(self.pwait([5]) + [("ve", qq_v)]) if mc == 0 else [],
                    inc="pe" if mc == 1 else None)
            pv = b.val("pe")
            for mc in range(2):
                b.op("scalar", lambda e, mc=mc: e.activation(
                    TT[:, mc * 256:(mc + 1) * 256],
                    self.bank(5, n=256, off=mc * 256), AF.Relu),
                    waits=[("pe", pv), ("ve", self.c_tt_free)], inc="se")
            sv = b.val("se")
            self.prelease([5], "se", sv)
            for mc in range(2):
                w3 = [("se", sv), ("pe", kvbf_pe)] if mc == 0 else []
                b.op("vector", lambda e, mc=mc: e.tensor_mul(
                    TT[:, mc * 256:(mc + 1) * 256], TT[:, mc * 256:(mc + 1) * 256],
                    TT[:, mc * 256:(mc + 1) * 256]), waits=w3)
                b.op("vector", lambda e, mc=mc: e.tensor_mul(
                    ATT[:, mc], TT[:, mc * 256:(mc + 1) * 256], p.CMASK[:, mc]),
                    inc="ve")
            at_ready = b.val("ve")
            self.c_tt_free = at_ready
            o_ve = []
            for et in range(16):
                bk = 6 + et % 2
                for mc in range(2):
                    w2 = []
                    if mc == 0:
                        w2 = self.pwait([bk]) + [("ve", at_ready), vt_ready]
                    b.op("tensor", lambda e, gp=gp, gi=gi, mc=mc, et=et, bk=bk: e.matmul(
                        self.bank(bk, n=256), VT[:, gi * 2 + mc, et * 128:(et + 1) * 128],
                        ATT[:, mc], start=(mc == 0), stop=False), waits=w2)
                b.op("tensor", lambda e, et=et, bk=bk, g0=g0: e.matmul(
                    self.bank(bk, n=256), KVBF[:, et * 128:(et + 1) * 128],
                    LQ[:, g0:g0 + 256], start=False, stop=True),
                    waits=[("ve", kv_v)] if et == 0 else [], inc="pe")
                pv = b.val("pe")
                b.op("vector", lambda e, et=et, g0=g0, bk=bk: e.tensor_mul(
                    OT[:, et, g0:g0 + 256], self.bank(bk, n=256), OT[:, et, g0:g0 + 256]),
                    waits=[("pe", pv), gt_ready], inc="ve")
                o_ve.append(b.val("ve"))
                self.prelease([bk], "ve", b.val("ve"))
            kvbf_pe = b.val("pe")
            att_pe[gp] = kvbf_pe
            for mc in range(2):
                b.op("tensor", lambda e, g0=g0, mc=mc: e.matmul(
                    self.bank(5, n=128, off=mc * 128),
                    LK[:, g0 + mc * 128: g0 + mc * 128 + 128], p.IDENT[:],
                    start=True, stop=True),
                    waits=(self.pwait([5]) + [("ve", qq_v)]) if mc == 0 else [],
                    inc="pe" if mc == 1 else None)
            pv = b.val("pe")
            for mc in range(2):
                b.op("vector", lambda e, mc=mc: e.tensor_copy(
                    LKT[:, mc], self.bank(5, n=128, off=mc * 128)),
                    waits=[("pe", pv)], inc="ve")
            lkt_v = b.val("ve")
            self.prelease([5], "ve", lkt_v)
            for ec in range(4):
                bk = ec % 2
                for mc in range(2):
                    w2 = []
                    if mc == 0:
                        w2 = self.pwait([bk]) + [("ve", lkt_v)]
                    b.op("tensor", lambda e, gi=gi, mc=mc, ec=ec, bk=bk: e.matmul(
                        self.bank(bk), LKT[:, mc],
                        VT[:, gi * 2 + mc, ec * 512:(ec + 1) * 512],
                        start=(mc == 0), stop=(mc == 1)), waits=w2,
                        inc="pe" if mc == 1 else None)
                pv = b.val("pe")
                b.op("vector", lambda e, ec=ec, bk=bk: e.tensor_add(
                    KV32[:, ec * 512:(ec + 1) * 512],
                    KV32[:, ec * 512:(ec + 1) * 512], self.bank(bk)),
                    waits=[("pe", pv), ("pe", kvbf_pe)], inc="ve")
                vv = b.val("ve")
                self.prelease([bk], "ve", vv)
                b.op("vector", lambda e, ec=ec: e.tensor_copy(
                    KVBF[:, ec * 512:(ec + 1) * 512],
                    KV32[:, ec * 512:(ec + 1) * 512]), inc="ve")
            kv_v = b.val("ve")
        o_ready = ("ve", b.val("ve"))
        if seq == 0 and gp == 0:
            p.dump("cot", OT[:]); p.dump("ckv", KV32[:]); p.dump("catt", ATT[:])
        for od in range(ND):
            par = od % 2
            b.dma("gpsimd", WOT[:, par],
                  wout[:, od * 128:(od + 1) * 128].rearrange("(t q) c -> q t c", q=128),
                  waits=[("pe", self.c_wot_free.get(par, 0))], inc="qw")
            wd = b.val("qw")
            b.dma("sync", XOLD[:, par], p.hb_xo[od, :, t0:t0 + 512],
                  waits=[("ve", self.c_xold_free.get(par, 0)),
                         ("qx", spill_qx)], inc="qx")
            xod = b.val("qx")
            bk = 4 + par
            for et in range(16):
                w2 = []
                if et == 0:
                    w2 = self.pwait([bk]) + [("qw", wd), o_ready]
                b.op("tensor", lambda e, od=od, et=et, bk=bk, par=par: e.matmul(
                    self.bank(bk), WOT[:, par, et], OT[:, et, 0:512],
                    start=(et == 0), stop=(et == 15)), waits=w2,
                    inc="pe" if et == 15 else None)
            pv = b.val("pe")
            self.c_wot_free[par] = pv
            b.op("vector", lambda e, od=od, par=par, bk=bk: e.scalar_tensor_tensor(
                XST[:, par], self.bank(bk), cS["bout"][:, od:od + 1], XOLD[:, par],
                ALU.add, ALU.add),
                waits=[("pe", pv), ("qx", xod), ("qx", self.c_bg_ld),
                       ("qx", self.c_xst_qx.get(par, 0)),
                       ("se", self.c_xst_se.get(par, 0))], inc="ve")
            vv = b.val("ve")
            self.prelease([bk], "ve", vv)
            self.c_xold_free[par] = vv
            b.dma("sync", p.hb_xo[od, :, t0:t0 + 512], XST[:, par],
                  waits=[("ve", vv)], inc="qx")
            self.c_xst_qx[par] = b.val("qx")
            b.op("scalar", lambda e, od=od, t0=t0, par=par: e.copy(
                p.XBF[:, od, 2 + t0:2 + t0 + 512], XST[:, par]),
                waits=[("ve", vv)], inc="se")
            self.c_xst_se[par] = b.val("se")
        self.join()
        if seq == 0 and gp == 0:
            p.dump("cxbf0", p.XBF[:])
    self.c_w_free = b.val("pe")
    return ("qx", b.val("qx"))


Stages.stage_c = _stage_c


def _spill_x(self, seq, X):
    p, b = self.p, self.b
    for od in range(ND):
        b.dma("sync", p.hb_xo[od], X[:, od], inc="qx")
    return b.val("qx")


def _write_output_hbm(self, seq, pool, srcs):
    p, b = self.p, self.b
    XF = pool.alloc("XFO", [128, ND, T], F32)
    XI = pool.alloc("XIO", [128, T], F32)
    prev_add = 0
    entry_qx = b.val("qx")
    for od in range(ND):
        b.dma("sync", XF[:, od], srcs[0][od], waits=[("qx", entry_qx)], inc="qx")
        if len(srcs) > 1:
            b.dma("sync", XI[:], srcs[1][od],
                  waits=[("ve", prev_add), ("qx", entry_qx)], inc="qx")
            qv = b.val("qx")
            b.op("vector", lambda e, od=od: e.tensor_add(XF[:, od], XF[:, od], XI[:]),
                 waits=[("qx", qv)], inc="ve")
            prev_add = b.val("ve")
    self.join(queues=True)
    if seq == 0:
        p.dump("xfout", XF[:])
    return self.write_output(seq, pool, X=XF)


Stages.spill_x = _spill_x
Stages.write_output_hbm = _write_output_hbm


def _stage_d(self, seq, pool, spill_qx=0):
    p, b = self.p, self.b
    cS = self.colslots

    sp = pool.fork()
    AREP, BREP, st_v = self.stats(sp, 0, T, False, waits=[("qx", spill_qx)])
    X32 = sp.alloc("DX32", [128, 2, T], F32)
    XSTG = sp.alloc("DXSTG", [128, 2, T], F32)
    prev = {"qx": 0, "se": 0, "ve": 0}
    for od in range(ND):
        par = od % 2
        b.dma("sync", X32[:, par], p.hb_xo[od],
              waits=[("qx", spill_qx), ("ve", prev.get(("n", par), 0))], inc="qx")
        qv = b.val("qx")
        b.op("vector", lambda e, par=par: e.tensor_mul(XSTG[:, par], X32[:, par], AREP[:]),
             waits=[("qx", qv), ("ve", st_v), ("qx", prev.get(("d", par), 0)),
                    ("se", prev.get(("s", par), 0))])
        b.op("vector", lambda e, par=par: e.tensor_sub(XSTG[:, par], XSTG[:, par], BREP[:]),
             inc="ve")
        vv = b.val("ve")
        prev[("n", par)] = vv
        b.dma("sync", p.hb_xi[od], XSTG[:, par], waits=[("ve", vv)], inc="qx")
        prev[("d", par)] = b.val("qx")
        b.op("scalar", lambda e, od=od, par=par: e.copy(
            p.XBF[:, od, 2:2 + T], XSTG[:, par]), waits=[("ve", vv)], inc="se")
        prev[("s", par)] = b.val("se")
    xi_qx = b.val("qx")
    self.join()
    if seq == 0:
        p.dump("d0x", p.XBF[:])

    dpool = pool.fork()
    for L in range(6):
        xi_qx = self.d_attn(seq, L, dpool.fork(), xi_qx)
        self.join()
        if seq == 0:
            p.dump(f"dl{L}a", p.XBF[:])
        xi_qx = self.d_ffn(seq, L, dpool.fork(), xi_qx)
        self.join()
        if seq == 0:
            p.dump(f"dl{L}f", p.XBF[:])
    return xi_qx


def _d_loadcols(self, L, names):
    p, b = self.p, self.b
    cS = self.colslots
    if "qs" in names:
        qs = p.w("L1_ffn_q_scale", [6, 64])
        ks = p.w("L1_ffn_k_scale", [6, 64])
        for half in range(2):
            b.dma("sync", cS["qs8"][half * 64:half * 64 + 64, :],
                  qs[L].unsqueeze(1), inc="qx")
            b.dma("sync", cS["ks"][half * 64:half * 64 + 64, :],
                  ks[L].unsqueeze(1), inc="qx")
        qv = b.val("qx")
        b.op("vector", lambda e: e.tensor_scalar(
            cS["qs8"][:], cS["qs8"][:], 8.0, None, ALU.mult),
            waits=[("qx", qv)], inc="ve")
        return b.val("ve"), qv
    if "ln" in names:
        b.dma("sync", cS["lng"], p.w("L1_ffn_ln_g", [6, D])[L].rearrange(
            "(d q) -> q d", q=128), inc="qx", slow=True)
        b.dma("sync", cS["lnb"], p.w("L1_ffn_ln_b", [6, D])[L].rearrange(
            "(d q) -> q d", q=128), inc="qx", slow=True)
        b1 = p.w("L1_ffn_b1", [6, 2 * DI])
        b.dma("sync", cS["b1a"][:, 0:21], b1[L, 0:21 * 128].rearrange(
            "(c q) -> q c", q=128), inc="qx", slow=True)
        b.dma("sync", cS["b1a"][0:C_LAST, 21:22], b1[L, 21 * 128:DI].unsqueeze(1),
              inc="qx")
        b.dma("sync", cS["b1g"][:, 0:21], b1[L, DI:DI + 21 * 128].rearrange(
            "(c q) -> q c", q=128), inc="qx", slow=True)
        b.dma("sync", cS["b1g"][0:C_LAST, 21:22], b1[L, DI + 21 * 128:].unsqueeze(1),
              inc="qx")
        b.dma("sync", cS["b2"], p.w("L1_ffn_b2", [6, D])[L].rearrange(
            "(d q) -> q d", q=128), inc="qx", slow=True)
        return b.val("qx")


def _d_attn(self, seq, L, pool, xi_qx):
    p, b = self.p, self.b
    cS = self.colslots
    qs_v, qs_qx = self._d_loadcols(L, ["qs"])

    TBL = pool.alloc("DTBL", [128, 4, T], BF16)
    WQK = pool.alloc("WQK", [128, ND, 2048], BF16)
    WVS = pool.alloc("WVS", [128, ND, D], BF16)
    WOS = pool.alloc("WOS", [128, ND, D], BF16)
    QF = pool.alloc("QF", [128, 8, 512], BF16)
    KF = pool.alloc("KF", [128, 8, 512], BF16)
    VT = pool.alloc("VTD", [128, 4, D], BF16)
    KFr = pool.alloc("KFr", [128, 8, 256], BF16)
    VTr = pool.alloc("VTr", [128, 2, D], BF16)
    OTN = pool.alloc("OTND", [128, 8, 512], BF16)
    PT = pool.alloc("PTD", [128, 384], BF16)
    PT2 = pool.alloc("PTD2", [128, 384], BF16)
    QS = pool.alloc("QSD", [128, 512], F32)
    SQ = pool.alloc("SQD", [128, 512], BF16)
    QB = pool.alloc("QBD", [128, 512], BF16)
    T4 = pool.alloc("T4D", [128, 512], BF16)
    RN16 = pool.alloc("RN16", [16, 512], F32)
    R16S = pool.alloc("R16S", [16, 512], BF16)
    CSQ = pool.alloc("CSQ", [128, 4, 512], BF16)
    XSTD = pool.alloc("XSTD", [128, ND, 512], F32)
    XOLD2 = pool.alloc("XOLD2", [128, 2, 512], F32)
    want_pt = ("dptdbg" in os.environ.get("BK_DUMP", "").split(",")
               and seq == 0 and L == 0)
    PTDBG = pool.alloc("PTDBG", [128, 8, 384], BF16) if want_pt else None
    want_ot = ("dotsnap" in os.environ.get("BK_DUMP", "").split(",")
               and seq == 0 and L == 0)
    OTSNAP = (p.nc.dram_tensor("dump_dotsnap", [128, 8, 512], BF16,
                               kind="ExternalOutput").ap() if want_ot else None)
    pt_slots = {(0, 8, 0): 0, (0, 8, 1): 1, (0, 9, 0): 2, (0, 0, 0): 3,
                (1, 0, 0): 4, (1, 0, 1): 5, (0, 12, 0): 6, (1, 8, 0): 7}

    wqkv = p.w("L1_ffn_Wqkv", [6, D, 3 * INNER])
    dqk = b.dma("gpsimd", WQK[:],
                wqkv[L, :, 0:2048].rearrange("(d q) o -> q d o", q=128),
                waits=[("pe", self.d_w_free)], inc="qw")
    dvs = b.dma("gpsimd", WVS[:],
                wqkv[L, :, 2048:3072].rearrange("(d q) o -> q d o", q=128), inc="qw")
    wo = p.w("L1_ffn_Wo", [6, INNER, D])
    dwo = b.dma("gpsimd", WOS[:],
                wo[L].rearrange("(t q) o -> q t o", q=128), inc="qw")
    dtb = None
    for i, nm in enumerate(("tcs", "tss", "tcsi", "tssi")):
        dtb = b.dma("gpsimd", TBL[:, i], p.extra(nm, p.tab[nm[1:]]), inc="qw")

    xbf_flush = []
    for q in range(4):
        t0 = q * 512
        w0 = [("ve", qs_v), ("qw", dtb)] if q == 0 else []
        for i in range(4):
            b.op("vector", lambda e, i=i, t0=t0: e.tensor_copy(
                CSQ[:, i], TBL[:, i, t0:t0 + 512]),
                waits=w0 if i == 0 else (), inc="ve" if i == 3 else None)
        csq_v = b.val("ve")
        xbf_se = self.d_xbf_se
        vt_ve = 0
        for tt in range(4):
            bk = (tt % 2) * 2
            for oc2 in range(2):
                for d in range(ND):
                    w2 = []
                    if d == 0:
                        w2 = self.pwait([bk + oc2]) + [("qw", dvs), ("se", xbf_se)]
                    b.op("tensor", lambda e, tt=tt, oc2=oc2, d=d, bk=bk, t0=t0: e.matmul(
                        self.bank(bk + oc2),
                        p.xbf_shift(d)[:, t0 + tt * 128: t0 + (tt + 1) * 128],
                        WVS[:, d, oc2 * 512:(oc2 + 1) * 512],
                        start=(d == 0), stop=(d == ND - 1)), waits=w2,
                        inc="pe" if (oc2 == 1 and d == ND - 1) else None)
            pv = b.val("pe")
            b.op("vector", lambda e, tt=tt, bk=bk: e.tensor_copy(
                VT[:, tt], self.PS[:, bk * 512: bk * 512 + 1024]),
                waits=[("pe", pv), ("pe", self.d_av_pe)], inc="ve")
            vt_ve = b.val("ve")
            self.prelease([bk, bk + 1], "ve", vt_ve)
        vt_ready = ("ve", vt_ve)
        fd_ve = 0
        for (isq, FDST, tbl0) in ((True, QF, 0), (False, KF, 2)):
            off_w = 0 if isq else 1024
            psn_pe = {}
            last_fd = 0
            for oc in range(8):
                bk = oc % 2
                for d in range(ND):
                    w2 = []
                    if d == 0:
                        w2 = self.pwait([bk]) + [("qw", dqk), ("se", xbf_se),
                                                 ("qx", qs_qx)]
                    b.op("tensor", lambda e, oc=oc, d=d, bk=bk, t0=t0, off_w=off_w: e.matmul(
                        self.bank(bk),
                        WQK[:, d, off_w + oc * 128: off_w + (oc + 1) * 128],
                        p.xbf_shift(d)[:, t0:t0 + 512],
                        start=(d == 0), stop=(d == ND - 1)), waits=w2,
                        inc="pe" if d == ND - 1 else None)
                pv = b.val("pe")
                b.op("scalar", lambda e, bk=bk: e.copy(QS[:], self.bank(bk)),
                     waits=[("pe", pv), ("ve", last_fd)], inc="se")
                sv = b.val("se")
                self.prelease([bk], "se", sv)
                b.op("vector", lambda e: e.tensor_mul(SQ[:], QS[:], QS[:]),
                     waits=[("se", sv), ("pe", psn_pe.get(oc - 1, 0))], inc="ve")
                sqv = b.val("ve")
                b.op("tensor", lambda e, oc=oc: e.matmul(
                    self.PS[0:16, 7 * 512:7 * 512 + 512],
                    p.BOSQ[:, oc * 16:(oc + 1) * 16], SQ[:],
                    start=(oc == 0), stop=(oc == 7), skip_group_check=True),
                    waits=([("ve", sqv)] + (self.pwait([7]) if oc == 0 else [])),
                    inc="pe")
                psn_pe[oc] = b.val("pe")
                b.op("vector", lambda e, oc=oc, FDST=FDST: e.tensor_copy(
                    FDST[:, oc], QS[:]),
                    waits=[("pe", self.d_fd_pe)], inc="ve")
                last_fd = b.val("ve")
            b.op("scalar", lambda e: e.activation(
                self.PS[0:16, 7 * 512:7 * 512 + 512],
                self.PS[0:16, 7 * 512:7 * 512 + 512], AF.Sqrt),
                waits=[("pe", psn_pe[7])], inc="se")
            sv = b.val("se")
            b.op("vector", lambda e: e.reciprocal(
                RN16[:], self.PS[0:16, 7 * 512:7 * 512 + 512]),
                waits=[("se", sv)], inc="ve")
            b.op("vector", lambda e: e.tensor_copy(R16S[:], RN16[:]), inc="ve")
            rn_v = b.val("ve")
            self.prelease([7], "ve", rn_v)
            rot_pe = 0
            for oc in range(8):
                b.op("tensor", lambda e, oc=oc: e.matmul(
                    self.bank(6), p.BOT16[:, oc * 128:(oc + 1) * 128], R16S[:],
                    start=True, stop=True),
                    waits=[("ve", rn_v)] + self.pwait([6]), inc="pe")
                pv = b.val("pe")
                scol = cS["qs8"] if isq else cS["ks"]
                b.op("vector", lambda e, oc=oc, FDST=FDST, scol=scol: e.scalar_tensor_tensor(
                    QB[:], FDST[:, oc], scol[:], self.bank(6),
                    ALU.mult, ALU.mult),
                    waits=[("pe", pv), ("pe", rot_pe), ("ve", qs_v)], inc="ve")
                qb_v = b.val("ve")
                self.prelease([6], "ve", qb_v)
                b.op("tensor", lambda e: e.matmul(
                    self.bank(6), p.ROT[:], QB[:], start=True, stop=True),
                    waits=[("ve", qb_v)] + self.pwait([6]), inc="pe")
                rot_pe = b.val("pe")
                b.op("vector", lambda e, tbl0=tbl0: e.tensor_mul(
                    T4[:], QB[:], CSQ[:, tbl0]), waits=[("ve", csq_v)])
                b.op("vector", lambda e, tbl0=tbl0: e.tensor_mul(
                    QB[:], self.bank(6), CSQ[:, tbl0 + 1]),
                    waits=[("pe", rot_pe)], inc="ve")
                self.prelease([6], "ve", b.val("ve"))
                b.op("vector", lambda e, oc=oc, FDST=FDST: e.tensor_add(
                    FDST[:, oc], T4[:], QB[:]), inc="ve")
                fd_ve = b.val("ve")
            self.d_fd_pe = 0
        qk_ready = ("ve", fd_ve)
        if seq == 0 and L == 0 and q == 0:
            p.dump("dqf", QF[:]); p.dump("dkf", KF[:]); p.dump("dvt", VT[:])
            p.dump("dcsq", CSQ[:])
        s_bank_waited = False
        s_started = set()
        snap_qx = 0
        slots = []
        for w in range(2):
            wg = 2 * q + w
            for ic in range(2):
                jts = []
                for jt in range(3):
                    jg = 2 * wg + ic - 2 + jt
                    if jg >= 0:
                        jts.append((jt, jg - 4 * q))
                for h in range(HEADS):
                    slots.append((w, h, ic, jts))
        sim_pe = {}
        pt_guard = {0: self.d_pt_pe, 1: self.d_pt_pe}

        def emit_sim(n):
            (w, h, ic, jts) = slots[n]
            hb = (h % 2) * 64
            ht = h // 2
            iq = w * 256 + ic * 128
            par = n % 2
            for (ji, (jt, jq)) in enumerate(jts):
                if jq < 0:
                    ksrc = KFr[hb:hb + 64, ht, (jq + 2) * 128:(jq + 3) * 128]
                else:
                    ksrc = KF[hb:hb + 64, ht, jq * 128:(jq + 1) * 128]
                w2 = []
                if ji == 0:
                    w2 = [qk_ready] + self.pwait([par])
                b.op("tensor", lambda e, ksrc=ksrc, jt=jt, hb=hb, ht=ht, iq=iq,
                     par=par: e.matmul(
                    self.bank(par, n=128, off=jt * 128), ksrc,
                    QF[hb:hb + 64, ht, iq:iq + 128],
                    start=True, stop=True, skip_group_check=True), waits=w2,
                    inc="pe" if ji == len(jts) - 1 else None)
            sim_pe[n] = b.val("pe")

        emit_sim(0)
        for n, (w, h, ic, jts) in enumerate(slots):
            hb = (h % 2) * 64
            ht = h // 2
            iq = w * 256 + ic * 128
            i0w = w * 256
            par = n % 2
            PTb = PT if par == 0 else PT2
            nj = len(jts)
            joff = (3 - nj) * 128
            b.op("scalar", lambda e, joff=joff, nj=nj, par=par, PTb=PTb: e.activation(
                PTb[:, joff:joff + nj * 128],
                self.bank(par, n=nj * 128, off=joff), AF.Exp),
                waits=[("pe", sim_pe[n]), ("pe", pt_guard[par])], inc="se")
            sv = b.val("se")
            self.prelease([par], "se", sv)
            b.op("vector", lambda e, joff=joff, nj=nj, PTb=PTb: e.tensor_mul(
                PTb[:, joff:joff + nj * 128], PTb[:, joff:joff + nj * 128],
                p.DMASKB[:, joff:joff + nj * 128]),
                waits=[("se", sv)], inc="ve")
            pt_v = b.val("ve")
            if want_pt and q == 0 and (w, h, ic) in pt_slots:
                sl = pt_slots[(w, h, ic)]
                b.op("scalar", lambda e, sl=sl, PTb=PTb: e.copy(
                    PTDBG[:, sl], PTb[:]), waits=[("ve", pt_v)], inc="se")
            if n + 1 < len(slots):
                emit_sim(n + 1)
            for (ji, (jt, jq)) in enumerate(jts):
                sw = [("ve", pt_v)] if ji == 0 else []
                st_ = iq not in s_started
                s_started.add(iq)
                if not s_bank_waited:
                    sw += self.pwait([6])
                    s_bank_waited = True
                b.op("tensor", lambda e, h=h, jt=jt, iq=iq, st_=st_,
                     lst=(h == HEADS - 1 and ji == len(jts) - 1),
                     PTb=PTb: e.matmul(
                    self.PS[0:16, 6 * 512 + iq:6 * 512 + iq + 128],
                    p.OC16[:, h * 16:(h + 1) * 16],
                    PTb[:, jt * 128:(jt + 1) * 128],
                    start=st_, stop=lst,
                    skip_group_check=True), waits=sw)
            for (ji, (jt, jq)) in enumerate(jts):
                if jq < 0:
                    vsrc = VTr[:, jq + 2, h * 64:(h + 1) * 64]
                else:
                    vsrc = VT[:, jq, h * 64:(h + 1) * 64]
                col = (2 + ht // 2) * 512 + (ht % 2) * 256 + ic * 128
                w2 = []
                if h % 4 == 0 and h % 2 == 0 and ic == 0 and ji == 0:
                    w2 = self.pwait([2 + ht // 2]) + [vt_ready]
                b.op("tensor", lambda e, vsrc=vsrc, jt=jt, hb=hb, col=col,
                     fst=(ji == 0), lst=(ji == len(jts) - 1), PTb=PTb: e.matmul(
                    self.PS[hb:hb + 64, col:col + 128],
                    vsrc, PTb[:, jt * 128:(jt + 1) * 128],
                    start=fst, stop=lst,
                    skip_group_check=True), waits=w2,
                    inc="pe" if ji == len(jts) - 1 else None)
            pt_guard[par] = b.val("pe")
            if h == HEADS - 1 and ic == 1:
                av_pe = b.val("pe")
                b.op("vector", lambda e, i0w=i0w: e.tensor_copy(
                    OTN[:, :, i0w:i0w + 256],
                    self.PS[:, 2 * 512:2 * 512 + 2048].rearrange(
                        "z (o t) -> z o t", o=8)),
                    waits=[("pe", av_pe)], inc="ve")
                if want_ot and q == 0:
                    b.dma("sync", OTSNAP[:, :, i0w:i0w + 256],
                          OTN[:, :, i0w:i0w + 256], waits=[("ve", b.val("ve"))],
                          inc="qx")
                    snap_qx = b.val("qx")
                self.prelease([2, 3, 4, 5], "ve", b.val("ve"))
        self.d_pt_pe = b.val("pe")
        self.d_av_pe = b.val("pe")
        if seq == 0 and L == 0 and q == 0:
            p.dump("dotnraw", OTN[:])
            if want_pt:
                p.dump("dptdbg", PTDBG[:])
        b.op("vector", lambda e: e.reciprocal(
            RN16[:], self.PS[0:16, 6 * 512:6 * 512 + 512]),
            waits=[("pe", self.d_av_pe)] +
                  ([("qx", snap_qx)] if snap_qx else []), inc="ve")
        b.op("vector", lambda e: e.tensor_copy(R16S[:], RN16[:]), inc="ve")
        r16_v = b.val("ve")
        self.prelease([6], "ve", r16_v)
        otn_ve = 0
        for oc in range(8):
            b.op("tensor", lambda e, oc=oc: e.matmul(
                self.bank(6), p.BOT16[:, oc * 128:(oc + 1) * 128], R16S[:],
                start=True, stop=True), waits=[("ve", r16_v)] + self.pwait([6]),
                inc="pe")
            pv = b.val("pe")
            b.op("vector", lambda e, oc=oc: e.tensor_mul(
                OTN[:, oc], OTN[:, oc], self.bank(6)),
                waits=[("pe", pv)], inc="ve")
            otn_ve = b.val("ve")
            self.prelease([6], "ve", otn_ve)
        otn_ready = ("ve", otn_ve)
        if seq == 0 and L == 0 and q == 0:
            p.dump("dotn", OTN[:]); p.dump("dr16", R16S[:]); p.dump("dpt", PT[:])
        b.op("vector", lambda e: e.tensor_copy(KFr[:], KF[:, :, 256:512]),
             waits=[("pe", self.d_av_pe)])
        b.op("vector", lambda e: e.tensor_copy(VTr[:, 0], VT[:, 2]))
        b.op("vector", lambda e: e.tensor_copy(VTr[:, 1], VT[:, 3]), inc="ve")
        for fn in xbf_flush:
            fn()
        xbf_flush = []
        self.d_xbf_se = b.val("se")
        for od in range(ND):
            par = od % 2
            b.dma("sync", XOLD2[:, par], p.hb_xi[od, :, t0:t0 + 512],
                  waits=[("ve", self.d_xold_free.get(par, 0)), ("qx", xi_qx)],
                  inc="qx")
            xod = b.val("qx")
            bk = par
            for et in range(ND):
                w2 = []
                if et == 0:
                    w2 = self.pwait([bk]) + [("qw", dwo), otn_ready]
                b.op("tensor", lambda e, od=od, et=et, bk=bk: e.matmul(
                    self.bank(bk), WOS[:, et, od * 128:(od + 1) * 128],
                    OTN[:, et], start=(et == 0), stop=(et == ND - 1)), waits=w2,
                    inc="pe" if et == ND - 1 else None)
            pv = b.val("pe")
            b.op("vector", lambda e, od=od, par=par, bk=bk: e.tensor_add(
                XSTD[:, od], self.bank(bk), XOLD2[:, par]),
                waits=[("pe", pv), ("qx", xod),
                       ("se", self.d_xstd_se.get(od, 0)),
                       ("qx", self.d_xstd_qx.get(od, 0))], inc="ve")
            vv = b.val("ve")
            self.prelease([bk], "ve", vv)
            self.d_xold_free[par] = vv
            b.dma("sync", p.hb_xi[od, :, t0:t0 + 512], XSTD[:, od],
                  waits=[("ve", vv)], inc="qx")
            self.d_xstd_qx[od] = b.val("qx")

            def mk_flush(od=od, t0=t0, vv=vv):
                def fl():
                    bb = self.b
                    bb.op("scalar", lambda e: e.copy(
                        p.XBF[:, od, 2 + t0:2 + t0 + 512], XSTD[:, od]),
                        waits=[("ve", vv), ("pe", bb.val("pe"))], inc="se")
                    self.d_xstd_se[od] = bb.val("se")
                return fl
            xbf_flush.append(mk_flush())
        if seq == 0 and L == 0 and q == 0:
            p.dump("dxstd0", XSTD[:])
        xi_qx = b.val("qx")
        self.join()
    for fn in xbf_flush:
        fn()
    self.d_xbf_se = b.val("se")
    self.d_w_free = b.val("pe")
    self.join()
    return xi_qx


Stages.d_attn = _d_attn



def _d_ffn(self, seq, L, pool, xi_qx):
    p, b = self.p, self.b
    cS = self.colslots
    ln_qx = self._d_loadcols(L, ["ln"])

    XN = pool.alloc("DXN", [128, ND, T], BF16)
    h2_off = pool.cur
    H2 = pool.alloc("DH2", [128, NC_CHUNKS, T], BF16)
    G = pool.alloc("DG", [128, 1024], F32)
    W1R = pool.alloc("W1R", [128, 2, 2, ND, 128], BF16)
    W2R = pool.alloc("W2R", [128, 2, NC_CHUNKS, 128], BF16)
    XOLD3 = pool.alloc("XOLD3", [128, 2, 1024], F32)
    XST3 = pool.alloc("XST3", [128, 2, 1024], F32)

    w1 = p.w("L1_ffn_W1", [6, D, 2 * DI])
    w2 = p.w("L1_ffn_W2", [6, DI, D])

    sp = PoolCursor(p.mem, h2_off)
    AREP, BREP, st_v = self.stats(sp, 0, T, True,
                                  waits=[("se", self.d_xbf_se)])
    TMPN = sp.alloc("DTMPN", [128, T], F32)
    g_cols = [cS["lng"][:, d:d + 1] for d in range(ND)]
    b_cols = [cS["lnb"][:, d:d + 1] for d in range(ND)]
    xn_vals = self.normalize(lambda d: XN[:, d], AREP, BREP, TMPN, 0, T,
                             src_view=lambda d: p.xbf_shift(d),
                             waits=[("ve", st_v), ("qx", ln_qx)],
                             gb=(g_cols, b_cols))
    xn_v = ("ve", xn_vals[-1])
    self.join()
    w1_dma = {}
    for c in range(NC_CHUNKS):
        cw = 128 if c < NC_CHUNKS - 1 else C_LAST
        par = c % 2
        w1_dma[c] = b.dma(
            "gpsimd", W1R[:, par, :, :, 0:cw],
            w1[L].rearrange("(d q) (h w) -> q h d w", q=128, h=2)[
                :, :, :, c * 128:c * 128 + cw],
            waits=[("pe", self.f_w1_free.get(par, 0))], inc="qw")
        for th in range(2):
            for ag in range(2):
                for tc in range(2):
                    bk = par * 4 + ag * 2 + tc
                    tcs = th * 1024 + tc * 512
                    for d in range(ND):
                        w2_ = []
                        if d == 0:
                            w2_ = self.pwait([bk]) + [("qw", w1_dma[c]), xn_v]
                        b.op("tensor", lambda e, c=c, cw=cw, par=par, ag=ag, d=d, bk=bk, tcs=tcs: e.matmul(
                            self.bank(bk, n=512)[0:cw, :] if False else
                            self.PS[0:cw, bk * 512:bk * 512 + 512],
                            W1R[:, par, ag, d, 0:cw], XN[:, d, tcs:tcs + 512],
                            start=(d == 0), stop=(d == ND - 1)), waits=w2_,
                            inc="pe" if d == ND - 1 else None)
            pv = b.val("pe")
            self.f_w1_free[par] = pv
            pa = par * 4
            b.op("scalar", lambda e, c=c, cw=cw, pa=pa: e.activation(
                G[0:cw, :], self.PS[0:cw, (pa + 2) * 512:(pa + 2) * 512 + 1024],
                AF.Gelu, bias=cS["b1g"][0:cw, c:c + 1]),
                waits=[("pe", pv), ("qx", ln_qx), ("ve", self.f_g_ve)], inc="se")
            sv = b.val("se")
            self.prelease([pa + 2, pa + 3], "se", sv)
            b.op("vector", lambda e, c=c, cw=cw, th=th, pa=pa: e.scalar_tensor_tensor(
                H2[0:cw, c, th * 1024:(th + 1) * 1024],
                self.PS[0:cw, pa * 512:pa * 512 + 1024],
                cS["b1a"][0:cw, c:c + 1], G[0:cw, :], ALU.add, ALU.mult),
                waits=[("se", sv)], inc="ve")
            self.f_g_ve = b.val("ve")
            self.prelease([pa, pa + 1], "ve", self.f_g_ve)
    h2_v = ("ve", self.f_g_ve)
    for od in range(ND):
        par = od % 2
        d1 = b.dma("gpsimd", W2R[:, par, 0:21, :],
                   w2[L][0:21 * 128].rearrange("(c q) o -> q c o", q=128)[
                       :, :, od * 128:(od + 1) * 128],
                   waits=[("pe", self.f_w2_free.get(par, 0))], inc="qw")
        d2 = b.dma("gpsimd", W2R[0:C_LAST, par, 21, :],
                   w2[L, 21 * 128:DI, od * 128:(od + 1) * 128], inc="qw")
        w2_qw = b.val("qw")
        for half in range(2):
            b.dma("sync", XOLD3[:, par], p.hb_xi[od, :, half * 1024:(half + 1) * 1024],
                  waits=[("ve", self.f_xold_free.get(par, 0)), ("qx", xi_qx)],
                  inc="qx")
            xod = b.val("qx")
            for tc in range(2):
                bk = 6 + (tc % 2)
                tcs = half * 1024 + tc * 512
                for c in range(NC_CHUNKS):
                    cw = 128 if c < NC_CHUNKS - 1 else C_LAST
                    w2_ = []
                    if c == 0:
                        w2_ = self.pwait([bk]) + [("qw", w2_qw), h2_v]
                    b.op("tensor", lambda e, od=od, par=par, c=c, cw=cw, bk=bk, tcs=tcs: e.matmul(
                        self.bank(bk), W2R[0:cw, par, c, :],
                        H2[0:cw, c, tcs:tcs + 512],
                        start=(c == 0), stop=(c == NC_CHUNKS - 1)), waits=w2_,
                        inc="pe" if c == NC_CHUNKS - 1 else None)
                pv = b.val("pe")
                if tc == 1:
                    self.f_w2_free[par] = pv
                b.op("vector", lambda e, od=od, par=par, bk=bk, tc=tc: e.scalar_tensor_tensor(
                    XST3[:, par, tc * 512:(tc + 1) * 512], self.bank(bk),
                    cS["b2"][:, od:od + 1], XOLD3[:, par, tc * 512:(tc + 1) * 512],
                    ALU.add, ALU.add),
                    waits=[("pe", pv), ("qx", xod), ("qx", ln_qx),
                           ("qx", self.f_xst_qx.get(par, 0)),
                           ("se", self.f_xst_se.get(par, 0))], inc="ve")
                vv = b.val("ve")
                self.prelease([bk], "ve", vv)
            self.f_xold_free[par] = vv
            b.dma("sync", p.hb_xi[od, :, half * 1024:(half + 1) * 1024],
                  XST3[:, par], waits=[("ve", vv)], inc="qx")
            self.f_xst_qx[par] = b.val("qx")
            b.op("scalar", lambda e, od=od, half=half, par=par: e.copy(
                p.XBF[:, od, 2 + half * 1024:2 + (half + 1) * 1024],
                XST3[:, par]), waits=[("ve", vv)], inc="se")
            self.f_xst_se[par] = b.val("se")
    self.d_xbf_se = b.val("se")
    self.d_w_free = b.val("pe")
    self.join()
    return b.val("qx")


Stages.d_ffn = _d_ffn

Stages.stage_d = _stage_d
Stages._d_loadcols = _d_loadcols


# version 04 (final)
# speedup vs baseline: 1.0159x; 1.0159x over previous
import os
import numpy as np
import ml_dtypes
import concourse.bass as bass
import concourse.mybir as mybir
from concourse.bass_utils import run_bass_kernel_spmd

F32 = mybir.dt.float32
BF16 = mybir.dt.bfloat16
I32 = mybir.dt.int32
AF = mybir.ActivationFunctionType
ALU = mybir.AluOpType
AX = mybir.AxisListType
BFNP = ml_dtypes.bfloat16

NCORES = 8
SEQS = 4
T = 2048
NT = 16
D = 1024
ND = 8
TP = T + 2
NEG = -1.0e38

HEADS = 16
DH = 64
WIN = 256
NW = T // WIN
INNER = 1024
DI = 2730
NC_CHUNKS = 22
C_LAST = DI - 21 * 128

DBG_STAGE = os.environ.get("BK_STAGE", "")


def host_tables():
    half = np.arange(0, DH, 2, dtype=np.float64)
    inv_freq = 1.0 / 10000.0 ** (half / DH)
    t = np.arange(T, dtype=np.float64)
    freqs = t[:, None] * inv_freq[None, :]
    freqs = np.concatenate([freqs, freqs], -1)
    base = (half + 0.4 * DH) / (1.4 * DH)
    power = (t - T // 2) / (WIN // 2)
    scale = base[None, :] ** power[:, None]
    scale = np.concatenate([scale, scale], -1)
    cos = np.cos(freqs)
    sin = np.sin(freqs)
    cs = (cos * scale).T
    ss = (sin * scale).T
    csi = (cos / scale).T
    ssi = (sin / scale).T
    out = {}
    for nm, v in (("cs", cs), ("ss", ss), ("csi", csi), ("ssi", ssi)):
        out[nm] = np.concatenate([v, v], 0).astype(np.float32)
    return out


def host_consts():
    c = {}
    c["ident"] = np.eye(128, dtype=BFNP)
    oh = np.zeros((128, 128), dtype=BFNP); oh[0, :] = 1
    c["onehot"] = oh
    c["onehot32"] = oh.astype(np.float32)
    rot = np.zeros((128, 128), dtype=np.float32)
    for m in range(128):
        blk = (m // 64) * 64
        j = m % 64
        if j < 32:
            rot[blk + j + 32, m] = -1.0
        else:
            rot[blk + j - 32, m] = 1.0
    c["rot"] = rot.astype(BFNP)
    bo = np.zeros((128, 2), dtype=BFNP)
    bo[0:64, 0] = 1; bo[64:128, 1] = 1
    c["bo"] = bo
    bot = np.zeros((16, 8 * 128), dtype=BFNP)
    for i in range(8):
        bot[2 * i, i * 128:i * 128 + 64] = 1
        bot[2 * i + 1, i * 128 + 64:i * 128 + 128] = 1
    c["bot16"] = bot
    oc16 = np.zeros((128, 16 * 16), dtype=BFNP)
    for h in range(16):
        oc16[:, h * 16 + h] = 1
    c["onescol16"] = oc16
    m0 = np.where(np.arange(128)[:, None] >= np.arange(128)[None, :], 0.0, NEG)
    m2 = np.where(np.arange(128)[:, None] <= np.arange(128)[None, :], 0.0, NEG)
    c["mask0"] = m0.astype(np.float32)
    c["mask2"] = m2.astype(np.float32)
    for mc in range(2):
        mm = (np.arange(128)[:, None] + mc * 128 <= np.arange(256)[None, :])
        c[f"cmask{mc}"] = mm.astype(BFNP)
    return c


class Builder:

    ENGS = ("tensor", "vector", "scalar", "gpsimd", "sync")

    def __init__(self, nc):
        self.nc = nc
        self.ops = []
        self.sem = {}
        self.cnt = {}
        self._es = []
        for name in ("pe", "ve", "se", "ge", "qw", "qx", "qs"):
            h = nc.semaphore(name)
            self.sem[name] = h.__enter__()
            self._es.append(h)
            self.cnt[name] = 0

    def op(self, eng, fn, waits=(), inc=None, amt=1):
        w = [(s, v) for (s, v) in waits if v > 0]
        incs = []
        r = None
        if inc is not None:
            self.cnt[inc] += amt
            r = self.cnt[inc]
            incs.append((inc, amt))
        self.ops.append((eng, fn, w, incs))
        return r

    def dma(self, queue_eng, out, in_, waits=(), inc=None, slow=False):
        fn = (lambda e, out=out, in_=in_, slow=slow: e.dma_start(
            out=out, in_=in_, allow_slow_non_contiguous=slow))
        return self.op(queue_eng, fn, waits=waits, inc=inc, amt=16)

    def val(self, sem):
        return self.cnt[sem]

    def emit(self):
        nc = self.nc
        with nc.Block() as block:
            for eng in self.ENGS:
                ops = [o for o in self.ops if o[0] == eng]
                if not ops:
                    continue
                dec = getattr(block, eng)

                def make(ops):
                    def body(e):
                        for (_, fn, waits, incs) in ops:
                            for (s, v) in waits:
                                e.wait_ge(self.sem[s], v)
                            ins = fn(e)
                            for (s, a) in incs:
                                ins.then_inc(self.sem[s], a)
                    return body
                dec(make(ops))

    def close(self):
        for h in self._es:
            h.__exit__(None, None, None)


class Mem:

    BASE = 20480
    CAP = 229340

    def __init__(self, nc):
        self.nc = nc
        self.cur = Mem.BASE
        self.pool_base = None
        self._uid = 0

    def _alloc(self, name, off, shape, dtype):
        self._uid += 1
        h = self.nc.alloc_sbuf_tensor_at(f"{name}_{self._uid}", list(shape), dtype, offset=off)
        return h.ap()

    def persist(self, name, shape, dtype, align=64):
        itm = 2 if dtype in (BF16, mybir.dt.float16) else 4
        n = int(np.prod(shape[1:])) * itm
        off = (self.cur + align - 1) // align * align
        ap = self._alloc(name, off, shape, dtype)
        self.cur = off + n
        assert self.cur <= Mem.CAP, f"persistent overflow at {name}: {self.cur}"
        return ap

    def freeze(self):
        self.pool_base = (self.cur + 63) // 64 * 64

    def pool(self):
        return PoolCursor(self)


class PoolCursor:
    def __init__(self, mem, cur=None):
        self.mem = mem
        self.cur = mem.pool_base if cur is None else cur

    def fork(self):
        return PoolCursor(self.mem, self.cur)

    def alloc(self, name, shape, dtype, align=64):
        itm = 2 if dtype in (BF16, mybir.dt.float16) else 4
        n = int(np.prod(shape[1:])) * itm
        off = (self.cur + align - 1) // align * align
        ap = self.mem._alloc(name, off, shape, dtype)
        self.cur = off + n
        assert self.cur <= Mem.CAP, f"pool overflow at {name}: {self.cur} > {Mem.CAP}"
        return ap


class Prog:
    def __init__(self):
        nc = bass.Bass("TRN2", target_bir_lowering=False,
                       detect_race_conditions=False)
        self.nc = nc
        self.b = Builder(nc)
        self.mem = Mem(nc)
        self.tab = host_tables()
        self.con = host_consts()
        self.inputs = {}

        self.x_in = nc.dram_tensor("x", [SEQS, T, D], F32, kind="ExternalInput").ap()
        self.x_out = nc.dram_tensor("y", [SEQS, T, D], F32, kind="ExternalOutput").ap()
        self.dram = {}
        self.hb_xo = nc.dram_tensor("hxo", [ND, 128, T], F32, kind="Internal").ap()
        self.hb_xi = nc.dram_tensor("hxi", [ND, 128, T], F32, kind="Internal").ap()
        self.hb_qrhs = nc.dram_tensor("hqrhs", [128, ND * D], BF16, kind="Internal").ap()

        self._win = {}

    def w(self, name, shape):
        if name not in self._win:
            self._win[name] = self.nc.dram_tensor(name, list(shape), F32,
                                                  kind="ExternalInput").ap()
        return self._win[name]

    def extra(self, name, arr):
        if name not in self._win:
            self._win[name] = self.nc.dram_tensor(
                name, list(arr.shape),
                BF16 if arr.dtype == BFNP else F32, kind="ExternalInput").ap()
            self.inputs[name] = arr
        return self._win[name]

    def dump(self, name, ap):
        want = os.environ.get("BK_DUMP", "")
        if name not in want.split(","):
            return
        shape = list(ap.shape)
        dt = ap.dtype
        d = self.nc.dram_tensor(f"dump_{name}", shape, dt, kind="ExternalOutput").ap()
        self.b.dma("sync", d, ap, waits=[(s, self.b.cnt[s]) for s in
                                          ("pe", "ve", "se", "ge", "qw", "qx")
                                          if self.b.cnt[s] > 0], inc="qx")

    def setup_persistent(self):
        m = self.mem
        self.XBF = m.persist("XBF", [128, ND, TP], BF16)
        self.COLS = m.persist("COLS", [128, 512], F32)
        self.RP32 = m.persist("RP32", [128, 1024], F32)
        self.ONEC = m.persist("ONEC", [128, 16], BF16)
        c = self.con
        self.IDENT = m.persist("IDENT", [128, 128], BF16)
        self.IDENT32 = m.persist("IDENT32", [128, 128], F32)
        self.OH = m.persist("OH", [128, 128], BF16)
        self.OH32 = m.persist("OH32", [128, 128], F32)
        self.ROT = m.persist("ROT", [128, 128], BF16)
        self.BOT16 = m.persist("BOT16", [16, 8 * 128], BF16)
        self.OC16 = m.persist("OC16", [128, 16 * 16], BF16)
        self.CMASK = m.persist("CMASK", [128, 2, 256], BF16)
        self.BOSQ = m.persist("BOSQ", [128, 128], BF16)
        self.DMASKB = m.persist("DMASKB", [128, 384], BF16)
        m.freeze()

        b = self.b
        nc = self.nc
        loads = [
            (self.IDENT, self.con["ident"]), (self.OH, self.con["onehot"]),
            (self.ROT, self.con["rot"]),
            (self.BOT16, self.con["bot16"]), (self.OC16, self.con["onescol16"]),
        ]
        loads32 = [
            (self.IDENT32, np.eye(128, dtype=np.float32)),
            (self.OH32, self.con["onehot32"]),
        ]
        self._ci = 0
        def _inline(arr):
            self._ci += 1
            return nc.inline_tensor(np.ascontiguousarray(arr), f"icst{self._ci}").ap()
        for ap, arr in loads + loads32:
            d = _inline(arr)
            b.dma("sync", ap, d, inc="qx")
        cm = np.stack([self.con["cmask0"], self.con["cmask1"]], 0)
        d = _inline(cm.transpose(1, 0, 2))
        b.dma("sync", self.CMASK, d, inc="qx")
        bosq = np.zeros((128, 128), dtype=BFNP)
        for oc in range(8):
            for pp in range(128):
                bosq[pp, oc * 16 + 2 * oc + (pp // 64)] = 1
        d = _inline(bosq)
        b.dma("sync", self.BOSQ, d, inc="qx")
        qq = np.arange(128)[:, None]; pp_ = np.arange(128)[None, :]
        dm = np.concatenate([(qq >= pp_), np.ones((128, 128), bool),
                             (qq <= pp_)], 1).astype(BFNP)
        d = _inline(dm)
        b.dma("sync", self.DMASKB, d, inc="qx")
        self.t_consts = b.val("qx")
        b.op("vector", lambda e: e.memset(self.RP32[:], 0.0))
        b.op("vector", lambda e: e.memset(self.ONEC[:], 1.0), inc="ve")
        self.t_zeros = b.val("ve")

    def xbf_data(self, d, t0=0, tn=T):
        return self.XBF[:, d, 2 + t0:2 + t0 + tn]

    def xbf_shift(self, d, t0=0, tn=T):
        if d < 4:
            return self.XBF[:, d, 2 + t0:2 + t0 + tn]
        return self.XBF[:, d, 1 + t0:1 + t0 + tn]

    def replicate32(self, pool_dst, row_ap, n, ps_bank_ap, pe_waits, ve_waits):
        b = self.b
        nchunk = (n + 511) // 512
        assert nchunk <= 1
        b.op("vector", lambda e: e.tensor_copy(self.RP32[0:1, 0:n], row_ap),
             waits=ve_waits, inc="ve")
        vcp = b.val("ve")
        b.op("tensor", lambda e: e.matmul(ps_bank_ap[:, 0:n], self.OH32[:],
                                          self.RP32[:, 0:n], start=True, stop=True),
             waits=[("ve", vcp)] + pe_waits, inc="pe")
        pv = b.val("pe")
        b.op("vector", lambda e: e.tensor_copy(pool_dst[:, 0:n], ps_bank_ap[:, 0:n]),
             waits=[("pe", pv)], inc="ve")
        return b.val("ve")


def _np_f32(x):
    return np.ascontiguousarray(x, dtype=np.float32)


class Stages:
    def __init__(self, p: Prog):
        self.p = p
        self.b = p.b
        self.nc = p.nc
        self.PSH = self.nc.alloc_psum_tensor("PS", [128, 4096], F32)
        self.PS = self.PSH.ap()
        self.ps_gate = [("pe", 0)] * 8
        self.wb_free = {}
        self.wo_free = {}
        self.c_w_free = 0
        self.c_wot_free = {}
        self.c_xold_free = {}
        self.c_xst_qx = {}
        self.c_xst_se = {}
        self.c_tt_free = 0
        self.a_t1_se = 0
        self.d_w_free = 0
        self.d_av_pe = 0
        self.d_fd_pe = 0
        self.d_pt_pe = 0
        self.d_xold_free = {}
        self.d_xstd_se = {}
        self.d_xstd_qx = {}
        self.d_xbf_se = 0
        self.f_w1_free = {}
        self.f_w2_free = {}
        self.f_g_ve = 0
        self.f_xold_free = {}
        self.f_xst_qx = {}
        self.f_xst_se = {}
        self.c_gau_v = 0
        self.c_bg_ld = 0
        self.acc_read = {}
        self.rp32_user = 0
        self.bocol_ld = 0
        self._colcur = 0
        self.colslots = {}
        for nm, n in (("tq", 16), ("tk", 16), ("tk2", 16), ("ntk2", 16),
                      ("acc", 2), ("bo", 8), ("rs", 8), ("gau", 8),
                      ("lng", 8), ("lnb", 8), ("b1a", 22), ("b1g", 22),
                      ("b2", 8), ("bout", 8), ("bin", 1), ("qs8", 1), ("ks", 1), ("binc", 1), ("bgc", 16), ("am", 8), ("rc", 8), ("ls", 16), ("lt", 2)):
            self.colslots[nm] = None

    def bind_cols(self):
        p = self.p
        cur = 0
        for nm, n in (("tq", 16), ("tk", 16), ("tk2", 16), ("ntk2", 16),
                      ("acc", 2), ("bo", 8), ("rs", 8), ("gau", 8),
                      ("lng", 8), ("lnb", 8), ("b1a", 22), ("b1g", 22),
                      ("b2", 8), ("bout", 8), ("bin", 1), ("qs8", 1), ("ks", 1), ("binc", 1), ("bgc", 16), ("am", 8), ("rc", 8), ("ls", 16), ("lt", 2)):
            self.colslots[nm] = p.COLS[:, cur:cur + n]
            cur += n
        assert cur <= 512

    def join(self, queues=False):
        b = self.b
        sems = ("pe", "ve", "se", "ge") + (("qw", "qx") if queues else ())
        snap = {s: b.cnt[s] for s in sems}
        semmap = {"pe": "tensor", "ve": "vector", "se": "scalar", "ge": "gpsimd"}
        for eng in ("tensor", "vector", "scalar", "gpsimd", "sync"):
            waits = [(s, v) for s, v in snap.items()
                     if v > 0 and semmap.get(s) != eng]
            if waits:
                b.op(eng, lambda e: e.engine_nop() if hasattr(e, "engine_nop") else e.nop(),
                     waits=waits)

    def bank(self, i, n=512, off=0):
        return self.PS[:, 512 * i + off: 512 * i + off + n]

    def pwait(self, banks):
        return [self.ps_gate[i] for i in banks]

    def prelease(self, banks, sem, val):
        for i in banks:
            self.ps_gate[i] = (sem, val)

    def stats(self, pool, t0, tn, shifted, waits):
        p, b = self.p, self.b
        R1 = pool.alloc("R1", [128, tn], F32)
        R2 = pool.alloc("R2", [128, tn], F32)
        R3 = pool.alloc("R3", [128, tn], F32)
        SQ = [pool.alloc("SQa", [128, tn], BF16), pool.alloc("SQb", [128, tn], BF16)]
        nch = tn // 512
        view = p.xbf_shift if shifted else p.xbf_data
        base = ("qx", p.t_consts)
        sq_done = {}
        pe_after_sqx = {}
        for d in range(ND):
            w = list(waits) + [base]
            if d >= 2:
                w.append(("pe", pe_after_sqx[d - 2]))
            b.op("vector", lambda e, d=d: e.tensor_mul(
                SQ[d % 2][:], view(d, t0, tn), view(d, t0, tn)), waits=w, inc="ve")
            sq_done[d] = b.val("ve")
            for c in range(nch):
                w2 = []
                if d == 0 and c == 0:
                    w2 = list(waits) + self.pwait(range(0, nch))
                b.op("tensor", lambda e, d=d, c=c: e.matmul(
                    self.PS[0:1, 512 * c:512 * c + 512],
                    p.ONEC[:, 0:1], view(d, t0 + 512 * c, 512),
                    start=(d == 0), stop=(d == ND - 1)), waits=w2,
                    inc="pe" if (d == ND - 1 and c == nch - 1) else None)
            for c in range(nch):
                w2 = [("ve", sq_done[d])] if c == 0 else []
                if d == 0 and c == 0:
                    w2 += self.pwait(range(4, 4 + nch))
                b.op("tensor", lambda e, d=d, c=c: e.matmul(
                    self.PS[0:1, 2048 + 512 * c:2048 + 512 * c + 512],
                    p.ONEC[:, 0:1], SQ[d % 2][:, 512 * c:512 * c + 512],
                    start=(d == 0), stop=(d == ND - 1)),
                    waits=w2,
                    inc="pe")
            pe_after_sqx[d] = b.val("pe")
        pe_done = b.val("pe")
        for (src_off, dst, bnk) in ((0, R1, 0), (2048, R2, 4)):
            for c in range(nch):
                b.op("vector", lambda e, so=src_off, c=c: e.tensor_copy(
                    p.RP32[0:1, 0:512],
                    self.PS[0:1, so + 512 * c: so + 512 * c + 512]),
                    waits=[("pe", pe_done), base, ("ve", p.t_zeros)], inc="ve")
                vv = b.val("ve")
                b.op("tensor", lambda e, bnk=bnk: e.matmul(
                    self.bank(bnk), p.OH32[:], p.RP32[:, 0:512],
                    start=True, stop=True), waits=[("ve", vv)], inc="pe")
                pv = b.val("pe")
                b.op("vector", lambda e, dst=dst, c=c, bnk=bnk: e.tensor_copy(
                    dst[:, 512 * c:512 * c + 512], self.bank(bnk)),
                    waits=[("pe", pv)], inc="ve")
        self.prelease(range(8), "ve", b.val("ve"))
        b.op("vector", lambda e: e.tensor_scalar_mul(R1[:], R1[:], 1.0 / D))
        b.op("vector", lambda e: e.tensor_scalar_mul(R2[:], R2[:], 1.0 / D))
        b.op("vector", lambda e: e.tensor_mul(R3[:], R1[:], R1[:]))
        b.op("vector", lambda e: e.tensor_sub(R2[:], R2[:], R3[:]))
        b.op("vector", lambda e: e.tensor_scalar_add(R2[:], R2[:], 1.0e-5), inc="ve")
        vv = b.val("ve")
        b.op("scalar", lambda e: e.activation(R2[:], R2[:], AF.Sqrt),
             waits=[("ve", vv)], inc="se")
        sv = b.val("se")
        b.op("vector", lambda e: e.reciprocal(R2[:], R2[:]), waits=[("se", sv)])
        b.op("vector", lambda e: e.tensor_mul(R3[:], R1[:], R2[:]), inc="ve")
        return R2, R3, b.val("ve")

    def normalize(self, dst_bf_view_fn, AREP, BREP, tmp, t0, tn, src_fp32=None,
                  waits=(), gb=None, src_view=None):
        p, b = self.p, self.b
        vals = []
        for d in range(ND):
            if src_view is not None:
                src = src_view(d)
            elif src_fp32 is not None:
                src = src_fp32(d)
            else:
                src = p.xbf_data(d, t0, tn)
            b.op("vector", lambda e, s=src: e.tensor_mul(tmp[:], s, AREP[:]),
                 waits=list(waits) if d == 0 else ())
            if gb is None:
                b.op("vector", lambda e, d=d: e.tensor_sub(
                    dst_bf_view_fn(d), tmp[:], BREP[:]), inc="ve")
            else:
                g_col, b_col = gb
                b.op("vector", lambda e: e.tensor_sub(tmp[:], tmp[:], BREP[:]))
                b.op("vector", lambda e, d=d: e.tensor_scalar(
                    dst_bf_view_fn(d), tmp[:], g_col[d], b_col[d],
                    ALU.mult, ALU.add), inc="ve")
            vals.append(b.val("ve"))
        return vals

    def load_input(self, seq, pool, X):
        p, b = self.p, self.b
        STG = pool.alloc("INSTG", [128, 2, D], F32)
        for d in range(ND):
            b.op("vector", lambda e, d=d: e.memset(p.XBF[:, d, 0:2], 0.0),
                 inc="ve" if d == ND - 1 else None)
        pe_tp = {}
        for tt in range(NT):
            w = [("pe", pe_tp[tt - 2])] if tt >= 2 else []
            b.dma("sync", STG[:, tt % 2],
                  p.x_in[seq, tt * 128:(tt + 1) * 128, :], waits=w, inc="qx")
            qv = b.val("qx")
            pb = (tt % 2) * 2
            for d in range(ND):
                w2 = [("qx", qv)] if d == 0 else []
                if d == 0:
                    w2 += self.pwait([pb, pb + 1])
                b.op("tensor", lambda e, d=d, tt=tt, pb=pb: e.matmul(
                    self.PS[:, pb * 512 + d * 128: pb * 512 + d * 128 + 128],
                    STG[:, tt % 2, d * 128:(d + 1) * 128], p.IDENT32[:],
                    start=True, stop=True), waits=w2,
                    inc="pe" if d == ND - 1 else None)
            pe_tp[tt] = b.val("pe")
            psv = self.PS[:, pb * 512: pb * 512 + 1024].rearrange(
                "p (d c) -> p d c", d=ND)
            b.op("vector", lambda e, tt=tt, psv=psv: e.tensor_copy(
                X[:, :, tt * 128:(tt + 1) * 128], psv),
                waits=[("pe", pe_tp[tt])], inc="ve")
            vcv = b.val("ve")
            b.op("scalar", lambda e, tt=tt, psv=psv: e.copy(
                p.XBF[:, :, 2 + tt * 128: 2 + (tt + 1) * 128], psv),
                waits=[("pe", pe_tp[tt]), ("ve", vcv)], inc="se")
            self.prelease([pb, pb + 1], "se", b.val("se"))
        return b.val("ve"), b.val("se")

    def write_output(self, seq, pool, X=None):
        p, b = self.p, self.b
        STG = pool.alloc("OSTG", [128, 2, D], F32)
        if X is None:
            XF = pool.alloc("XF", [128, 2, T], F32)
            LD = pool.alloc("OLD", [128, 2, T], F32)
        st_val = None
        ve_tile = {}
        for tt in range(NT):
            pb = (tt % 2) * 2
            for d in range(ND):
                if X is not None:
                    lhs = X[:, d, tt * 128:(tt + 1) * 128]
                    w2 = []
                else:
                    raise NotImplementedError
                if d == 0:
                    w2 = w2 + self.pwait([pb, pb + 1])
                b.op("tensor", lambda e, lhs=lhs, d=d, pb=pb: e.matmul(
                    self.PS[:, pb * 512 + d * 128: pb * 512 + d * 128 + 128],
                    lhs, p.IDENT32[:], start=True, stop=True), waits=w2,
                    inc="pe" if d == ND - 1 else None)
            pv = b.val("pe")
            w3 = [("pe", pv)]
            if tt >= 2:
                w3.append(("qx", ve_tile[tt - 2]))
            psv = self.PS[:, pb * 512: pb * 512 + 1024].rearrange(
                "p (d c) -> p d c", d=ND)
            b.op("vector", lambda e, tt=tt, psv=psv: e.tensor_copy(
                STG[:, tt % 2], psv), waits=w3, inc="ve")
            vv = b.val("ve")
            self.prelease([pb, pb + 1], "ve", vv)
            b.dma("sync", p.x_out[seq, tt * 128:(tt + 1) * 128, :],
                  STG[:, tt % 2], waits=[("ve", vv)], inc="qx")
            ve_tile[tt] = b.val("qx")
        return b.val("qx")

    def stage_a(self, seq, pool, X, x_ready):
        p, b = self.p, self.b
        Vtok = pool.alloc("VTOK", [128, NT, D], BF16)
        PH = pool
        ph0 = PH.cur
        sp = PH.fork()
        AREP, BREP, st_v = self.stats(sp, 0, T, False, waits=[x_ready])
        TMP = sp.alloc("TMPA", [128, T], F32)
        xn_vals = self.normalize(lambda d: p.xbf_data(d), AREP, BREP, TMP, 0, T,
                                 src_fp32=lambda d: X[:, d],
                                 waits=[("ve", st_v)])
        xn_done = ("ve", xn_vals[-1])
        self.join()
        if seq == 0:
            p.dump("arep", AREP[:])
            p.dump("brep", BREP[:])
            p.dump("xn", p.XBF[:])
        ap = PH.fork()
        WB = ap.alloc("WA", [128, 2, ND, D], BF16)
        RV = ap.alloc("RV", [128, 2, D], F32)
        SREP = ap.alloc("SREPA", [128, D], BF16)
        B3 = ap.alloc("B3A", [128, D], F32)
        T1 = ap.alloc("T1A", [128, D], F32)
        QD = ap.alloc("QDA", [128, D], BF16)
        assert ap.cur - ph0 <= 57344 + 40960, "A1 phase overflow"
        cS = self.colslots
        TQ, TK = cS["tq"], cS["tk"]
        ACC = cS["acc"]

        wnames = [("L0_attn_Wq", "L0_attn_gq", "L0_attn_vq", "L0_attn_bq", "L0_attn_mq"),
                  ("L0_attn_Wk", "L0_attn_gk", "L0_attn_vk", "L0_attn_bk", "L0_attn_mk"),
                  ("L0_attn_Wv", "L0_attn_gv", "L0_attn_vv", "L0_attn_bv", "L0_attn_mv")]
        wq_dma = {}
        wre = p.w(wnames[0][0], [D, D]).rearrange("(d p) o -> p d o", p=128)
        wq_dma[0] = b.dma("gpsimd", WB[:, 0], wre,
                          waits=[("pe", self.wb_free.get(0, 0))], inc="qw")
        fold_done = {}
        last_row_mm = 0
        sig_vals = []
        for pi in range(3):
            ws = pi % 2
            if pi + 1 < 3:
                wre2 = p.w(wnames[pi + 1][0], [D, D]).rearrange("(d p) o -> p d o", p=128)
                wq_dma[pi + 1] = b.dma("gpsimd", WB[:, 1 - ws], wre2,
                                       waits=[("pe", self.wb_free.get(1 - ws, 0))],
                                       inc="qw")
            Wn, gn, vn, bn, mn = wnames[pi]
            for j, nm in ((0, gn), (1, vn)):
                last_row_mm = self._rep_row(p.w(nm, [D]), RV[:, j], last_row_mm)
            b.op("vector", lambda e: e.tensor_scalar_add(T1[:], RV[:, 1], 1.0e-5),
                 inc="ve")
            vv = b.val("ve")
            b.op("scalar", lambda e: e.activation(T1[:], T1[:], AF.Sqrt),
                 waits=[("ve", vv)], inc="se")
            sv = b.val("se")
            b.op("vector", lambda e: e.reciprocal(T1[:], T1[:]), waits=[("se", sv)])
            b.op("vector", lambda e: e.tensor_mul(T1[:], T1[:], RV[:, 0]))
            b.op("vector", lambda e: e.tensor_copy(SREP[:], T1[:]), inc="ve")
            srep_v = b.val("ve")
            for j, nm in ((0, bn), (1, mn)):
                last_row_mm = self._rep_row(p.w(nm, [D]), RV[:, j], last_row_mm)
            b.op("vector", lambda e: e.tensor_mul(B3[:], RV[:, 1], T1[:]))
            b.op("vector", lambda e: e.tensor_sub(B3[:], RV[:, 0], B3[:]))
            b.op("vector", lambda e: e.tensor_scalar_sub(B3[:], B3[:], 1.0), inc="ve")
            b3_v = b.val("ve")
            for d in range(ND):
                w2 = [("qw", wq_dma[pi]), ("ve", srep_v)] if d == 0 else []
                b.op("vector", lambda e, d=d, ws=ws: e.tensor_mul(
                    WB[:, ws, d], WB[:, ws, d], SREP[:]), waits=w2,
                    inc="ve" if d == ND - 1 else None)
            fold_done[pi] = b.val("ve")
            for tt in range(NT):
                pb = (tt % 2) * 2
                for oc in range(2):
                    for d in range(ND):
                        w2 = []
                        if oc == 0 and d == 0:
                            w2 = [xn_done, ("ve", fold_done[pi])] + self.pwait([pb, pb + 1])
                        b.op("tensor", lambda e, d=d, tt=tt, oc=oc, pb=pb, ws=ws: e.matmul(
                            self.PS[:, pb * 512 + oc * 512: pb * 512 + oc * 512 + 512],
                            p.xbf_data(d)[:, tt * 128:(tt + 1) * 128],
                            WB[:, ws, d, oc * 512:oc * 512 + 512],
                            start=(d == 0), stop=(d == ND - 1)), waits=w2,
                            inc="pe" if (oc == 1 and d == ND - 1) else None)
                mm_v = b.val("pe")
                self.wb_free[ws] = mm_v
                psv = self.PS[:, pb * 512: pb * 512 + 1024]
                b.op("vector", lambda e, psv=psv: e.tensor_add(T1[:], psv, B3[:]),
                     waits=[("pe", mm_v), ("ve", b3_v),
                            ("se", self.a_t1_se)], inc="ve")
                t1v = b.val("ve")
                self.prelease([pb, pb + 1], "ve", t1v)
                if pi == 2:
                    b.op("scalar", lambda e, tt=tt: e.activation(
                        Vtok[:, tt], T1[:], AF.Sigmoid, scale=4.0),
                        waits=[("ve", t1v)], inc="se")
                    sig_vals.append(b.val("se"))
                    self.a_t1_se = b.val("se")
                else:
                    ab = tt % 2
                    w3 = [("ve", t1v)]
                    if tt >= 2:
                        w3.append(("ve", self.acc_read.get((pi, tt - 2), 0)))
                    b.op("scalar", lambda e, ab=ab: e.activation(
                        QD[:], T1[:], AF.Sigmoid, scale=4.0,
                        accum_out=ACC[:, ab:ab + 1]), waits=w3, inc="se")
                    sv = b.val("se")
                    dstc = (TQ if pi == 0 else TK)
                    self.a_t1_se = sv
                    b.op("vector", lambda e, tt=tt, ab=ab, dstc=dstc: e.tensor_scalar(
                        dstc[:, tt:tt + 1], ACC[:, ab:ab + 1], -1.0 / D, 1.0,
                        ALU.mult, ALU.add), waits=[("se", sv)], inc="ve")
                    self.acc_read[(pi, tt)] = b.val("ve")
        v_ready = ("se", sig_vals[-1])
        tqk_ready = ("ve", self.acc_read[(1, NT - 1)])
        self.join()
        if seq == 0:
            p.dump("tq", cS["tq"][:])
            p.dump("tk", cS["tk"][:])
            p.dump("vtok", Vtok[:])

        a2 = PH.fork()
        AVT = a2.alloc("AVT", [128, ND, T], BF16)
        TQR = a2.alloc("TQR", [128, 512], F32)
        E1 = a2.alloc("E1", [128, 512], F32)
        E2 = a2.alloc("E2", [128, 512], F32)
        AB = a2.alloc("AB", [128, 2, 512], BF16)
        TK2 = cS["tk2"]; NTK2 = cS["ntk2"]
        TQT = a2.alloc("TQT", [16, 128], F32)
        b.op("vector", lambda e: e.tensor_scalar_mul(TK2[:], TK[:], 2.0),
             waits=[tqk_ready])
        b.op("vector", lambda e: e.tensor_scalar_mul(NTK2[:], TK[:], -2.0), inc="ve")
        tk2_v = b.val("ve")
        b.op("tensor", lambda e: e.matmul(
            self.PS[:, 512:512 + 128].bitcast(F32)[0:16, :] if False else
            self.PS[0:16, 512:512 + 128], TQ[:], p.IDENT32[:],
            start=True, stop=True),
            waits=[tqk_ready, ("ve", tk2_v)] + self.pwait([1]), inc="pe")
        pv = b.val("pe")
        b.op("vector", lambda e: e.tensor_copy(TQT[:], self.PS[0:16, 512:512 + 128]),
             waits=[("pe", pv)], inc="ve")
        tqt_v = b.val("ve")
        self.prelease([1], "ve", tqt_v)
        av_pe = {}
        abuf_v = {}
        evac_v = 0
        for qg in range(4):
            w = [("ve", tqt_v), ("ve", p.t_zeros), ("pe", self.rp32_user)]
            b.dma("sync", p.RP32[0:1, 0:512],
                  TQT[qg * 4:(qg + 1) * 4, :], waits=w, inc="qx")
            qv = b.val("qx")
            b.op("tensor", lambda e: e.matmul(
                self.bank(0), p.OH32[:], p.RP32[:, 0:512], start=True, stop=True),
                waits=[("qx", qv)] + self.pwait([0]), inc="pe")
            self.rp32_user = b.val("pe")
            b.op("vector", lambda e: e.tensor_copy(TQR[:], self.bank(0)),
                 waits=[("pe", self.rp32_user)], inc="ve")
            tqr_v = b.val("ve")
            self.prelease([0], "ve", tqr_v)
            for kt in range(NT):
                par = kt % 2
                w_s1 = [("ve", tqr_v), ("ve", tk2_v)]
                if (qg, kt) != (0, 0):
                    w_s1.append(("ve", abuf_v.get((qg, kt - 1), abuf_v.get((qg - 1, NT - 1), 0))))
                b.op("scalar", lambda e, kt=kt: e.activation(
                    E1[:], TQR[:], AF.Exp, scale=-2.0, bias=TK2[:, kt:kt + 1]),
                    waits=w_s1)
                b.op("scalar", lambda e, kt=kt: e.activation(
                    E2[:], TQR[:], AF.Exp, scale=2.0, bias=NTK2[:, kt:kt + 1]),
                    inc="se")
                sv = b.val("se")
                b.op("vector", lambda e: e.tensor_tensor(E1[:], E1[:], E2[:], ALU.min),
                     waits=[("se", sv)])
                b.op("vector", lambda e, kt=kt: e.tensor_scalar(
                    E2[:], TQR[:], TK[:, kt:kt + 1], 2.0, ALU.is_lt, ALU.mult))
                b.op("vector", lambda e: e.tensor_mul(E2[:], E1[:], E2[:]))
                b.op("vector", lambda e: e.tensor_sub(E2[:], E2[:], E1[:]))
                wv = []
                if kt >= 2:
                    wv = [("pe", av_pe[(qg, kt - 2)])]
                elif qg > 0 and kt < 2:
                    wv = [("pe", av_pe[(qg - 1, NT - 2 + kt)])]
                b.op("vector", lambda e, par=par: e.tensor_scalar(
                    AB[:, par], E2[:], 0.9, 0.9, ALU.mult, ALU.add), waits=wv,
                    inc="ve")
                abuf_v[(qg, kt)] = b.val("ve")
                for et in range(ND):
                    w2 = []
                    if et == 0:
                        w2 = [("ve", abuf_v[(qg, kt)])]
                        if kt == 0:
                            w2 += self.pwait(range(8)) + [v_ready]
                    b.op("tensor", lambda e, kt=kt, et=et, par=par: e.matmul(
                        self.bank(et), Vtok[:, kt, et * 128:(et + 1) * 128],
                        AB[:, par], start=(kt == 0), stop=(kt == NT - 1)),
                        waits=w2, inc="pe" if et == ND - 1 else None)
                av_pe[(qg, kt)] = b.val("pe")
            for et in range(ND):
                b.op("vector", lambda e, et=et, qg=qg: e.tensor_copy(
                    AVT[:, et, qg * 512:(qg + 1) * 512], self.bank(et)),
                    waits=[("pe", av_pe[(qg, NT - 1)])], inc="ve")
            evac_v = b.val("ve")
            self.prelease(range(8), "ve", evac_v)
        avt_ready = ("ve", evac_v)
        self.join()
        if seq == 0:
            p.dump("avt", AVT[:])

        a3off = a2.cur
        WO = a2.alloc("WOA", [128, 2, ND, 128], BF16)
        wo = p.w("L0_attn_Wo", [D, D])
        wo_dma = {}
        for od in range(2):
            wo_dma[od] = b.dma("gpsimd", WO[:, od % 2],
                               wo[:, od * 128:(od + 1) * 128].rearrange(
                                   "(e p) c -> p e c", p=128), inc="qw")
        bo_col = self.colslots["bo"]
        if seq == 0:
            p.dump("wo0", WO[:, 0])
            p.dump("bocol", bo_col)
        res_v = {}
        for od in range(ND):
            for tc in range(4):
                bk = tc % 2
                for et in range(ND):
                    w2 = []
                    if et == 0:
                        w2 = self.pwait([bk]) + [("qw", wo_dma[od]), avt_ready]
                    b.op("tensor", lambda e, od=od, tc=tc, et=et, bk=bk: e.matmul(
                        self.bank(bk), WO[:, od % 2, et], AVT[:, et, tc * 512:(tc + 1) * 512],
                        start=(et == 0), stop=(et == ND - 1)), waits=w2,
                        inc="pe" if et == ND - 1 else None)
                pv = b.val("pe")
                if tc == 3:
                    self.wo_free[od % 2] = pv
                b.op("vector", lambda e, od=od, tc=tc, bk=bk: e.scalar_tensor_tensor(
                    X[:, od, tc * 512:(tc + 1) * 512], self.bank(bk),
                    bo_col[:, od:od + 1], X[:, od, tc * 512:(tc + 1) * 512],
                    ALU.add, ALU.add), waits=[("pe", pv), ("qx", self.bocol_ld)],
                    inc="ve")
                self.prelease([bk], "ve", b.val("ve"))
            res_v[od] = b.val("ve")
            if od + 2 < ND:
                wo_dma[od + 2] = b.dma(
                    "gpsimd", WO[:, od % 2],
                    wo[:, (od + 2) * 128:(od + 3) * 128].rearrange(
                        "(e p) c -> p e c", p=128),
                    waits=[("pe", self.wo_free[od % 2])], inc="qw")
            b.op("scalar", lambda e, od=od: e.copy(
                p.XBF[:, od, 2:2 + T], X[:, od]), waits=[("ve", res_v[od])],
                inc="se")
        self.join()
        if seq == 0:
            p.dump("xafter", X[:])
        return ("se", b.val("se"))

    def _rep_row(self, row_dram, dst, prev_mm):
        p, b = self.p, self.b
        n = row_dram.shape[0]
        b.dma("sync", p.RP32[0:1, 0:n], row_dram.unsqueeze(0),
              waits=[("ve", p.t_zeros), ("pe", self.rp32_user)], inc="qx")
        qv = b.val("qx")
        nch = (n + 511) // 512
        for c in range(nch):
            b.op("tensor", lambda e, c=c: e.matmul(
                self.bank(0 if c == 0 else 1),
                p.OH32[:], p.RP32[:, c * 512:(c + 1) * 512],
                start=True, stop=True),
                waits=[("qx", qv)] + self.pwait([c]), inc="pe")
        self.rp32_user = b.val("pe")
        pv = self.rp32_user
        for c in range(nch):
            b.op("vector", lambda e, c=c: e.tensor_copy(
                dst[:, c * 512:(c + 1) * 512], self.bank(c)),
                waits=[("pe", pv)], inc="ve")
        self.prelease(list(range(nch)), "ve", b.val("ve"))
        return b.val("ve")


def build_core_program():
    p = Prog()
    p.setup_persistent()
    st = Stages(p)
    st.bind_cols()
    b = p.b

    bo = p.w("L0_attn_bo", [D])
    b.dma("sync", st.colslots["bo"], bo.rearrange("(d q) -> q d", q=128), inc="qx", slow=True)
    st.bocol_ld = b.val("qx")
    for i, nm in enumerate(["g_qq", "b_qq", "g_qk", "b_qk", "g_lq", "b_lq",
                            "g_lk", "b_lk"]):
        b.dma("sync", st.colslots["gau"][:, i:i + 1],
              p.w(f"L1_attn_{nm}", [128]).unsqueeze(1), inc="qx")
    b.dma("sync", st.colslots["bgc"],
          p.w("L1_attn_bg", [2048]).rearrange("(e q) -> q e", q=128), inc="qx",
          slow=True)
    b.dma("sync", st.colslots["binc"],
          p.w("L1_attn_bin_", [128]).unsqueeze(1), inc="qx")
    b.dma("sync", st.colslots["bout"],
          p.w("L1_attn_bout", [D]).rearrange("(d q) -> q d", q=128), inc="qx",
          slow=True)
    st.c_bg_ld = b.val("qx")
    cqk = float(2048.0 ** -0.25)
    for i in range(4):
        b.op("vector", lambda e, i=i: e.tensor_scalar(
            st.colslots["gau"][:, i:i + 1], st.colslots["gau"][:, i:i + 1],
            cqk, None, ALU.mult),
            waits=[("qx", st.c_bg_ld)] if i == 0 else (),
            inc="ve" if i == 3 else None)
    st.c_gau_v = b.b.val("ve") if False else b.val("ve")

    dbg = DBG_STAGE
    def seq_end():
        st.join()
    for seq in range(SEQS):
        pool = p.mem.pool()
        X = pool.alloc("X", [128, ND, T], F32)
        ld_ve, ld_se = st.load_input(seq, pool.fork(), X)
        x_ready = ("se", ld_se)
        if dbg == "in":
            st.join()
            last = st.write_output(seq, pool.fork(), X=X)
            seq_end()
            continue
        st.join()
        xr = st.stage_a(seq, pool.fork(), X, x_ready)
        st.join()
        if dbg == "a":
            last = st.write_output(seq, pool.fork(), X=X)
            seq_end()
            continue
        xr = st.stage_b(seq, pool.fork(), X, xr)
        st.join()
        if dbg == "b":
            last = st.write_output(seq, pool.fork(), X=X)
            seq_end()
            continue
        spill_qx = st.spill_x(seq, X)
        st.join()
        st.stage_c(seq, p.mem.pool(), spill_qx)
        st.join()
        if dbg == "c":
            last = st.write_output_hbm(seq, p.mem.pool(), [p.hb_xo])
            seq_end()
            continue
        st.stage_d(seq, p.mem.pool(), p.b.val("qx"))
        st.join()
        last = st.write_output_hbm(seq, p.mem.pool(), [p.hb_xo, p.hb_xi])
        seq_end()
    b.op("sync", lambda e: e.nop(), waits=[("qx", b.val("qx"))])
    b.emit()
    return p


def kernel(**inputs):
    x_full = np.ascontiguousarray(inputs["L0_attn_x"], dtype=np.float32)
    assert x_full.shape == (NCORES * SEQS, T, D)
    p = build_core_program()
    in_maps = []
    wcache = {}
    for name in p._win:
        if name in p.inputs:
            wcache[name] = p.inputs[name]
        else:
            wcache[name] = np.ascontiguousarray(inputs[name], dtype=np.float32)
    for c in range(NCORES):
        m = dict(wcache)
        m["x"] = x_full[c * SEQS:(c + 1) * SEQS]
        in_maps.append(m)
    res = run_bass_kernel_spmd(p.nc, in_maps, core_ids=list(range(NCORES)))
    out = np.concatenate([np.asarray(r["y"]) for r in res.results], axis=0)
    return out.astype(np.float32)


def _stage_b(self, seq, pool, X, xr):
    p, b = self.p, self.b
    cS = self.colslots
    ph0 = pool.cur
    if seq == 0:
        f1 = pool.fork()
        RH32 = f1.alloc("RH32", [128, ND, D], F32)
        RT32 = f1.alloc("RT32", [128, ND, D], F32)
        rhs = p.w("L0_ffn_rhs", [D, D])
        b.dma("sync", RH32[:], rhs.rearrange("(d q) o -> q d o", q=128),
              waits=[xr], inc="qx")
        qv = b.val("qx")
        for ot in range(ND):
            pb = (ot % 2) * 2
            for d in range(ND):
                w2 = [("qx", qv)] + self.pwait([pb, pb + 1]) if d == 0 else []
                b.op("tensor", lambda e, ot=ot, d=d, pb=pb: e.matmul(
                    self.PS[:, pb * 512 + d * 128: pb * 512 + d * 128 + 128],
                    RH32[:, d, ot * 128:(ot + 1) * 128], p.IDENT32[:],
                    start=True, stop=True), waits=w2,
                    inc="pe" if d == ND - 1 else None)
            pv = b.val("pe")
            psv = self.PS[:, pb * 512: pb * 512 + 1024]
            b.op("vector", lambda e, ot=ot, psv=psv: e.tensor_copy(
                RT32[:, ot], psv), waits=[("pe", pv)], inc="ve")
            self.prelease([pb, pb + 1], "ve", b.val("ve"))
        for ot in range(ND):
            b.op("vector", lambda e, ot=ot: e.tensor_reduce(
                cS["am"][:, ot:ot + 1], RT32[:, ot], AX.X, ALU.max,
                apply_absolute_value=True), inc="ve" if ot == ND - 1 else None)
        amv = b.val("ve")
        b.op("vector", lambda e: e.tensor_scalar(cS["rs"][:], cS["am"][:],
                                                 1.0 / 127.0, None, ALU.mult),
             waits=[("ve", amv)])
        b.op("vector", lambda e: e.reciprocal(cS["rc"][:], cS["am"][:]),
             waits=[("ve", amv)], inc="ve")
        rcv0 = b.val("ve")
        b.op("vector", lambda e: e.tensor_scalar(cS["rc"][:], cS["rc"][:],
                                                 127.0, None, ALU.mult),
             waits=[("ve", rcv0)], inc="ve")
        rc_v = b.val("ve")
        self.join()
        f2 = PoolCursor(p.mem, ph0)
        QR = f2.alloc("QRH", [128, ND, D], BF16)
        QTB = f2.alloc("QTB", [128, D], BF16)
        TMPQ = f2.alloc("TMPQB", [128, D], F32)
        I32T = f2.alloc("I32B", [128, D], I32)
        assert f2.cur <= ph0 + 32768 + 4096, "B prep overlay overflow"
        for ot in range(ND):
            b.op("vector", lambda e, ot=ot: e.tensor_scalar(
                TMPQ[:], RT32[:, ot], cS["rc"][:, ot:ot + 1], None, ALU.mult),
                waits=[("ve", rc_v)] if ot == 0 else ())
            b.op("vector", lambda e: e.tensor_copy(I32T[:], TMPQ[:]))
            b.op("vector", lambda e: e.tensor_copy(TMPQ[:], I32T[:]))
            b.op("vector", lambda e: e.tensor_copy(QTB[:], TMPQ[:]), inc="ve")
            qtv = b.val("ve")
            for d in range(ND):
                pb2 = 4 + (d % 2)
                b.op("tensor", lambda e, d=d, pb2=pb2: e.matmul(
                    self.bank(pb2, n=128), QTB[:, d * 128:(d + 1) * 128],
                    p.IDENT[:], start=True, stop=True),
                    waits=[("ve", qtv)] + self.pwait([pb2]), inc="pe")
                pv = b.val("pe")
                b.op("vector", lambda e, d=d, ot=ot, pb2=pb2: e.tensor_copy(
                    QR[:, d, ot * 128:(ot + 1) * 128], self.bank(pb2, n=128)),
                    waits=[("pe", pv)], inc="ve")
                self.prelease([pb2], "ve", b.val("ve"))
        b.dma("sync", p.hb_qrhs.rearrange("q (d o) -> q d o", d=ND), QR[:],
              waits=[("ve", b.val("ve"))], inc="qx")
        self.join()
    f = PoolCursor(p.mem, ph0)
    QR = f.alloc("QRS", [128, ND, D], BF16)
    LSREP = f.alloc("LSREP", [128, T], F32)
    LST = f.alloc("LST", [16, 128], F32)
    TMPQ = f.alloc("TMPQS", [128, D], F32)
    I32T = f.alloc("I32S", [128, D], I32)
    TMP2 = f.alloc("TMP2B", [128, 512], F32)
    qr_dma = b.dma("gpsimd", QR[:], p.hb_qrhs, waits=[("qx", b.val("qx"))], inc="qw")
    sp = f.fork()
    XNT = f.alloc("XNT", [128, NT, D], BF16)
    AREP, BREP, st_v = self.stats(sp, 0, T, False, waits=[xr])
    TMPN = sp.alloc("TMPNB", [128, T], F32)
    xn_vals = self.normalize(lambda d: p.xbf_data(d), AREP, BREP, TMPN, 0, T,
                             src_fp32=lambda d: X[:, d], waits=[("ve", st_v)])
    xn_done = ("ve", xn_vals[-1])
    self.join()
    if seq == 0:
        p.dump("bxn", p.XBF[:])
    for tt in range(NT):
        pb = (tt % 2) * 2
        for d in range(ND):
            w2 = self.pwait([pb, pb + 1]) if d == 0 else []
            b.op("tensor", lambda e, tt=tt, d=d, pb=pb: e.matmul(
                self.PS[:, pb * 512 + d * 128: pb * 512 + d * 128 + 128],
                p.xbf_data(d)[:, tt * 128:(tt + 1) * 128], p.IDENT[:],
                start=True, stop=True), waits=w2,
                inc="pe" if d == ND - 1 else None)
        pv = b.val("pe")
        psv = self.PS[:, pb * 512: pb * 512 + 1024]
        b.op("vector", lambda e, psv=psv: e.tensor_copy(TMPQ[:], psv),
             waits=[("pe", pv)], inc="ve")
        self.prelease([pb, pb + 1], "ve", b.val("ve"))
        b.op("vector", lambda e: e.tensor_reduce(
            cS["lt"][:, 0:1], TMPQ[:], AX.X, ALU.max, apply_absolute_value=True),
            inc="ve")
        ltv0 = b.val("ve")
        b.op("vector", lambda e, tt=tt: e.tensor_scalar(
            cS["ls"][:, tt:tt + 1], cS["lt"][:, 0:1], 1.0 / 127.0, None, ALU.mult),
            waits=[("ve", ltv0)])
        b.op("vector", lambda e: e.reciprocal(cS["lt"][:, 1:2], cS["lt"][:, 0:1]),
             waits=[("ve", ltv0)], inc="ve")
        ltv1 = b.val("ve")
        b.op("vector", lambda e: e.tensor_scalar(
            cS["lt"][:, 1:2], cS["lt"][:, 1:2], 127.0, None, ALU.mult),
            waits=[("ve", ltv1)], inc="ve")
        ltv = b.val("ve")
        b.op("vector", lambda e: e.tensor_scalar(
            TMPQ[:], TMPQ[:], cS["lt"][:, 1:2], None, ALU.mult),
            waits=[("ve", ltv)])
        b.op("vector", lambda e: e.tensor_copy(I32T[:], TMPQ[:]))
        b.op("vector", lambda e: e.tensor_copy(TMPQ[:], I32T[:]))
        b.op("vector", lambda e, tt=tt: e.tensor_copy(XNT[:, tt], TMPQ[:]), inc="ve")
    if seq == 0:
        self.join()
        p.dump("qlhs", XNT[:])
        p.dump("tmpq", TMPQ[:])
        p.dump("i32t", I32T[:])
        p.dump("ltcol", cS["lt"])
        p.dump("lscol", cS["ls"])
        p.dump("rscol", cS["rs"])
        p.dump("qrhs", QR[:])
    b.op("tensor", lambda e: e.matmul(self.PS[0:16, 512:512 + 128], cS["ls"][:],
                                      p.IDENT32[:], start=True, stop=True),
         waits=[("ve", b.val("ve"))] + self.pwait([1]), inc="pe")
    pv = b.val("pe")
    b.op("vector", lambda e: e.tensor_copy(LST[:], self.PS[0:16, 512:512 + 128]),
         waits=[("pe", pv)], inc="ve")
    lst_v = b.val("ve")
    self.prelease([1], "ve", lst_v)
    for tc in range(4):
        b.dma("sync", p.RP32[0:1, 0:512], LST[tc * 4:(tc + 1) * 4, :],
              waits=[("ve", lst_v), ("ve", p.t_zeros), ("pe", self.rp32_user)],
              inc="qx")
        qv = b.val("qx")
        b.op("tensor", lambda e: e.matmul(self.bank(1), p.OH32[:],
                                          p.RP32[:, 0:512], start=True, stop=True),
             waits=[("qx", qv)] + self.pwait([1]), inc="pe")
        self.rp32_user = b.val("pe")
        b.op("vector", lambda e, tc=tc: e.tensor_copy(
            LSREP[:, tc * 512:(tc + 1) * 512], self.bank(1)),
            waits=[("pe", self.rp32_user)], inc="ve")
        self.prelease([1], "ve", b.val("ve"))
    lsrep_v = b.val("ve")
    tpq_done = {}
    for d in range(ND):
        for tg in range(4):
            pb = 4 + (tg % 2)
            for ti in range(4):
                tt = tg * 4 + ti
                w2 = self.pwait([pb]) if ti == 0 else []
                b.op("tensor", lambda e, d=d, tt=tt, ti=ti, pb=pb: e.matmul(
                    self.bank(pb, n=128, off=ti * 128),
                    XNT[:, tt, d * 128:(d + 1) * 128], p.IDENT[:],
                    start=True, stop=True), waits=w2,
                    inc="pe" if ti == 3 else None)
            pv = b.val("pe")
            b.op("vector", lambda e, d=d, tg=tg, pb=pb: e.tensor_copy(
                p.xbf_data(d)[:, tg * 512:(tg + 1) * 512], self.bank(pb)),
                waits=[("pe", pv)], inc="ve")
            self.prelease([pb], "ve", b.val("ve"))
    qc_done = b.val("ve")
    for od in range(ND):
        for tc in range(4):
            bk = tc % 2
            for d in range(ND):
                w2 = []
                if d == 0:
                    w2 = self.pwait([bk]) + [("ve", qc_done), ("qw", qr_dma)]
                b.op("tensor", lambda e, od=od, tc=tc, d=d, bk=bk: e.matmul(
                    self.bank(bk), QR[:, d, od * 128:(od + 1) * 128],
                    p.xbf_data(d)[:, tc * 512:(tc + 1) * 512],
                    start=(d == 0), stop=(d == ND - 1)), waits=w2,
                    inc="pe" if d == ND - 1 else None)
            pv = b.val("pe")
            b.op("vector", lambda e, od=od, tc=tc, bk=bk: e.scalar_tensor_tensor(
                TMP2[:], self.bank(bk), cS["rs"][:, od:od + 1],
                LSREP[:, tc * 512:(tc + 1) * 512], ALU.mult, ALU.mult),
                waits=[("pe", pv), ("ve", lsrep_v)], inc="ve")
            vv = b.val("ve")
            self.prelease([bk], "ve", vv)
            b.op("vector", lambda e, od=od, tc=tc: e.tensor_add(
                X[:, od, tc * 512:(tc + 1) * 512],
                X[:, od, tc * 512:(tc + 1) * 512], TMP2[:]), inc="ve")
    bmm_done = b.val("pe")
    res_done = b.val("ve")
    if seq == 0:
        self.join()
        p.dump("lsrep", LSREP[:])
        p.dump("xafterb", X[:])
    for od in range(ND):
        b.op("scalar", lambda e, od=od: e.copy(
            p.XBF[:, od, 2:2 + T], X[:, od]),
            waits=[("pe", bmm_done), ("ve", res_done)], inc="se")
    self.join()
    return ("se", b.val("se"))


Stages.stage_b = _stage_b


def _stage_c(self, seq, pool, spill_qx=0):
    p, b = self.p, self.b
    cS = self.colslots
    GP = 4
    WV = pool.alloc("WVC", [128, ND, 2048], BF16)
    WG = pool.alloc("WGC", [128, ND, 2048], BF16)
    WIN = pool.alloc("WINC", [128, ND, 128], BF16)
    WOT = pool.alloc("WOTC", [128, 2, 16, 128], BF16)
    XNC = pool.alloc("XNC", [128, ND, 512], BF16)
    VT = pool.alloc("VTC", [128, 4, 2048], BF16)
    OT = pool.alloc("OTC", [128, 16, 512], BF16)
    KV32 = pool.alloc("KV32", [128, 2048], F32)
    KVBF = pool.alloc("KVBF", [128, 2048], BF16)
    BVR = pool.alloc("BVR", [128, 2048], F32)
    scratch_off = pool.cur
    XH = pool.alloc("XH", [128, 512], F32)
    QQ = pool.alloc("QQC", [128, 512], BF16)
    QK = pool.alloc("QKC", [128, 512], BF16)
    LQ = pool.alloc("LQC", [128, 512], BF16)
    LK = pool.alloc("LKC", [128, 512], BF16)
    ATT = pool.alloc("ATTC", [128, 2, 256], BF16)
    LKT = pool.alloc("LKT", [128, 2, 128], BF16)
    TT = pool.alloc("TTC", [128, 512], F32)
    TV = pool.alloc("TVC", [128, 512], F32)
    XST = pool.alloc("XSTC", [128, 2, 512], F32)
    XOLD = pool.alloc("XOLDC", [128, 2, 512], F32)

    wv = p.w("L1_attn_Wv", [D, 2048]).rearrange("(d q) o -> q d o", q=128)
    wg = p.w("L1_attn_Wg", [D, 2048]).rearrange("(d q) o -> q d o", q=128)
    win = p.w("L1_attn_Win", [D, 128]).rearrange("(d q) o -> q d o", q=128)
    dv = b.dma("gpsimd", WV[:], wv,
               waits=[("pe", self.c_w_free), ("qx", spill_qx)], inc="qw")
    dg = b.dma("gpsimd", WG[:], wg, waits=[("qx", spill_qx)], inc="qw")
    dn = b.dma("gpsimd", WIN[:], win, inc="qw")
    bvw = p.w("L1_attn_bv", [2048])
    for c in range(4):
        b.dma("sync", p.RP32[0:1, 0:512], bvw[c * 512:(c + 1) * 512].unsqueeze(0),
              waits=[("ve", p.t_zeros), ("pe", self.rp32_user)], inc="qx")
        qv = b.val("qx")
        b.op("tensor", lambda e: e.matmul(self.bank(0), p.OH32[:], p.RP32[:, 0:512],
                                          start=True, stop=True),
             waits=[("qx", qv)] + self.pwait([0]), inc="pe")
        self.rp32_user = b.val("pe")
        b.op("vector", lambda e, c=c: e.tensor_copy(
            BVR[:, c * 512:(c + 1) * 512], self.bank(0)),
            waits=[("pe", self.rp32_user)], inc="ve")
        self.prelease([0], "ve", b.val("ve"))
    bvr_v = b.val("ve")
    b.op("vector", lambda e: e.memset(KV32[:], 0.0))
    b.op("vector", lambda e: e.memset(KVBF[:], 0.0), inc="ve")
    kv_v = b.val("ve")
    kvbf_pe = 0

    wout = p.w("L1_attn_Wout", [2048, D])
    gaus = self.c_gau_v

    xh_se = 0
    att_pe = {}
    for gp in range(GP):
        t0 = gp * 512
        sp = PoolCursor(p.mem, scratch_off)
        AREP, BREP, st_v = self.stats(sp, t0, 512, False, waits=[("ve", kv_v)])
        TMPN = sp.alloc("TMPNC", [128, 512], F32)
        xn_vals = self.normalize(lambda d: XNC[:, d], AREP, BREP, TMPN, t0, 512,
                                 waits=[("ve", st_v)])
        xn_v = ("ve", xn_vals[-1])
        vt_se = []
        for tt in range(4):
            for oc in range(4):
                bk = (tt * 4 + oc) % 2
                for d in range(ND):
                    w2 = []
                    if d == 0:
                        w2 = self.pwait([bk]) + [xn_v, ("qw", dv)]
                    b.op("tensor", lambda e, tt=tt, oc=oc, d=d, bk=bk: e.matmul(
                        self.bank(bk), XNC[:, d, tt * 128:(tt + 1) * 128],
                        WV[:, d, oc * 512:(oc + 1) * 512],
                        start=(d == 0), stop=(d == ND - 1)), waits=w2,
                        inc="pe" if d == ND - 1 else None)
                pv = b.val("pe")
                b.op("vector", lambda e, oc=oc, bk=bk: e.tensor_add(
                    TV[:], self.bank(bk), BVR[:, oc * 512:(oc + 1) * 512]),
                    waits=[("pe", pv), ("ve", bvr_v),
                           ("se", vt_se[-1] if vt_se else 0)], inc="ve")
                vv = b.val("ve")
                self.prelease([bk], "ve", vv)
                b.op("scalar", lambda e, tt=tt, oc=oc: e.activation(
                    VT[:, tt, oc * 512:(oc + 1) * 512], TV[:], AF.Silu),
                    waits=[("ve", vv)], inc="se")
                vt_se.append(b.val("se"))
        vt_ready = ("se", vt_se[-1])
        if seq == 0 and gp == 0:
            p.dump("cxn", XNC[:])
            p.dump("cvt", VT[:])
        gt_se = []
        for et in range(16):
            bk = 2 + et % 2
            for d in range(ND):
                w2 = []
                if d == 0:
                    w2 = self.pwait([bk]) + [xn_v, ("qw", dg), ("qx", self.c_bg_ld)]
                b.op("tensor", lambda e, et=et, d=d, bk=bk: e.matmul(
                    self.bank(bk), WG[:, d, et * 128:(et + 1) * 128],
                    XNC[:, d, 0:512],
                    start=(d == 0), stop=(d == ND - 1)), waits=w2,
                    inc="pe" if d == ND - 1 else None)
            pv = b.val("pe")
            b.op("scalar", lambda e, et=et, bk=bk: e.activation(
                OT[:, et], self.bank(bk), AF.Silu,
                bias=cS["bgc"][:, et:et + 1]), waits=[("pe", pv)], inc="se")
            gt_se.append(b.val("se"))
            self.prelease([bk], "se", b.val("se"))
        gt_ready = ("se", gt_se[-1])
        if seq == 0 and gp == 0:
            p.dump("cgate", OT[:])
        for d in range(ND):
            w2 = []
            if d == 0:
                w2 = self.pwait([4]) + [xn_v, ("qw", dn), ("qx", self.c_bg_ld)]
            b.op("tensor", lambda e, d=d: e.matmul(
                self.bank(4), WIN[:, d], XNC[:, d, 0:512],
                start=(d == 0), stop=(d == ND - 1)), waits=w2,
                inc="pe" if d == ND - 1 else None)
        pv = b.val("pe")
        b.op("scalar", lambda e: e.activation(
            XH[:], self.bank(4), AF.Silu, bias=cS["binc"][:]),
            waits=[("pe", pv), ("se", xh_se)], inc="se")
        xh_se = b.val("se")
        self.prelease([4], "se", xh_se)
        if seq == 0 and gp == 0:
            p.dump("cxh", XH[:])
        for (dst, gsl, bsl) in ((QQ, 0, 1), (QK, 2, 3), (LQ, 4, 5), (LK, 6, 7)):
            b.op("vector", lambda e, dst=dst, gsl=gsl, bsl=bsl: e.tensor_scalar(
                dst[:], XH[:], cS["gau"][:, gsl:gsl + 1], cS["gau"][:, bsl:bsl + 1],
                ALU.mult, ALU.add),
                waits=[("se", xh_se), ("ve", gaus), ("pe", att_pe.get(gp - 1, 0))],
                inc="ve")
        qq_v = b.val("ve")
        if seq == 0 and gp == 0:
            p.dump("cqq", QQ[:]); p.dump("cqk", QK[:])
            p.dump("clq", LQ[:]); p.dump("clk", LK[:])
        for gi in range(2):
            g0 = gi * 256
            at_v = []
            for mc in range(2):
                b.op("tensor", lambda e, g0=g0, mc=mc: e.matmul(
                    self.bank(5, n=256, off=mc * 256),
                    QK[:, g0 + mc * 128: g0 + mc * 128 + 128],
                    QQ[:, g0:g0 + 256], start=True, stop=True),
                    waits=(self.pwait([5]) + [("ve", qq_v)]) if mc == 0 else [],
                    inc="pe" if mc == 1 else None)
            pv = b.val("pe")
            for mc in range(2):
                b.op("scalar", lambda e, mc=mc: e.activation(
                    TT[:, mc * 256:(mc + 1) * 256],
                    self.bank(5, n=256, off=mc * 256), AF.Relu),
                    waits=[("pe", pv), ("ve", self.c_tt_free)], inc="se")
            sv = b.val("se")
            self.prelease([5], "se", sv)
            for mc in range(2):
                w3 = [("se", sv), ("pe", kvbf_pe)] if mc == 0 else []
                b.op("vector", lambda e, mc=mc: e.tensor_mul(
                    TT[:, mc * 256:(mc + 1) * 256], TT[:, mc * 256:(mc + 1) * 256],
                    TT[:, mc * 256:(mc + 1) * 256]), waits=w3)
                b.op("vector", lambda e, mc=mc: e.tensor_mul(
                    ATT[:, mc], TT[:, mc * 256:(mc + 1) * 256], p.CMASK[:, mc]),
                    inc="ve")
            at_ready = b.val("ve")
            self.c_tt_free = at_ready
            o_ve = []
            for et in range(16):
                bk = 6 + et % 2
                for mc in range(2):
                    w2 = []
                    if mc == 0:
                        w2 = self.pwait([bk]) + [("ve", at_ready), vt_ready]
                    b.op("tensor", lambda e, gp=gp, gi=gi, mc=mc, et=et, bk=bk: e.matmul(
                        self.bank(bk, n=256), VT[:, gi * 2 + mc, et * 128:(et + 1) * 128],
                        ATT[:, mc], start=(mc == 0), stop=False), waits=w2)
                b.op("tensor", lambda e, et=et, bk=bk, g0=g0: e.matmul(
                    self.bank(bk, n=256), KVBF[:, et * 128:(et + 1) * 128],
                    LQ[:, g0:g0 + 256], start=False, stop=True),
                    waits=[("ve", kv_v)] if et == 0 else [], inc="pe")
                pv = b.val("pe")
                b.op("vector", lambda e, et=et, g0=g0, bk=bk: e.tensor_mul(
                    OT[:, et, g0:g0 + 256], self.bank(bk, n=256), OT[:, et, g0:g0 + 256]),
                    waits=[("pe", pv), gt_ready], inc="ve")
                o_ve.append(b.val("ve"))
                self.prelease([bk], "ve", b.val("ve"))
            kvbf_pe = b.val("pe")
            att_pe[gp] = kvbf_pe
            for mc in range(2):
                b.op("tensor", lambda e, g0=g0, mc=mc: e.matmul(
                    self.bank(5, n=128, off=mc * 128),
                    LK[:, g0 + mc * 128: g0 + mc * 128 + 128], p.IDENT[:],
                    start=True, stop=True),
                    waits=(self.pwait([5]) + [("ve", qq_v)]) if mc == 0 else [],
                    inc="pe" if mc == 1 else None)
            pv = b.val("pe")
            for mc in range(2):
                b.op("vector", lambda e, mc=mc: e.tensor_copy(
                    LKT[:, mc], self.bank(5, n=128, off=mc * 128)),
                    waits=[("pe", pv)], inc="ve")
            lkt_v = b.val("ve")
            self.prelease([5], "ve", lkt_v)
            for ec in range(4):
                bk = ec % 2
                for mc in range(2):
                    w2 = []
                    if mc == 0:
                        w2 = self.pwait([bk]) + [("ve", lkt_v)]
                    b.op("tensor", lambda e, gi=gi, mc=mc, ec=ec, bk=bk: e.matmul(
                        self.bank(bk), LKT[:, mc],
                        VT[:, gi * 2 + mc, ec * 512:(ec + 1) * 512],
                        start=(mc == 0), stop=(mc == 1)), waits=w2,
                        inc="pe" if mc == 1 else None)
                pv = b.val("pe")
                b.op("vector", lambda e, ec=ec, bk=bk: e.tensor_add(
                    KV32[:, ec * 512:(ec + 1) * 512],
                    KV32[:, ec * 512:(ec + 1) * 512], self.bank(bk)),
                    waits=[("pe", pv), ("pe", kvbf_pe)], inc="ve")
                vv = b.val("ve")
                self.prelease([bk], "ve", vv)
                b.op("vector", lambda e, ec=ec: e.tensor_copy(
                    KVBF[:, ec * 512:(ec + 1) * 512],
                    KV32[:, ec * 512:(ec + 1) * 512]), inc="ve")
            kv_v = b.val("ve")
        o_ready = ("ve", b.val("ve"))
        if seq == 0 and gp == 0:
            p.dump("cot", OT[:]); p.dump("ckv", KV32[:]); p.dump("catt", ATT[:])
        for od in range(ND):
            par = od % 2
            b.dma("gpsimd", WOT[:, par],
                  wout[:, od * 128:(od + 1) * 128].rearrange("(t q) c -> q t c", q=128),
                  waits=[("pe", self.c_wot_free.get(par, 0))], inc="qw")
            wd = b.val("qw")
            b.dma("sync", XOLD[:, par], p.hb_xo[od, :, t0:t0 + 512],
                  waits=[("ve", self.c_xold_free.get(par, 0)),
                         ("qx", spill_qx)], inc="qx")
            xod = b.val("qx")
            bk = 4 + par
            for et in range(16):
                w2 = []
                if et == 0:
                    w2 = self.pwait([bk]) + [("qw", wd), o_ready]
                b.op("tensor", lambda e, od=od, et=et, bk=bk, par=par: e.matmul(
                    self.bank(bk), WOT[:, par, et], OT[:, et, 0:512],
                    start=(et == 0), stop=(et == 15)), waits=w2,
                    inc="pe" if et == 15 else None)
            pv = b.val("pe")
            self.c_wot_free[par] = pv
            b.op("vector", lambda e, od=od, par=par, bk=bk: e.scalar_tensor_tensor(
                XST[:, par], self.bank(bk), cS["bout"][:, od:od + 1], XOLD[:, par],
                ALU.add, ALU.add),
                waits=[("pe", pv), ("qx", xod), ("qx", self.c_bg_ld),
                       ("qx", self.c_xst_qx.get(par, 0)),
                       ("se", self.c_xst_se.get(par, 0))], inc="ve")
            vv = b.val("ve")
            self.prelease([bk], "ve", vv)
            self.c_xold_free[par] = vv
            b.dma("sync", p.hb_xo[od, :, t0:t0 + 512], XST[:, par],
                  waits=[("ve", vv)], inc="qx")
            self.c_xst_qx[par] = b.val("qx")
            b.op("scalar", lambda e, od=od, t0=t0, par=par: e.copy(
                p.XBF[:, od, 2 + t0:2 + t0 + 512], XST[:, par]),
                waits=[("ve", vv)], inc="se")
            self.c_xst_se[par] = b.val("se")
        self.join()
        if seq == 0 and gp == 0:
            p.dump("cxbf0", p.XBF[:])
    self.c_w_free = b.val("pe")
    return ("qx", b.val("qx"))


Stages.stage_c = _stage_c


def _spill_x(self, seq, X):
    p, b = self.p, self.b
    for od in range(ND):
        b.dma("sync", p.hb_xo[od], X[:, od], inc="qx")
    return b.val("qx")


def _write_output_hbm(self, seq, pool, srcs):
    p, b = self.p, self.b
    XF = pool.alloc("XFO", [128, ND, T], F32)
    XI = pool.alloc("XIO", [128, T], F32)
    prev_add = 0
    entry_qx = b.val("qx")
    for od in range(ND):
        b.dma("sync", XF[:, od], srcs[0][od], waits=[("qx", entry_qx)], inc="qx")
        if len(srcs) > 1:
            b.dma("sync", XI[:], srcs[1][od],
                  waits=[("ve", prev_add), ("qx", entry_qx)], inc="qx")
            qv = b.val("qx")
            b.op("vector", lambda e, od=od: e.tensor_add(XF[:, od], XF[:, od], XI[:]),
                 waits=[("qx", qv)], inc="ve")
            prev_add = b.val("ve")
    self.join(queues=True)
    if seq == 0:
        p.dump("xfout", XF[:])
    return self.write_output(seq, pool, X=XF)


Stages.spill_x = _spill_x
Stages.write_output_hbm = _write_output_hbm


def _stage_d(self, seq, pool, spill_qx=0):
    p, b = self.p, self.b
    cS = self.colslots

    sp = pool.fork()
    AREP, BREP, st_v = self.stats(sp, 0, T, False, waits=[("qx", spill_qx)])
    X32 = sp.alloc("DX32", [128, 2, T], F32)
    XSTG = sp.alloc("DXSTG", [128, 2, T], F32)
    prev = {"qx": 0, "se": 0, "ve": 0}
    for od in range(ND):
        par = od % 2
        b.dma("sync", X32[:, par], p.hb_xo[od],
              waits=[("qx", spill_qx), ("ve", prev.get(("n", par), 0))], inc="qx")
        qv = b.val("qx")
        b.op("vector", lambda e, par=par: e.tensor_mul(XSTG[:, par], X32[:, par], AREP[:]),
             waits=[("qx", qv), ("ve", st_v), ("qx", prev.get(("d", par), 0)),
                    ("se", prev.get(("s", par), 0))])
        b.op("vector", lambda e, par=par: e.tensor_sub(XSTG[:, par], XSTG[:, par], BREP[:]),
             inc="ve")
        vv = b.val("ve")
        prev[("n", par)] = vv
        b.dma("sync", p.hb_xi[od], XSTG[:, par], waits=[("ve", vv)], inc="qx")
        prev[("d", par)] = b.val("qx")
        b.op("scalar", lambda e, od=od, par=par: e.copy(
            p.XBF[:, od, 2:2 + T], XSTG[:, par]), waits=[("ve", vv)], inc="se")
        prev[("s", par)] = b.val("se")
    xi_qx = b.val("qx")
    self.join()
    if seq == 0:
        p.dump("d0x", p.XBF[:])

    dpool = pool.fork()
    for L in range(6):
        xi_qx = self.d_attn(seq, L, dpool.fork(), xi_qx)
        self.join()
        if seq == 0:
            p.dump(f"dl{L}a", p.XBF[:])
        xi_qx = self.d_ffn(seq, L, dpool.fork(), xi_qx)
        self.join()
        if seq == 0:
            p.dump(f"dl{L}f", p.XBF[:])
    return xi_qx


def _d_loadcols(self, L, names):
    p, b = self.p, self.b
    cS = self.colslots
    if "qs" in names:
        qs = p.w("L1_ffn_q_scale", [6, 64])
        ks = p.w("L1_ffn_k_scale", [6, 64])
        for half in range(2):
            b.dma("sync", cS["qs8"][half * 64:half * 64 + 64, :],
                  qs[L].unsqueeze(1), inc="qx")
            b.dma("sync", cS["ks"][half * 64:half * 64 + 64, :],
                  ks[L].unsqueeze(1), inc="qx")
        qv = b.val("qx")
        b.op("vector", lambda e: e.tensor_scalar(
            cS["qs8"][:], cS["qs8"][:], 8.0, None, ALU.mult),
            waits=[("qx", qv)], inc="ve")
        return b.val("ve"), qv
    if "ln" in names:
        b.dma("sync", cS["lng"], p.w("L1_ffn_ln_g", [6, D])[L].rearrange(
            "(d q) -> q d", q=128), inc="qx", slow=True)
        b.dma("sync", cS["lnb"], p.w("L1_ffn_ln_b", [6, D])[L].rearrange(
            "(d q) -> q d", q=128), inc="qx", slow=True)
        b1 = p.w("L1_ffn_b1", [6, 2 * DI])
        b.dma("sync", cS["b1a"][:, 0:21], b1[L, 0:21 * 128].rearrange(
            "(c q) -> q c", q=128), inc="qx", slow=True)
        b.dma("sync", cS["b1a"][0:C_LAST, 21:22], b1[L, 21 * 128:DI].unsqueeze(1),
              inc="qx")
        b.dma("sync", cS["b1g"][:, 0:21], b1[L, DI:DI + 21 * 128].rearrange(
            "(c q) -> q c", q=128), inc="qx", slow=True)
        b.dma("sync", cS["b1g"][0:C_LAST, 21:22], b1[L, DI + 21 * 128:].unsqueeze(1),
              inc="qx")
        b.dma("sync", cS["b2"], p.w("L1_ffn_b2", [6, D])[L].rearrange(
            "(d q) -> q d", q=128), inc="qx", slow=True)
        return b.val("qx")


def _d_attn(self, seq, L, pool, xi_qx):
    p, b = self.p, self.b
    cS = self.colslots
    qs_v, qs_qx = self._d_loadcols(L, ["qs"])

    TBL = pool.alloc("DTBL", [128, 4, T], BF16)
    WQK = pool.alloc("WQK", [128, ND, 2048], BF16)
    WVS = pool.alloc("WVS", [128, ND, D], BF16)
    WOS = pool.alloc("WOS", [128, ND, D], BF16)
    QF = pool.alloc("QF", [128, 8, 512], BF16)
    KF = pool.alloc("KF", [128, 8, 512], BF16)
    VT = pool.alloc("VTD", [128, 4, D], BF16)
    KFr = pool.alloc("KFr", [128, 8, 256], BF16)
    VTr = pool.alloc("VTr", [128, 2, D], BF16)
    OTN = pool.alloc("OTND", [128, 8, 512], BF16)
    PT = pool.alloc("PTD", [128, 384], BF16)
    PT2 = pool.alloc("PTD2", [128, 384], BF16)
    PT3 = pool.alloc("PTD3", [128, 384], BF16)
    QS = pool.alloc("QSD", [128, 512], F32)
    SQ = pool.alloc("SQD", [128, 512], BF16)
    QB = pool.alloc("QBD", [128, 512], BF16)
    T4 = pool.alloc("T4D", [128, 512], BF16)
    RN16 = pool.alloc("RN16", [16, 512], F32)
    R16S = pool.alloc("R16S", [16, 512], BF16)
    CSQ = pool.alloc("CSQ", [128, 4, 512], BF16)
    XSTD = pool.alloc("XSTD", [128, ND, 512], F32)
    XOLD2 = pool.alloc("XOLD2", [128, 2, 512], F32)
    want_pt = ("dptdbg" in os.environ.get("BK_DUMP", "").split(",")
               and seq == 0 and L == 0)
    PTDBG = pool.alloc("PTDBG", [128, 8, 384], BF16) if want_pt else None
    want_ot = ("dotsnap" in os.environ.get("BK_DUMP", "").split(",")
               and seq == 0 and L == 0)
    OTSNAP = (p.nc.dram_tensor("dump_dotsnap", [128, 8, 512], BF16,
                               kind="ExternalOutput").ap() if want_ot else None)
    pt_slots = {(0, 8, 0): 0, (0, 8, 1): 1, (0, 9, 0): 2, (0, 0, 0): 3,
                (1, 0, 0): 4, (1, 0, 1): 5, (0, 12, 0): 6, (1, 8, 0): 7}

    wqkv = p.w("L1_ffn_Wqkv", [6, D, 3 * INNER])
    dqk = b.dma("gpsimd", WQK[:],
                wqkv[L, :, 0:2048].rearrange("(d q) o -> q d o", q=128),
                waits=[("pe", self.d_w_free)], inc="qw")
    dvs = b.dma("gpsimd", WVS[:],
                wqkv[L, :, 2048:3072].rearrange("(d q) o -> q d o", q=128), inc="qw")
    wo = p.w("L1_ffn_Wo", [6, INNER, D])
    dwo = b.dma("gpsimd", WOS[:],
                wo[L].rearrange("(t q) o -> q t o", q=128), inc="qw")
    dtb = None
    for i, nm in enumerate(("tcs", "tss", "tcsi", "tssi")):
        dtb = b.dma("gpsimd", TBL[:, i], p.extra(nm, p.tab[nm[1:]]), inc="qw")

    xbf_flush = []
    for q in range(4):
        t0 = q * 512
        w0 = [("ve", qs_v), ("qw", dtb)] if q == 0 else []
        for i in range(4):
            b.op("vector", lambda e, i=i, t0=t0: e.tensor_copy(
                CSQ[:, i], TBL[:, i, t0:t0 + 512]),
                waits=w0 if i == 0 else (), inc="ve" if i == 3 else None)
        csq_v = b.val("ve")
        xbf_se = self.d_xbf_se
        vt_ve = 0
        for tt in range(4):
            bk = (tt % 2) * 2
            for oc2 in range(2):
                for d in range(ND):
                    w2 = []
                    if d == 0:
                        w2 = self.pwait([bk + oc2]) + [("qw", dvs), ("se", xbf_se)]
                    b.op("tensor", lambda e, tt=tt, oc2=oc2, d=d, bk=bk, t0=t0: e.matmul(
                        self.bank(bk + oc2),
                        p.xbf_shift(d)[:, t0 + tt * 128: t0 + (tt + 1) * 128],
                        WVS[:, d, oc2 * 512:(oc2 + 1) * 512],
                        start=(d == 0), stop=(d == ND - 1)), waits=w2,
                        inc="pe" if (oc2 == 1 and d == ND - 1) else None)
            pv = b.val("pe")
            b.op("vector", lambda e, tt=tt, bk=bk: e.tensor_copy(
                VT[:, tt], self.PS[:, bk * 512: bk * 512 + 1024]),
                waits=[("pe", pv), ("pe", self.d_av_pe)], inc="ve")
            vt_ve = b.val("ve")
            self.prelease([bk, bk + 1], "ve", vt_ve)
        vt_ready = ("ve", vt_ve)
        fd_ve = 0
        for (isq, FDST, tbl0) in ((True, QF, 0), (False, KF, 2)):
            off_w = 0 if isq else 1024
            psn_pe = {}
            last_fd = 0
            for oc in range(8):
                bk = oc % 2
                for d in range(ND):
                    w2 = []
                    if d == 0:
                        w2 = self.pwait([bk]) + [("qw", dqk), ("se", xbf_se),
                                                 ("qx", qs_qx)]
                    b.op("tensor", lambda e, oc=oc, d=d, bk=bk, t0=t0, off_w=off_w: e.matmul(
                        self.bank(bk),
                        WQK[:, d, off_w + oc * 128: off_w + (oc + 1) * 128],
                        p.xbf_shift(d)[:, t0:t0 + 512],
                        start=(d == 0), stop=(d == ND - 1)), waits=w2,
                        inc="pe" if d == ND - 1 else None)
                pv = b.val("pe")
                b.op("scalar", lambda e, bk=bk: e.copy(QS[:], self.bank(bk)),
                     waits=[("pe", pv), ("ve", last_fd)], inc="se")
                sv = b.val("se")
                self.prelease([bk], "se", sv)
                b.op("vector", lambda e: e.tensor_mul(SQ[:], QS[:], QS[:]),
                     waits=[("se", sv), ("pe", psn_pe.get(oc - 1, 0))], inc="ve")
                sqv = b.val("ve")
                b.op("tensor", lambda e, oc=oc: e.matmul(
                    self.PS[0:16, 7 * 512:7 * 512 + 512],
                    p.BOSQ[:, oc * 16:(oc + 1) * 16], SQ[:],
                    start=(oc == 0), stop=(oc == 7), skip_group_check=True),
                    waits=([("ve", sqv)] + (self.pwait([7]) if oc == 0 else [])),
                    inc="pe")
                psn_pe[oc] = b.val("pe")
                b.op("vector", lambda e, oc=oc, FDST=FDST: e.tensor_copy(
                    FDST[:, oc], QS[:]),
                    waits=[("pe", self.d_fd_pe)], inc="ve")
                last_fd = b.val("ve")
            b.op("scalar", lambda e: e.activation(
                self.PS[0:16, 7 * 512:7 * 512 + 512],
                self.PS[0:16, 7 * 512:7 * 512 + 512], AF.Sqrt),
                waits=[("pe", psn_pe[7])], inc="se")
            sv = b.val("se")
            b.op("vector", lambda e: e.reciprocal(
                RN16[:], self.PS[0:16, 7 * 512:7 * 512 + 512]),
                waits=[("se", sv)], inc="ve")
            b.op("vector", lambda e: e.tensor_copy(R16S[:], RN16[:]), inc="ve")
            rn_v = b.val("ve")
            self.prelease([7], "ve", rn_v)
            rot_pe = 0
            for oc in range(8):
                b.op("tensor", lambda e, oc=oc: e.matmul(
                    self.bank(6), p.BOT16[:, oc * 128:(oc + 1) * 128], R16S[:],
                    start=True, stop=True),
                    waits=[("ve", rn_v)] + self.pwait([6]), inc="pe")
                pv = b.val("pe")
                scol = cS["qs8"] if isq else cS["ks"]
                b.op("vector", lambda e, oc=oc, FDST=FDST, scol=scol: e.scalar_tensor_tensor(
                    QB[:], FDST[:, oc], scol[:], self.bank(6),
                    ALU.mult, ALU.mult),
                    waits=[("pe", pv), ("pe", rot_pe), ("ve", qs_v)], inc="ve")
                qb_v = b.val("ve")
                self.prelease([6], "ve", qb_v)
                b.op("tensor", lambda e: e.matmul(
                    self.bank(6), p.ROT[:], QB[:], start=True, stop=True),
                    waits=[("ve", qb_v)] + self.pwait([6]), inc="pe")
                rot_pe = b.val("pe")
                b.op("vector", lambda e, tbl0=tbl0: e.tensor_mul(
                    T4[:], QB[:], CSQ[:, tbl0]), waits=[("ve", csq_v)])
                b.op("vector", lambda e, tbl0=tbl0: e.tensor_mul(
                    QB[:], self.bank(6), CSQ[:, tbl0 + 1]),
                    waits=[("pe", rot_pe)], inc="ve")
                self.prelease([6], "ve", b.val("ve"))
                b.op("vector", lambda e, oc=oc, FDST=FDST: e.tensor_add(
                    FDST[:, oc], T4[:], QB[:]), inc="ve")
                fd_ve = b.val("ve")
            self.d_fd_pe = 0
        qk_ready = ("ve", fd_ve)
        if seq == 0 and L == 0 and q == 0:
            p.dump("dqf", QF[:]); p.dump("dkf", KF[:]); p.dump("dvt", VT[:])
            p.dump("dcsq", CSQ[:])
        s_bank_waited = False
        s_started = set()
        snap_qx = 0
        slots = []
        for w in range(2):
            wg = 2 * q + w
            for ic in range(2):
                jts = []
                for jt in range(3):
                    jg = 2 * wg + ic - 2 + jt
                    if jg >= 0:
                        jts.append((jt, jg - 4 * q))
                for h in range(HEADS):
                    slots.append((w, h, ic, jts))
        sim_pe = {}
        BK3 = (0, 1, 7)
        PTS = (PT, PT2, PT3)
        pt_guard = {0: self.d_pt_pe, 1: self.d_pt_pe, 2: self.d_pt_pe}

        def emit_sim(n):
            (w, h, ic, jts) = slots[n]
            hb = (h % 2) * 64
            ht = h // 2
            iq = w * 256 + ic * 128
            bk3 = BK3[n % 3]
            for (ji, (jt, jq)) in enumerate(jts):
                if jq < 0:
                    ksrc = KFr[hb:hb + 64, ht, (jq + 2) * 128:(jq + 3) * 128]
                else:
                    ksrc = KF[hb:hb + 64, ht, jq * 128:(jq + 1) * 128]
                w2 = []
                if ji == 0:
                    w2 = [qk_ready] + self.pwait([bk3])
                b.op("tensor", lambda e, ksrc=ksrc, jt=jt, hb=hb, ht=ht, iq=iq,
                     bk3=bk3: e.matmul(
                    self.bank(bk3, n=128, off=jt * 128), ksrc,
                    QF[hb:hb + 64, ht, iq:iq + 128],
                    start=True, stop=True, skip_group_check=True), waits=w2,
                    inc="pe" if ji == len(jts) - 1 else None)
            sim_pe[n] = b.val("pe")

        emit_sim(0)
        emit_sim(1)
        for n, (w, h, ic, jts) in enumerate(slots):
            hb = (h % 2) * 64
            ht = h // 2
            iq = w * 256 + ic * 128
            i0w = w * 256
            par = n % 3
            bk3 = BK3[par]
            PTb = PTS[par]
            nj = len(jts)
            joff = (3 - nj) * 128
            b.op("scalar", lambda e, joff=joff, nj=nj, bk3=bk3, PTb=PTb: e.activation(
                PTb[:, joff:joff + nj * 128],
                self.bank(bk3, n=nj * 128, off=joff), AF.Exp),
                waits=[("pe", sim_pe[n]), ("pe", pt_guard[par])], inc="se")
            sv = b.val("se")
            self.prelease([bk3], "se", sv)
            b.op("vector", lambda e, joff=joff, nj=nj, PTb=PTb: e.tensor_mul(
                PTb[:, joff:joff + nj * 128], PTb[:, joff:joff + nj * 128],
                p.DMASKB[:, joff:joff + nj * 128]),
                waits=[("se", sv)], inc="ve")
            pt_v = b.val("ve")
            if want_pt and q == 0 and (w, h, ic) in pt_slots:
                sl = pt_slots[(w, h, ic)]
                b.op("scalar", lambda e, sl=sl, PTb=PTb: e.copy(
                    PTDBG[:, sl], PTb[:]), waits=[("ve", pt_v)], inc="se")
            if n + 2 < len(slots):
                emit_sim(n + 2)
            for (ji, (jt, jq)) in enumerate(jts):
                sw = [("ve", pt_v)] if ji == 0 else []
                st_ = iq not in s_started
                s_started.add(iq)
                if not s_bank_waited:
                    sw += self.pwait([6])
                    s_bank_waited = True
                b.op("tensor", lambda e, h=h, jt=jt, iq=iq, st_=st_,
                     lst=(h == HEADS - 1 and ji == len(jts) - 1),
                     PTb=PTb: e.matmul(
                    self.PS[0:16, 6 * 512 + iq:6 * 512 + iq + 128],
                    p.OC16[:, h * 16:(h + 1) * 16],
                    PTb[:, jt * 128:(jt + 1) * 128],
                    start=st_, stop=lst,
                    skip_group_check=True), waits=sw)
            for (ji, (jt, jq)) in enumerate(jts):
                if jq < 0:
                    vsrc = VTr[:, jq + 2, h * 64:(h + 1) * 64]
                else:
                    vsrc = VT[:, jq, h * 64:(h + 1) * 64]
                col = (2 + ht // 2) * 512 + (ht % 2) * 256 + ic * 128
                w2 = []
                if h % 4 == 0 and h % 2 == 0 and ic == 0 and ji == 0:
                    w2 = self.pwait([2 + ht // 2]) + [vt_ready]
                b.op("tensor", lambda e, vsrc=vsrc, jt=jt, hb=hb, col=col,
                     fst=(ji == 0), lst=(ji == len(jts) - 1), PTb=PTb: e.matmul(
                    self.PS[hb:hb + 64, col:col + 128],
                    vsrc, PTb[:, jt * 128:(jt + 1) * 128],
                    start=fst, stop=lst,
                    skip_group_check=True), waits=w2,
                    inc="pe" if ji == len(jts) - 1 else None)
            pt_guard[par] = b.val("pe")
            if h == HEADS - 1 and ic == 1:
                av_pe = b.val("pe")
                b.op("vector", lambda e, i0w=i0w: e.tensor_copy(
                    OTN[:, :, i0w:i0w + 256],
                    self.PS[:, 2 * 512:2 * 512 + 2048].rearrange(
                        "z (o t) -> z o t", o=8)),
                    waits=[("pe", av_pe)], inc="ve")
                if want_ot and q == 0:
                    b.dma("sync", OTSNAP[:, :, i0w:i0w + 256],
                          OTN[:, :, i0w:i0w + 256], waits=[("ve", b.val("ve"))],
                          inc="qx")
                    snap_qx = b.val("qx")
                self.prelease([2, 3, 4, 5], "ve", b.val("ve"))
        self.d_pt_pe = b.val("pe")
        self.d_av_pe = b.val("pe")
        if seq == 0 and L == 0 and q == 0:
            p.dump("dotnraw", OTN[:])
            if want_pt:
                p.dump("dptdbg", PTDBG[:])
        b.op("vector", lambda e: e.reciprocal(
            RN16[:], self.PS[0:16, 6 * 512:6 * 512 + 512]),
            waits=[("pe", self.d_av_pe)] +
                  ([("qx", snap_qx)] if snap_qx else []), inc="ve")
        b.op("vector", lambda e: e.tensor_copy(R16S[:], RN16[:]), inc="ve")
        r16_v = b.val("ve")
        self.prelease([6], "ve", r16_v)
        otn_ve = 0
        for oc in range(8):
            b.op("tensor", lambda e, oc=oc: e.matmul(
                self.bank(6), p.BOT16[:, oc * 128:(oc + 1) * 128], R16S[:],
                start=True, stop=True), waits=[("ve", r16_v)] + self.pwait([6]),
                inc="pe")
            pv = b.val("pe")
            b.op("vector", lambda e, oc=oc: e.tensor_mul(
                OTN[:, oc], OTN[:, oc], self.bank(6)),
                waits=[("pe", pv)], inc="ve")
            otn_ve = b.val("ve")
            self.prelease([6], "ve", otn_ve)
        otn_ready = ("ve", otn_ve)
        if seq == 0 and L == 0 and q == 0:
            p.dump("dotn", OTN[:]); p.dump("dr16", R16S[:]); p.dump("dpt", PT[:])
        b.op("vector", lambda e: e.tensor_copy(KFr[:], KF[:, :, 256:512]),
             waits=[("pe", self.d_av_pe)])
        b.op("vector", lambda e: e.tensor_copy(VTr[:, 0], VT[:, 2]))
        b.op("vector", lambda e: e.tensor_copy(VTr[:, 1], VT[:, 3]), inc="ve")
        for fn in xbf_flush:
            fn()
        xbf_flush = []
        self.d_xbf_se = b.val("se")
        for od in range(ND):
            par = od % 2
            b.dma("sync", XOLD2[:, par], p.hb_xi[od, :, t0:t0 + 512],
                  waits=[("ve", self.d_xold_free.get(par, 0)), ("qx", xi_qx)],
                  inc="qx")
            xod = b.val("qx")
            bk = par
            for et in range(ND):
                w2 = []
                if et == 0:
                    w2 = self.pwait([bk]) + [("qw", dwo), otn_ready]
                b.op("tensor", lambda e, od=od, et=et, bk=bk: e.matmul(
                    self.bank(bk), WOS[:, et, od * 128:(od + 1) * 128],
                    OTN[:, et], start=(et == 0), stop=(et == ND - 1)), waits=w2,
                    inc="pe" if et == ND - 1 else None)
            pv = b.val("pe")
            b.op("vector", lambda e, od=od, par=par, bk=bk: e.tensor_add(
                XSTD[:, od], self.bank(bk), XOLD2[:, par]),
                waits=[("pe", pv), ("qx", xod),
                       ("se", self.d_xstd_se.get(od, 0)),
                       ("qx", self.d_xstd_qx.get(od, 0))], inc="ve")
            vv = b.val("ve")
            self.prelease([bk], "ve", vv)
            self.d_xold_free[par] = vv
            b.dma("sync", p.hb_xi[od, :, t0:t0 + 512], XSTD[:, od],
                  waits=[("ve", vv)], inc="qx")
            self.d_xstd_qx[od] = b.val("qx")

            def mk_flush(od=od, t0=t0, vv=vv):
                def fl():
                    bb = self.b
                    bb.op("scalar", lambda e: e.copy(
                        p.XBF[:, od, 2 + t0:2 + t0 + 512], XSTD[:, od]),
                        waits=[("ve", vv), ("pe", bb.val("pe"))], inc="se")
                    self.d_xstd_se[od] = bb.val("se")
                return fl
            xbf_flush.append(mk_flush())
        if seq == 0 and L == 0 and q == 0:
            p.dump("dxstd0", XSTD[:])
        xi_qx = b.val("qx")
        self.join()
    for fn in xbf_flush:
        fn()
    self.d_xbf_se = b.val("se")
    self.d_w_free = b.val("pe")
    self.join()
    return xi_qx


Stages.d_attn = _d_attn



def _d_ffn(self, seq, L, pool, xi_qx):
    p, b = self.p, self.b
    cS = self.colslots
    ln_qx = self._d_loadcols(L, ["ln"])

    XN = pool.alloc("DXN", [128, ND, T], BF16)
    h2_off = pool.cur
    H2 = pool.alloc("DH2", [128, NC_CHUNKS, T], BF16)
    G = pool.alloc("DG", [128, 1024], F32)
    W1R = pool.alloc("W1R", [128, 2, 2, ND, 128], BF16)
    W2R = pool.alloc("W2R", [128, 2, NC_CHUNKS, 128], BF16)
    XOLD3 = pool.alloc("XOLD3", [128, 2, 1024], F32)
    XST3 = pool.alloc("XST3", [128, 2, 1024], F32)

    w1 = p.w("L1_ffn_W1", [6, D, 2 * DI])
    w2 = p.w("L1_ffn_W2", [6, DI, D])

    sp = PoolCursor(p.mem, h2_off)
    AREP, BREP, st_v = self.stats(sp, 0, T, True,
                                  waits=[("se", self.d_xbf_se)])
    TMPN = sp.alloc("DTMPN", [128, T], F32)
    g_cols = [cS["lng"][:, d:d + 1] for d in range(ND)]
    b_cols = [cS["lnb"][:, d:d + 1] for d in range(ND)]
    xn_vals = self.normalize(lambda d: XN[:, d], AREP, BREP, TMPN, 0, T,
                             src_view=lambda d: p.xbf_shift(d),
                             waits=[("ve", st_v), ("qx", ln_qx)],
                             gb=(g_cols, b_cols))
    xn_v = ("ve", xn_vals[-1])
    self.join()
    w1_dma = {}
    for c in range(NC_CHUNKS):
        cw = 128 if c < NC_CHUNKS - 1 else C_LAST
        par = c % 2
        w1_dma[c] = b.dma(
            "gpsimd", W1R[:, par, :, :, 0:cw],
            w1[L].rearrange("(d q) (h w) -> q h d w", q=128, h=2)[
                :, :, :, c * 128:c * 128 + cw],
            waits=[("pe", self.f_w1_free.get(par, 0))], inc="qw")
        for th in range(2):
            for ag in range(2):
                for tc in range(2):
                    bk = par * 4 + ag * 2 + tc
                    tcs = th * 1024 + tc * 512
                    for d in range(ND):
                        w2_ = []
                        if d == 0:
                            w2_ = self.pwait([bk]) + [("qw", w1_dma[c]), xn_v]
                        b.op("tensor", lambda e, c=c, cw=cw, par=par, ag=ag, d=d, bk=bk, tcs=tcs: e.matmul(
                            self.bank(bk, n=512)[0:cw, :] if False else
                            self.PS[0:cw, bk * 512:bk * 512 + 512],
                            W1R[:, par, ag, d, 0:cw], XN[:, d, tcs:tcs + 512],
                            start=(d == 0), stop=(d == ND - 1)), waits=w2_,
                            inc="pe" if d == ND - 1 else None)
            pv = b.val("pe")
            self.f_w1_free[par] = pv
            pa = par * 4
            b.op("scalar", lambda e, c=c, cw=cw, pa=pa: e.activation(
                G[0:cw, :], self.PS[0:cw, (pa + 2) * 512:(pa + 2) * 512 + 1024],
                AF.Gelu, bias=cS["b1g"][0:cw, c:c + 1]),
                waits=[("pe", pv), ("qx", ln_qx), ("ve", self.f_g_ve)], inc="se")
            sv = b.val("se")
            self.prelease([pa + 2, pa + 3], "se", sv)
            b.op("vector", lambda e, c=c, cw=cw, th=th, pa=pa: e.scalar_tensor_tensor(
                H2[0:cw, c, th * 1024:(th + 1) * 1024],
                self.PS[0:cw, pa * 512:pa * 512 + 1024],
                cS["b1a"][0:cw, c:c + 1], G[0:cw, :], ALU.add, ALU.mult),
                waits=[("se", sv)], inc="ve")
            self.f_g_ve = b.val("ve")
            self.prelease([pa, pa + 1], "ve", self.f_g_ve)
    h2_v = ("ve", self.f_g_ve)
    for od in range(ND):
        par = od % 2
        d1 = b.dma("gpsimd", W2R[:, par, 0:21, :],
                   w2[L][0:21 * 128].rearrange("(c q) o -> q c o", q=128)[
                       :, :, od * 128:(od + 1) * 128],
                   waits=[("pe", self.f_w2_free.get(par, 0))], inc="qw")
        d2 = b.dma("gpsimd", W2R[0:C_LAST, par, 21, :],
                   w2[L, 21 * 128:DI, od * 128:(od + 1) * 128], inc="qw")
        w2_qw = b.val("qw")
        for half in range(2):
            b.dma("sync", XOLD3[:, par], p.hb_xi[od, :, half * 1024:(half + 1) * 1024],
                  waits=[("ve", self.f_xold_free.get(par, 0)), ("qx", xi_qx)],
                  inc="qx")
            xod = b.val("qx")
            for tc in range(2):
                bk = 6 + (tc % 2)
                tcs = half * 1024 + tc * 512
                for c in range(NC_CHUNKS):
                    cw = 128 if c < NC_CHUNKS - 1 else C_LAST
                    w2_ = []
                    if c == 0:
                        w2_ = self.pwait([bk]) + [("qw", w2_qw), h2_v]
                    b.op("tensor", lambda e, od=od, par=par, c=c, cw=cw, bk=bk, tcs=tcs: e.matmul(
                        self.bank(bk), W2R[0:cw, par, c, :],
                        H2[0:cw, c, tcs:tcs + 512],
                        start=(c == 0), stop=(c == NC_CHUNKS - 1)), waits=w2_,
                        inc="pe" if c == NC_CHUNKS - 1 else None)
                pv = b.val("pe")
                if tc == 1:
                    self.f_w2_free[par] = pv
                b.op("vector", lambda e, od=od, par=par, bk=bk, tc=tc: e.scalar_tensor_tensor(
                    XST3[:, par, tc * 512:(tc + 1) * 512], self.bank(bk),
                    cS["b2"][:, od:od + 1], XOLD3[:, par, tc * 512:(tc + 1) * 512],
                    ALU.add, ALU.add),
                    waits=[("pe", pv), ("qx", xod), ("qx", ln_qx),
                           ("qx", self.f_xst_qx.get(par, 0)),
                           ("se", self.f_xst_se.get(par, 0))], inc="ve")
                vv = b.val("ve")
                self.prelease([bk], "ve", vv)
            self.f_xold_free[par] = vv
            b.dma("sync", p.hb_xi[od, :, half * 1024:(half + 1) * 1024],
                  XST3[:, par], waits=[("ve", vv)], inc="qx")
            self.f_xst_qx[par] = b.val("qx")
            b.op("scalar", lambda e, od=od, half=half, par=par: e.copy(
                p.XBF[:, od, 2 + half * 1024:2 + (half + 1) * 1024],
                XST3[:, par]), waits=[("ve", vv)], inc="se")
            self.f_xst_se[par] = b.val("se")
    self.d_xbf_se = b.val("se")
    self.d_w_free = b.val("pe")
    self.join()
    return b.val("qx")


Stages.d_ffn = _d_ffn

Stages.stage_d = _stage_d
Stages._d_loadcols = _d_loadcols
